# version 02
# baseline (speedup 1.0000x reference)
import contextlib
import math
import numpy as np
import ml_dtypes
import concourse.bass as bass
import concourse.mybir as mybir
from concourse.bass_utils import run_bass_kernel_spmd

F32 = mybir.dt.float32
BF16 = mybir.dt.bfloat16
I32 = mybir.dt.int32
AF = mybir.ActivationFunctionType
ALU = mybir.AluOpType
AX = mybir.AxisListType
BF = ml_dtypes.bfloat16

NCORE = 8
B_LOC = 2
S = 2048
D = 2048
T_LOC = B_LOC * S
NT = T_LOC // 128
NTS = S // 128
KC = D // 128
H0 = 16
I0 = 8192
G1, HG1 = 4, 4
DFF1 = 5461
NFC = (DFF1 + 127) // 128
LASTF = DFF1 - 42 * 128
RSC = float(128.0 ** -0.5)
RSC1 = float(128.0 ** -0.25)


def tsl(i, n):
    return slice(i * n, (i + 1) * n)


def _mk_waiter(sem):
    def wait_all(eng, guards):
        for gsp in guards:
            if gsp:
                eng.wait_ge(sem[gsp[0]], gsp[1])
    return wait_all


def host_constants():
    c = {}
    t = np.arange(S, dtype=np.float64)
    invf = 1.0 / 10000.0 ** (np.arange(0, 64, 2, dtype=np.float64) / 64.0)
    ang = t[:, None] * invf[None, :]
    c["cos0"] = np.tile(np.cos(ang), (1, H0)).astype(np.float32)
    c["sin0"] = np.tile(np.sin(ang), (1, H0)).astype(np.float32)
    ang1 = t[:, None] * np.power(10000.0, -np.arange(64, dtype=np.float64) / 64.0)[None, :]
    c["cos1q"] = np.tile(np.cos(ang1), (1, H0)).astype(np.float32)
    c["sin1q"] = np.tile(np.sin(ang1), (1, H0)).astype(np.float32)
    c["cos1k"] = np.tile(np.cos(ang1), (1, G1)).astype(np.float32)
    c["sin1k"] = np.tile(np.sin(ang1), (1, G1)).astype(np.float32)
    import jax
    with jax.default_device(jax.devices("cpu")[0]):
        u = np.asarray(jax.random.uniform(jax.random.key(42), (32768, 8), dtype=np.float32))
    c["gumbel"] = (-np.log(-np.log(u + 1e-10) + 1e-10)).astype(np.float32)
    c["ones1"] = np.ones((1, 128), np.float32).astype(BF)
    c["ones512"] = np.ones((1, 512), np.float32).astype(BF)
    m = np.zeros((128, 128), np.float32)
    m[np.triu_indices(128, 1)] = -1e30
    c["maskdiag"] = m
    E = np.zeros((128, 8), np.float32)
    for p in range(128):
        E[p, p // 16] = 1.0
    c["Emat"] = E
    c["I8"] = np.eye(8, dtype=np.float32)
    return c


class Rec:
    ENGINES = ("sync", "tensor", "vector", "scalar", "gpsimd")

    def __init__(self, nc, stack):
        self.nc = nc
        self.stack = stack
        self.seq = {e: [] for e in self.ENGINES}
        self.sem = {}
        self.cnt = {}
        self.dram = {}
        self.sb = {}
        self.ps = {}
        self.consts = {}
        self.const_tile = None
        self.layout = {}

    def new_sem(self, name):
        self.sem[name] = self.stack.enter_context(self.nc.semaphore(name))
        self.cnt[name] = 0

    def inc(self, name, n=1):
        self.cnt[name] += n
        return self.cnt[name]

    def emit(self, engine, fn):
        self.seq[engine].append(fn)

    def lay(self, name, off, n, dt):
        self.layout[name] = (off, n, dt)

    def buf(self, name):
        off, n, dt = self.layout[name]
        a = self.sb["arena"]
        if dt == BF16:
            return a[:, off:off + n]
        return a[:, off:off + 2 * n].bitcast(dt)

    def const(self, val):
        val = float(val)
        if val not in self.consts:
            idx = len(self.consts)
            assert idx < 60
            self.consts[val] = idx
        i = self.consts[val]
        return self.const_tile[:, i:i + 1]

    def const_array(self):
        a = np.zeros((128, 64), np.float32)
        for v, i in self.consts.items():
            a[:, i] = v
        return a

    def run_block(self):
        nc = self.nc
        with nc.Block() as block:
            @block.sync
            def _(eng):
                for f in self.seq["sync"]:
                    f()

            @block.tensor
            def _(eng):
                for f in self.seq["tensor"]:
                    f()

            @block.vector
            def _(eng):
                for f in self.seq["vector"]:
                    f()

            @block.scalar
            def _(eng):
                for f in self.seq["scalar"]:
                    f()

            @block.gpsimd
            def _(eng):
                for f in self.seq["gpsimd"]:
                    f()

    def barrier(self):
        tgt = self.cnt["ph"] + 5
        self.cnt["ph"] = tgt
        nc, sems = self.nc, self.sem

        def mk(en):
            def f():
                eng = getattr(nc, en)
                eng.sem_inc(sems["ph"], 1)
                eng.wait_ge(sems["ph"], tgt)
            return f
        for e in self.ENGINES:
            self.emit(e, mk(e))


def emit_p1_ln0(c):
    nc, d, sem = c.nc, c.dram, c.sem
    sm = c.sb["sm32"]
    spacer = c.const_tile[:, 63:64]
    c.lay("p1_f32", 0, 10240, F32)
    c.lay("p1_bf", 20480, 8192, BF16)
    t32 = c.buf("p1_f32")
    wbf = c.buf("p1_bf")
    invD = 1.0 / D
    eps = 1e-5
    gp_after = {}

    for i in range(NT):
        s_ = i % 2
        xv = t32[:, tsl(s_, 2048)]
        xn = t32[:, 4096 + s_ * 2048: 4096 + (s_ + 1) * 2048]
        junk = t32[:, 8192:10240]
        col = lambda k: sm[:, s_ * 16 + k: s_ * 16 + k + 1]
        nmu, ssq, var_, rstd = col(0), col(1), col(2), col(3)
        nmu2, ssq2, var2, rstd2 = col(4), col(5), col(6), col(7)

        ld_i = c.inc("ld", 16)

        def L1(i=i, xv=xv):
            nc.sync.dma_start(out=xv, in_=d["x0"][tsl(i, 128), :]).then_inc(sem["ld"], 16)
        c.emit("sync", L1)

        gp_guard = gp_after.get(i - 2, 0)
        v1 = c.inc("dv")

        def V1(xv=xv, nmu=nmu, ld_i=ld_i, gp_guard=gp_guard):
            nc.vector.wait_ge(sem["ld"], ld_i)
            if gp_guard:
                nc.vector.wait_ge(sem["gp"], gp_guard)
            nc.vector.reduce_sum(nmu, xv, axis=AX.X, negate=True).then_inc(sem["dv"], 1)
        c.emit("vector", V1)

        a1 = c.inc("ac")

        def A1(xv=xv, junk=junk, ssq=ssq, ld_i=ld_i, gp_guard=gp_guard):
            nc.scalar.wait_ge(sem["ld"], ld_i)
            if gp_guard:
                nc.scalar.wait_ge(sem["gp"], gp_guard)
            nc.scalar.activation(junk, xv, AF.Square, accum_out=ssq).then_inc(sem["ac"], 1)
        c.emit("scalar", A1)

        g1 = c.inc("gp")

        def G1(nmu=nmu, ssq=ssq, var_=var_, v1=v1, a1=a1):
            nc.gpsimd.wait_ge(sem["dv"], v1)
            nc.gpsimd.wait_ge(sem["ac"], a1)
            nc.gpsimd.tensor_scalar(nmu, nmu, invD, None, ALU.mult)
            nc.gpsimd.tensor_tensor(var_, nmu, nmu, ALU.mult)
            nc.gpsimd.tensor_scalar(ssq, ssq, invD, None, ALU.mult)
            nc.gpsimd.tensor_tensor(var_, ssq, var_, ALU.subtract).then_inc(sem["gp"], 1)
        c.emit("gpsimd", G1)

        a2 = c.inc("ac", 2)

        def A2(var_=var_, rstd=rstd, g1=g1):
            nc.scalar.wait_ge(sem["gp"], g1)
            nc.scalar.activation(rstd, var_, AF.Ln, bias=c.const(eps)).then_inc(sem["ac"], 1)
            nc.scalar.activation(spacer, spacer, AF.Identity)
            nc.scalar.activation(rstd, rstd, AF.Exp, scale=-0.5).then_inc(sem["ac"], 1)
        c.emit("scalar", A2)

        v2 = c.inc("dv")

        def V2(xv=xv, xn=xn, nmu=nmu, rstd=rstd, a2=a2):
            nc.vector.wait_ge(sem["ac"], a2)
            nc.vector.tensor_scalar(xn, xv, nmu, rstd, ALU.add, op1=ALU.mult).then_inc(sem["dv"], 1)
        c.emit("vector", V2)

        g2 = c.inc("gp")

        def G2(xv=xv, xn=xn, v2=v2):
            nc.gpsimd.wait_ge(sem["dv"], v2)
            nc.gpsimd.tensor_tensor(xv, xv, xn, ALU.add).then_inc(sem["gp"], 1)
        c.emit("gpsimd", G2)

        c.inc("st", 16)

        def L2(xv=xv, i=i, g2=g2):
            nc.sync.wait_ge(sem["gp"], g2)
            nc.sync.dma_start(out=d["xb"][tsl(i, 128), :], in_=xv).then_inc(sem["st"], 16)
        c.emit("sync", L2)

        xnb = wbf[:, tsl(s_, 2048)]
        v3 = c.inc("dv")

        def V3(xn=xn, xnb=xnb):
            nc.vector.tensor_copy(xnb, xn).then_inc(sem["dv"], 1)
        c.emit("vector", V3)

        c.inc("st", 16)

        def L3(xnb=xnb, i=i, v3=v3):
            nc.sync.wait_ge(sem["dv"], v3)
            nc.sync.dma_start(out=d["xn0b"][tsl(i, 128), :], in_=xnb).then_inc(sem["st"], 16)
        c.emit("sync", L3)

        v4 = c.inc("dv")

        def V4(xn=xn, nmu2=nmu2):
            nc.vector.reduce_sum(nmu2, xn, axis=AX.X, negate=True).then_inc(sem["dv"], 1)
        c.emit("vector", V4)

        a3 = c.inc("ac")

        def A3(xn=xn, junk=junk, ssq2=ssq2, v2=v2):
            nc.scalar.wait_ge(sem["dv"], v2)
            nc.scalar.activation(junk, xn, AF.Square, accum_out=ssq2).then_inc(sem["ac"], 1)
        c.emit("scalar", A3)

        g3 = c.inc("gp")

        def G3(nmu2=nmu2, ssq2=ssq2, var2=var2, v4=v4, a3=a3):
            nc.gpsimd.wait_ge(sem["dv"], v4)
            nc.gpsimd.wait_ge(sem["ac"], a3)
            nc.gpsimd.tensor_scalar(nmu2, nmu2, invD, None, ALU.mult)
            nc.gpsimd.tensor_tensor(var2, nmu2, nmu2, ALU.mult)
            nc.gpsimd.tensor_scalar(ssq2, ssq2, invD, None, ALU.mult)
            nc.gpsimd.tensor_tensor(var2, ssq2, var2, ALU.subtract).then_inc(sem["gp"], 1)
        c.emit("gpsimd", G3)
        gp_after[i] = c.cnt["gp"]

        a4 = c.inc("ac", 2)

        def A4(var2=var2, rstd2=rstd2, g3=g3):
            nc.scalar.wait_ge(sem["gp"], g3)
            nc.scalar.activation(rstd2, var2, AF.Ln, bias=c.const(eps)).then_inc(sem["ac"], 1)
            nc.scalar.activation(spacer, spacer, AF.Identity)
            nc.scalar.activation(rstd2, rstd2, AF.Exp, scale=-0.5).then_inc(sem["ac"], 1)
        c.emit("scalar", A4)

        zbt = wbf[:, 4096 + s_ * 2048: 4096 + (s_ + 1) * 2048]
        v5 = c.inc("dv")

        def V5(xn=xn, zbt=zbt, nmu2=nmu2, rstd2=rstd2, a4=a4):
            nc.vector.wait_ge(sem["ac"], a4)
            nc.vector.tensor_scalar(zbt, xn, nmu2, rstd2, ALU.add, op1=ALU.mult).then_inc(sem["dv"], 1)
        c.emit("vector", V5)

        c.inc("st", 16)

        def L4(zbt=zbt, i=i, v5=v5):
            nc.sync.wait_ge(sem["dv"], v5)
            nc.sync.dma_start(out=d["zb"][tsl(i, 128), :], in_=zbt).then_inc(sem["st"], 16)
        c.emit("sync", L4)

    st_total = c.cnt["st"]

    def FS():
        nc.sync.wait_ge(sem["st"], st_total)
    c.emit("sync", FS)
    c.barrier()


P2_FG = list(range(12))


def emit_p2_qkv0(c):
    nc, d, sem = c.nc, c.dram, c.sem
    sm = c.sb["sm32"]
    c.lay("p2_afm", 0, KC * T_LOC, BF16)
    c.lay("p2_cos", 65536, 8192, BF16)
    c.lay("p2_sin", 73728, 8192, BF16)
    c.lay("p2_wsl", 81920, 8192, BF16)
    c.lay("p2_stg", 90112, 8192, BF16)
    afm = c.buf("p2_afm")
    cosb = c.buf("p2_cos")
    sinb = c.buf("p2_sin")
    wsl = c.buf("p2_wsl")
    stg = c.buf("p2_stg")
    ps = c.ps
    lnw = sm[:, 32:48]
    lnbc = sm[:, 48:64]
    lnb_bf = sm[:, 64:72].bitcast(BF16)
    beta = sm[:, 72:328].bitcast(BF16)
    tmpA = sm[:, 1024:1152]
    tmpB = sm[:, 1152:1280]
    tmpC = sm[:, 1280:1408]
    tmpD = sm[:, 1408:1536]
    tmpE = sm[:, 1536:1664]
    tmpF = sm[:, 1664:1792]
    evac32 = sm[:, 328:840]

    for kc in range(KC):
        c.inc("tp", 16)

        def TL(kc=kc):
            nc.sync.dma_start_transpose(
                afm[:, tsl(kc, T_LOC)], d["zb"][:, tsl(kc, 128)]).then_inc(sem["tp"], 16)
        c.emit("sync", TL)
    tp_z = c.cnt["tp"]

    c.inc("wd", 32)

    def GT():
        nc.gpsimd.dma_start(
            out=cosb[:].rearrange("p (a f) -> p a f", a=16),
            in_=d["cos0"][:].rearrange("(a p) f -> p a f", p=128)
        ).then_inc(sem["wd"], 16)
        nc.gpsimd.dma_start(
            out=sinb[:].rearrange("p (a f) -> p a f", a=16),
            in_=d["sin0"][:].rearrange("(a p) f -> p a f", p=128)
        ).then_inc(sem["wd"], 16)
    c.emit("gpsimd", GT)

    ld_ln = c.inc("ld", 48)

    def LLN():
        nc.sync.dma_start(out=lnw, in_=d["ln_w_cols"][:]).then_inc(sem["ld"], 16)
        nc.sync.dma_start(out=lnbc, in_=d["ln_b_cols"][:]).then_inc(sem["ld"], 16)
        nc.sync.dma_start(out=stg[0:1, 0:128], in_=d["ones1"][:]).then_inc(sem["ld"], 16)
    c.emit("sync", LLN)

    v_lnb = c.inc("dv")

    def VLNB():
        nc.vector.wait_ge(sem["ld"], ld_ln)
        nc.vector.tensor_copy(lnb_bf[:, 0:16], lnbc).then_inc(sem["dv"], 1)
    c.emit("vector", VLNB)

    gp_tab = c.cnt["wd"]

    def VWT():
        nc.vector.wait_ge(sem["wd"], gp_tab)
    c.emit("vector", VWT)
    evac_done = {}
    st_done = {}

    for fg in P2_FG:
        kind = "q" if fg < 4 else ("k" if fg < 8 else "v")
        mm_prev = c.cnt["mm"]
        g_sl = c.inc("wd", 16)

        def GSL(fg=fg, mm_prev=mm_prev):
            if fg > 0:
                nc.gpsimd.wait_ge(sem["mm"], mm_prev)
            nc.gpsimd.dma_start(
                out=wsl[:].rearrange("p (cc f) -> p cc f", cc=16),
                in_=d["W_in"][:, tsl(fg, 512)].rearrange("(cc p) f -> p cc f", p=128),
            ).then_inc(sem["wd"], 16)
        c.emit("gpsimd", GSL)

        mmb = c.inc("mm")
        prev_abe = c.cnt["ac"]

        def MB(fg=fg, g_sl=g_sl, prev_abe=prev_abe):
            nc.tensor.wait_ge(sem["wd"], g_sl)
            nc.tensor.wait_ge(sem["ac"], prev_abe)
            if fg == 0:
                nc.tensor.wait_ge(sem["dv"], v_lnb)
                nc.tensor.wait_ge(sem["ld"], ld_ln)
            mmi = None
            for kc in range(KC):
                mmi = nc.tensor.matmul(ps["d"][0:1, 0:512], lnb_bf[:, kc:kc + 1],
                                       wsl[:, tsl(kc, 512)], start=(kc == 0), stop=(kc == KC - 1))
            mmi.then_inc(sem["mm"], 1)
        c.emit("tensor", MB)

        a_be = c.inc("ac")

        def ABE(mmb=mmb):
            nc.scalar.wait_ge(sem["mm"], mmb)
            nc.scalar.copy(beta[0:1, 0:512], ps["d"][0:1, 0:512]).then_inc(sem["ac"], 1)
        c.emit("scalar", ABE)

        v_sc = c.inc("dv", KC)

        def VSC(mmb=mmb):
            nc.vector.wait_ge(sem["mm"], mmb)
            for kc in range(KC):
                nc.vector.tensor_scalar(wsl[:, tsl(kc, 512)], wsl[:, tsl(kc, 512)],
                                        lnw[:, kc:kc + 1], None, ALU.mult).then_inc(sem["dv"], 1)
        c.emit("vector", VSC)
        dv_slab = c.cnt["dv"]

        for tt in range(NT):
            pt = tt % NTS
            pbank = ps["a"] if tt % 2 == 0 else ps["b"]
            guard = evac_done.get((fg, tt - 2))
            if guard is None and fg > 0:
                guard = evac_done.get((fg - 1, NT - 2 + (tt % 2)))
            mq = c.inc("mm")

            def MM(fg=fg, tt=tt, pbank=pbank, guard=guard, dv_slab=dv_slab, a_be=a_be):
                nc.tensor.wait_ge(sem["dv"], dv_slab)
                nc.tensor.wait_ge(sem["ac"], a_be)
                if fg == 0 and tt == 0:
                    nc.tensor.wait_ge(sem["tp"], tp_z)
                if guard:
                    for s_g, v_g in guard:
                        nc.tensor.wait_ge(sem[s_g], v_g)
                for kc in range(KC):
                    nc.tensor.matmul(pbank[:, 0:512],
                                     afm[:, kc * T_LOC + tt * 128: kc * T_LOC + (tt + 1) * 128],
                                     wsl[:, tsl(kc, 512)], start=(kc == 0), stop=False)
                nc.tensor.matmul(pbank[:, 0:512], stg[0:1, 0:128],
                                 beta[0:1, 0:512], start=False, stop=True).then_inc(sem["mm"], 1)
            c.emit("tensor", MM)

            sslot = stg[:, 2048 + (tt % 3) * 2048: 2048 + (tt % 3) * 2048 + 512]
            slot_guard = st_done.get((fg, tt - 3))
            if slot_guard is None and fg > 0:
                slot_guard = st_done.get((fg - 1, NT - 3 + (tt % 3)))
            if kind == "v":
                a_e = c.inc("ac")

                def AE(pbank=pbank, sslot=sslot, mq=mq, slot_guard=slot_guard):
                    nc.scalar.wait_ge(sem["mm"], mq)
                    if slot_guard:
                        nc.scalar.wait_ge(sem["st"], slot_guard)
                    nc.scalar.copy(sslot, pbank[:, 0:512]).then_inc(sem["ac"], 1)
                c.emit("scalar", AE)
                c.inc("st", 16)

                def SE(sslot=sslot, fg=fg, tt=tt, a_e=a_e):
                    nc.sync.wait_ge(sem["ac"], a_e)
                    nc.sync.dma_start(out=d["v0"][tsl(tt, 128), tsl(fg - 8, 512)],
                                      in_=sslot).then_inc(sem["st"], 16)
                c.emit("sync", SE)
                evac_done[(fg, tt)] = [("ac", a_e)]
                st_done[(fg, tt)] = c.cnt["st"]
            else:
                a_ev = c.inc("ac")
                prev_vr = c.cnt["dv"]

                def AEV(pbank=pbank, mq=mq, prev_vr=prev_vr):
                    nc.scalar.wait_ge(sem["mm"], mq)
                    nc.scalar.wait_ge(sem["dv"], prev_vr)
                    nc.scalar.copy(evac32, pbank[:, 0:512]).then_inc(sem["ac"], 1)
                c.emit("scalar", AEV)
                rotv = evac32.rearrange("p (h x) -> p h x", h=4)[:, :, 0:64]
                rotp = rotv.rearrange("p h (j t) -> p h j t", t=2)
                Ev = rotp[:, :, :, 0]
                Od = rotp[:, :, :, 1]
                pass_ = evac32.rearrange("p (h x) -> p h x", h=4)[:, :, 64:128]
                cbase = pt * 512 + (fg % 4) * 128
                ct = cosb[:, cbase:cbase + 128].rearrange("p (h j) -> p h j", h=4)
                st_ = sinb[:, cbase:cbase + 128].rearrange("p (h j) -> p h j", h=4)
                oview = sslot.rearrange("p (h x) -> p h x", h=4)
                orot = oview[:, :, 0:64].rearrange("p h (j t) -> p h j t", t=2)
                oEv, oOd = orot[:, :, :, 0], orot[:, :, :, 1]
                opass = oview[:, :, 64:128]
                t3A = tmpA.rearrange("p (h j) -> p h j", h=4)
                t3B = tmpB.rearrange("p (h j) -> p h j", h=4)
                t3C = tmpC.rearrange("p (h j) -> p h j", h=4)
                t3D = tmpD.rearrange("p (h j) -> p h j", h=4)
                t3E = tmpE.rearrange("p (h j) -> p h j", h=4)
                t3F = tmpF.rearrange("p (h j) -> p h j", h=4)
                sc = RSC if kind == "q" else 1.0
                v_r = c.inc("dv", 2)
                a_p = c.inc("ac")

                def VR(Ev=Ev, Od=Od, ct=ct, st_=st_, t3A=t3A, t3B=t3B, t3C=t3C, t3D=t3D,
                       t3E=t3E, t3F=t3F, oEv=oEv, oOd=oOd, sc=sc, mq=mq, a_ev=a_ev,
                       slot_guard=slot_guard):
                    nc.vector.wait_ge(sem["ac"], a_ev)
                    if slot_guard:
                        nc.vector.wait_ge(sem["st"], slot_guard)
                    nc.vector.tensor_scalar(t3A, Ev, sc, None, ALU.mult)
                    nc.vector.tensor_scalar(t3B, Od, sc, None, ALU.mult)
                    nc.vector.tensor_tensor(t3C, t3A, ct, ALU.mult)
                    nc.vector.tensor_tensor(t3D, t3B, st_, ALU.mult)
                    nc.vector.tensor_tensor(t3E, t3B, ct, ALU.mult)
                    nc.vector.tensor_tensor(t3F, t3A, st_, ALU.mult)
                    nc.vector.tensor_tensor(oEv, t3C, t3D, ALU.subtract).then_inc(sem["dv"], 1)
                    nc.vector.tensor_tensor(oOd, t3E, t3F, ALU.add).then_inc(sem["dv"], 1)
                c.emit("vector", VR)

                def AP_(pass_=pass_, opass=opass, sc=sc, a_ev=a_ev, slot_guard=slot_guard):
                    nc.scalar.wait_ge(sem["ac"], a_ev)
                    if slot_guard:
                        nc.scalar.wait_ge(sem["st"], slot_guard)
                    nc.scalar.activation(opass, pass_, AF.Identity, scale=sc).then_inc(sem["ac"], 1)
                c.emit("scalar", AP_)

                dst = d["q0"] if kind == "q" else d["k0"]
                c.inc("st", 16)

                def SE2(sslot=sslot, dst=dst, fg=fg, tt=tt, v_r=v_r, a_p=a_p):
                    nc.sync.wait_ge(sem["dv"], v_r)
                    nc.sync.wait_ge(sem["ac"], a_p)
                    nc.sync.dma_start(out=dst[tsl(tt, 128), tsl(fg % 4, 512)],
                                      in_=sslot).then_inc(sem["st"], 16)
                c.emit("sync", SE2)
                evac_done[(fg, tt)] = [("ac", a_ev)]
                st_done[(fg, tt)] = c.cnt["st"]

    st_total = c.cnt["st"]

    def FS():
        nc.sync.wait_ge(sem["st"], st_total)
    c.emit("sync", FS)
    c.barrier()


def emit_p3_attn0(c):
    nc, d, sem = c.nc, c.dram, c.sem
    sm = c.sb["sm32"]
    ps = c.ps
    c.lay("p3_ctx", 0, 32768, BF16)
    c.lay("p3_wout", 32768, 32768, BF16)
    c.lay("p3_qf", 65536, 4096, BF16)
    c.lay("p3_kf", 69632, 4096, BF16)
    c.lay("p3_vt", 73728, 4096, BF16)
    c.lay("p3_pb", 77824, 4096, BF16)
    c.lay("p3_pn", 81920, 4096, BF16)
    c.lay("p3_pT", 86016, 8192, BF16)
    c.lay("p3_xb", 94208, 2048, F32)
    ctxb = c.buf("p3_ctx")
    woutb = c.buf("p3_wout")
    qfb = c.buf("p3_qf")
    kfb = c.buf("p3_kf")
    vtb = c.buf("p3_vt")
    pbb = c.buf("p3_pb")
    pnb = c.buf("p3_pn")
    pTb = c.buf("p3_pT")
    xbt = [c.buf("p3_xb"), c.buf("p3_xb")]
    one_col = c.const(1.0)

    bank_free = {}
    pb_free = [None, None]
    pn_free = [None, None]
    pT_free = None
    s4_free = None
    prev_sumread = 0
    wait_all = _mk_waiter(sem)

    wd_wout = c.inc("wd", 16)

    def GW():
        nc.gpsimd.dma_start(
            out=woutb[:].rearrange("p (cc f) -> p cc f", cc=16),
            in_=d["W_out"][:].rearrange("(cc p) f -> p cc f", p=128),
        ).then_inc(sem["wd"], 16)
    c.emit("gpsimd", GW)

    prev_head_mm = 0
    for seq in range(B_LOC):
        for h in range(H0):
            sl = h % 2
            qf = qfb[:, tsl(sl, 2048)]
            kf = kfb[:, tsl(sl, 2048)]
            vt = vtb[:, tsl(sl, 2048)]
            guard_mm = prev_head_mm
            tq = c.inc("tp", 32)
            lv = c.inc("ld", 16)

            def LD(seq=seq, h=h, qf=qf, kf=kf, vt=vt, guard_mm=guard_mm):
                if guard_mm:
                    nc.sync.wait_ge(sem["mm"], guard_mm)
                nc.sync.dma_start_transpose(
                    qf, d["q0"][tsl(seq, S), tsl(h, 128)]).then_inc(sem["tp"], 16)
                nc.sync.dma_start_transpose(
                    kf, d["k0"][tsl(seq, S), tsl(h, 128)]).then_inc(sem["tp"], 16)
                nc.sync.dma_start(
                    out=vt[:].rearrange("p (cc f) -> p cc f", cc=16),
                    in_=d["v0"][tsl(seq, S), tsl(h, 128)].rearrange("(cc p) f -> p cc f", p=128),
                ).then_inc(sem["ld"], 16)
            c.emit("sync", LD)
            head_mm_before = c.cnt["mm"]

            for g in range(4):
                for q4 in range(4):
                    qt = g * 4 + q4
                    ms = c.inc("mm")

                    def MS(qf=qf, kf=kf, qt=qt, tq=tq, s4g=s4_free):
                        nc.tensor.wait_ge(sem["tp"], tq)
                        wait_all(nc.tensor, [s4g])
                        mmi = None
                        for cc in range(4):
                            mmi = nc.tensor.matmul(ps["s4"][:, tsl(cc, 512)],
                                                   qf[:, tsl(qt, 128)], kf[:, tsl(cc, 512)],
                                                   start=True, stop=True)
                        mmi.then_inc(sem["mm"], 1)
                    c.emit("tensor", MS)

                    nmx = sm[:, 0:1]
                    sums = sm[:, 1:2]
                    inv = sm[:, 2:3]
                    vmx = c.inc("dv")

                    def VMX(ms=ms, nmx=nmx):
                        nc.vector.wait_ge(sem["mm"], ms)
                        nc.vector.reduce_max(nmx, ps["s4"][:], axis=AX.X, negate=True).then_inc(sem["dv"], 1)
                    c.emit("vector", VMX)

                    pb = pbb[:, tsl(qt % 2, 2048)]
                    aex = c.inc("ac")

                    def AEXP(vmx=vmx, pb=pb, nmx=nmx, sums=sums,
                             gpb=pb_free[qt % 2], psv=prev_sumread):
                        nc.scalar.wait_ge(sem["dv"], vmx)
                        wait_all(nc.scalar, [gpb])
                        if psv:
                            nc.scalar.wait_ge(sem["dv"], psv)
                        nc.scalar.activation(pb, ps["s4"][:], AF.Exp, bias=nmx,
                                             accum_out=sums).then_inc(sem["ac"], 1)
                    c.emit("scalar", AEXP)
                    s4_free = ("ac", aex)

                    pn = pnb[:, tsl(qt % 2, 2048)]
                    gi = c.inc("dv", 2)
                    vsc = gi

                    def VSCL(aex=aex, pn=pn, pb=pb, inv=inv, sums=sums,
                             gpn=pn_free[qt % 2]):
                        nc.vector.wait_ge(sem["ac"], aex)
                        wait_all(nc.vector, [gpn])
                        nc.vector.reciprocal(inv, sums).then_inc(sem["dv"], 1)
                        nc.vector.tensor_copy(c.const_tile[:, 62:63], c.const_tile[:, 63:64])
                        nc.vector.tensor_scalar(pn, pb, inv, None, ALU.mult).then_inc(sem["dv"], 1)
                    c.emit("vector", VSCL)
                    pb_free[qt % 2] = ("dv", vsc)
                    prev_sumread = vsc

                    tpq = c.inc("tp", 16)

                    def TPQ(vsc=vsc, pn=pn, q4=q4, gpT=pT_free):
                        nc.sync.wait_ge(sem["dv"], vsc)
                        wait_all(nc.sync, [gpT])
                        nc.sync.dma_start_transpose(
                            pTb[:].rearrange("p (cc g2) -> p cc g2", cc=16)[:, :, tsl(q4, 128)],
                            pn.rearrange("p (cc f) -> p cc f", cc=16),
                        ).then_inc(sem["tp"], 16)
                    c.emit("sync", TPQ)
                    pn_free[qt % 2] = ("tp", tpq)
                    tp_group_done = c.cnt["tp"]

                cps = ps["a"] if g % 2 == 0 else ps["b"]
                bkey = "a" if g % 2 == 0 else "b"
                mc = c.inc("mm")

                def MC(cps=cps, vt=vt, lv=lv, tp_need=tp_group_done,
                       gbank=bank_free.get(bkey)):
                    nc.tensor.wait_ge(sem["tp"], tp_need)
                    nc.tensor.wait_ge(sem["ld"], lv)
                    wait_all(nc.tensor, [gbank])
                    mmi = None
                    for cc in range(16):
                        mmi = nc.tensor.matmul(cps[:, 0:512], vt[:, tsl(cc, 128)],
                                               pTb[:, tsl(cc, 512)],
                                               start=(cc == 0), stop=(cc == 15))
                    mmi.then_inc(sem["mm"], 1)
                c.emit("tensor", MC)
                pT_free = ("mm", mc)

                ce = c.inc("ac")

                def CEV(mc=mc, cps=cps, h=h, g=g):
                    nc.scalar.wait_ge(sem["mm"], mc)
                    nc.scalar.copy(ctxb[:, h * 2048 + g * 512: h * 2048 + (g + 1) * 512],
                                   cps[:, 0:512]).then_inc(sem["ac"], 1)
                c.emit("scalar", CEV)
                bank_free[bkey] = ("ac", ce)
            prev_head_mm = head_mm_before

        cev_all = c.cnt["ac"]
        for qt in range(NTS):
            i_glob = seq * NTS + qt
            xslot = xbt[qt % 2]
            lxb = c.inc("ld", 16)

            def LXB(xslot=xslot, i_glob=i_glob):
                nc.sync.dma_start(out=xslot[:, 0:2048],
                                  in_=d["xb"][tsl(i_glob, 128), :]).then_inc(sem["ld"], 16)
            c.emit("sync", LXB)

            pbanks = {0: ("a", ps["a"]), 1: ("b", ps["b"]), 2: ("c", ps["c"]), 3: ("d", ps["d"])}
            for og in range(4):
                bkey, bank = pbanks[og]
                mp = c.inc("mm")

                def MP(qt=qt, og=og, bank=bank, cev_all=cev_all, wd_wout=wd_wout,
                       gbank=bank_free.get(bkey)):
                    nc.tensor.wait_ge(sem["ac"], cev_all)
                    nc.tensor.wait_ge(sem["wd"], wd_wout)
                    wait_all(nc.tensor, [gbank])
                    mmi = None
                    for hh in range(16):
                        mmi = nc.tensor.matmul(
                            bank[:, 0:512],
                            ctxb[:, hh * 2048 + qt * 128: hh * 2048 + (qt + 1) * 128],
                            woutb[:, hh * 2048 + og * 512: hh * 2048 + (og + 1) * 512],
                            start=(hh == 0), stop=(hh == 15))
                    mmi.then_inc(sem["mm"], 1)
                c.emit("tensor", MP)

                vp = c.inc("dv")

                def VPR(mp=mp, og=og, bank=bank, xslot=xslot, lxb=lxb):
                    nc.vector.wait_ge(sem["mm"], mp)
                    nc.vector.wait_ge(sem["ld"], lxb)
                    nc.vector.tensor_tensor(xslot[:, tsl(og, 512)], bank[:, 0:512],
                                            xslot[:, tsl(og, 512)], ALU.add).then_inc(sem["dv"], 1)
                c.emit("vector", VPR)
                bank_free[bkey] = ("dv", vp)

            vlast = c.cnt["dv"]
            c.inc("st", 16)

            def SX(xslot=xslot, i_glob=i_glob, vlast=vlast):
                nc.sync.wait_ge(sem["dv"], vlast)
                nc.sync.dma_start(out=d["x1a"][tsl(i_glob, 128), :],
                                  in_=xslot[:, 0:2048]).then_inc(sem["st"], 16)
            c.emit("sync", SX)
        mm_proj_done = c.cnt["mm"]
        prev_head_mm = max(prev_head_mm, mm_proj_done)

    st_total = c.cnt["st"]

    def FS():
        nc.sync.wait_ge(sem["st"], st_total)
    c.emit("sync", FS)
    c.barrier()


def emit_p4_ffn0(c):
    nc, d, sem = c.nc, c.dram, c.sem
    sm = c.sb["sm32"]
    ps = c.ps
    wait_all = _mk_waiter(sem)

    c.lay("p4_dwi", 0, 4096, I32)
    c.lay("p4_dwb", 16384, 4096, BF16)
    c.lay("p4_dwt", 20480, 4096, BF16)
    dwi = c.buf("p4_dwi")
    dwb = c.buf("p4_dwb")
    dwt = c.buf("p4_dwt")
    bias_c = sm[:, 8:9]
    scale_c = sm[:, 9:25]

    ld_sc = c.inc("ld", 16)

    def LSC():
        nc.sync.dma_start(out=scale_c, in_=d["down_scale_cols"][:]).then_inc(sem["ld"], 16)
    c.emit("sync", LSC)

    prev_tp4a = 0
    prev_st4a = 0
    prev_ac4a = 0
    for hc in range(16):
        for half in range(2):
            ld_i = c.inc("ld", 16)

            def L4A(hc=hc, half=half, pst=prev_st4a):
                if pst:
                    nc.sync.wait_ge(sem["st"], pst)
                nc.sync.dma_start(out=dwi[:],
                                  in_=d["down_w"][tsl(hc, 128), tsl(half, 4096)]).then_inc(sem["ld"], 16)
            c.emit("sync", L4A)

            gb = c.inc("gp")

            def GB(hc=hc, pac=prev_ac4a):
                nc.gpsimd.wait_ge(sem["ld"], ld_sc)
                if pac:
                    nc.gpsimd.wait_ge(sem["ac"], pac)
                nc.gpsimd.tensor_scalar(bias_c, scale_c[:, hc:hc + 1], -128.0, None, ALU.mult).then_inc(sem["gp"], 1)
            c.emit("gpsimd", GB)

            aq = c.inc("ac")

            def A4A(hc=hc, ld_i=ld_i, gb=gb, ptp=prev_tp4a):
                nc.scalar.wait_ge(sem["ld"], ld_i)
                nc.scalar.wait_ge(sem["gp"], gb)
                if ptp:
                    nc.scalar.wait_ge(sem["tp"], ptp)
                nc.scalar.activation(dwb[:], dwi[:], AF.Identity,
                                     bias=bias_c, scale=scale_c[:, hc:hc + 1]).then_inc(sem["ac"], 1)
            c.emit("scalar", A4A)
            prev_ac4a = aq

            tpq = c.inc("tp", 16)

            def T4A(aq=aq, pst=prev_st4a):
                nc.sync.wait_ge(sem["ac"], aq)
                if pst:
                    nc.sync.wait_ge(sem["st"], pst)
                nc.sync.dma_start_transpose(
                    dwt[:].rearrange("p (ic f) -> p ic f", ic=32),
                    dwb[:].rearrange("p (ic f) -> p ic f", ic=32),
                ).then_inc(sem["tp"], 16)
            c.emit("sync", T4A)
            prev_tp4a = tpq

            stq = c.inc("st", 16)

            def S4A(hc=hc, half=half, tpq=tpq):
                nc.sync.wait_ge(sem["tp"], tpq)
                nc.sync.dma_start(
                    out=d["wdt"][tsl(half, 4096), tsl(hc, 128)].rearrange("(ic p) h -> p ic h", p=128),
                    in_=dwt[:].rearrange("p (ic f) -> p ic f", ic=32),
                ).then_inc(sem["st"], 16)
            c.emit("sync", S4A)
            prev_st4a = stq

    c.lay("p4_afm", 0, KC * T_LOC, BF16)
    c.lay("p4_gwi", 65536, 2048, I32)
    c.lay("p4_uwi", 73728, 2048, I32)
    c.lay("p4_gbf", 81920, 2048, BF16)
    c.lay("p4_ubf", 83968, 2048, BF16)
    c.lay("p4_gT", 86016, 2048, BF16)
    c.lay("p4_uT", 88064, 2048, BF16)
    c.lay("p4_hst", 90112, 4096, BF16)
    afm = c.buf("p4_afm")
    gwi = c.buf("p4_gwi")
    uwi = c.buf("p4_uwi")
    gbf = c.buf("p4_gbf")
    ubf = c.buf("p4_ubf")
    gT = c.buf("p4_gT")
    uT = c.buf("p4_uT")
    hst = c.buf("p4_hst")
    gsc = sm[:, 32:96]
    usc = sm[:, 96:160]
    gbias = sm[:, 8:9]
    ubias = sm[:, 25:26]
    sst = sm[:, 1024:1280].bitcast(BF16)
    st4a_total = c.cnt["st"]

    def FS4A():
        nc.sync.wait_ge(sem["st"], st4a_total)
    c.emit("sync", FS4A)
    c.barrier()

    for kc in range(KC):
        c.inc("tp", 16)

        def TLX(kc=kc):
            nc.sync.dma_start_transpose(
                afm[:, tsl(kc, T_LOC)], d["xn0b"][:, tsl(kc, 128)]).then_inc(sem["tp"], 16)
        c.emit("sync", TLX)
    tp_x = c.cnt["tp"]

    ld_gs = c.inc("ld", 32)

    def LGS():
        nc.sync.dma_start(out=gsc, in_=d["gate_scale_cols"][:]).then_inc(sem["ld"], 16)
        nc.sync.dma_start(out=usc, in_=d["up_scale_cols"][:]).then_inc(sem["ld"], 16)
    c.emit("sync", LGS)

    bank_free = {}
    prev_mm_ic = 0
    prev_tp_ic = 0
    prev_st_hh = 0
    prev_dv_sil = 0
    for ic in range(64):
        ld_i = c.inc("ld", 32)

        def L4B(ic=ic, pmm=prev_mm_ic):
            if pmm:
                nc.sync.wait_ge(sem["mm"], pmm)
            nc.sync.dma_start(out=gwi[:], in_=d["gate_w"][tsl(ic, 128), :]).then_inc(sem["ld"], 16)
            nc.sync.dma_start(out=uwi[:], in_=d["up_w"][tsl(ic, 128), :]).then_inc(sem["ld"], 16)
        c.emit("sync", L4B)

        gb2 = c.inc("gp", 2)
        prev_adq = c.cnt["ac"]

        def GB2(ic=ic, pac=prev_adq):
            nc.gpsimd.wait_ge(sem["ld"], ld_gs)
            if pac:
                nc.gpsimd.wait_ge(sem["ac"], pac)
            nc.gpsimd.tensor_scalar(gbias, gsc[:, ic:ic + 1], -128.0, None, ALU.mult).then_inc(sem["gp"], 1)
            nc.gpsimd.tensor_scalar(ubias, usc[:, ic:ic + 1], -128.0, None, ALU.mult).then_inc(sem["gp"], 1)
        c.emit("gpsimd", GB2)

        adq = c.inc("ac", 2)

        def A4B(ic=ic, ld_i=ld_i, gb2=gb2, ptp=prev_tp_ic):
            nc.scalar.wait_ge(sem["ld"], ld_i)
            nc.scalar.wait_ge(sem["gp"], gb2)
            if ptp:
                nc.scalar.wait_ge(sem["tp"], ptp)
            nc.scalar.activation(gbf[:], gwi[:], AF.Identity,
                                 bias=gbias, scale=gsc[:, ic:ic + 1]).then_inc(sem["ac"], 1)
            nc.scalar.activation(ubf[:], uwi[:], AF.Identity,
                                 bias=ubias, scale=usc[:, ic:ic + 1]).then_inc(sem["ac"], 1)
        c.emit("scalar", A4B)

        tp2 = c.inc("tp", 32)

        def T4B(adq=adq, pmm=prev_mm_ic):
            nc.sync.wait_ge(sem["ac"], adq)
            if pmm:
                nc.sync.wait_ge(sem["mm"], pmm)
            nc.sync.dma_start_transpose(
                gT[:].rearrange("p (hc f) -> p hc f", hc=16),
                gbf[:].rearrange("p (hc f) -> p hc f", hc=16)).then_inc(sem["tp"], 16)
            nc.sync.dma_start_transpose(
                uT[:].rearrange("p (hc f) -> p hc f", hc=16),
                ubf[:].rearrange("p (hc f) -> p hc f", hc=16)).then_inc(sem["tp"], 16)
        c.emit("sync", T4B)
        prev_tp_ic = tp2

        for tg in range(8):
            pg = ps["a"] if tg % 2 == 0 else ps["c"]
            pu = ps["b"] if tg % 2 == 0 else ps["d"]
            kg = "a" if tg % 2 == 0 else "cc"
            ku = "b" if tg % 2 == 0 else "dd"
            mg = c.inc("mm", 2)

            def MG(tg=tg, pg=pg, pu=pu, tp2=tp2,
                   ga=bank_free.get(kg), gb_=bank_free.get(ku)):
                nc.tensor.wait_ge(sem["tp"], tp2)
                nc.tensor.wait_ge(sem["tp"], tp_x)
                wait_all(nc.tensor, [ga, gb_])
                mmi = None
                for hc in range(KC):
                    mmi = nc.tensor.matmul(pg[:, 0:512], gT[:, tsl(hc, 128)],
                                           afm[:, hc * T_LOC + tg * 512: hc * T_LOC + (tg + 1) * 512],
                                           start=(hc == 0), stop=(hc == KC - 1))
                mmi.then_inc(sem["mm"], 1)
                for hc in range(KC):
                    mmi = nc.tensor.matmul(pu[:, 0:512], uT[:, tsl(hc, 128)],
                                           afm[:, hc * T_LOC + tg * 512: hc * T_LOC + (tg + 1) * 512],
                                           start=(hc == 0), stop=(hc == KC - 1))
                mmi.then_inc(sem["mm"], 1)
            c.emit("tensor", MG)

            asi = c.inc("ac")

            def ASI(mg=mg, pg=pg, pdv=prev_dv_sil):
                nc.scalar.wait_ge(sem["mm"], mg - 1)
                if pdv:
                    nc.scalar.wait_ge(sem["dv"], pdv)
                nc.scalar.activation(sst, pg[:, 0:512], AF.Silu).then_inc(sem["ac"], 1)
            c.emit("scalar", ASI)
            bank_free[kg] = ("ac", asi)

            vmu = c.inc("dv")

            def VMU(mg=mg, asi=asi, pu=pu, tg=tg, pst=prev_st_hh):
                nc.vector.wait_ge(sem["mm"], mg)
                nc.vector.wait_ge(sem["ac"], asi)
                if pst:
                    nc.vector.wait_ge(sem["st"], pst)
                nc.vector.tensor_tensor(hst[:, tsl(tg, 512)], sst, pu[:, 0:512],
                                        ALU.mult).then_inc(sem["dv"], 1)
            c.emit("vector", VMU)
            bank_free[ku] = ("dv", vmu)
            prev_dv_sil = vmu

        sth = c.inc("st", 16)
        vlast = c.cnt["dv"]

        def SHH(ic=ic, vlast=vlast):
            nc.sync.wait_ge(sem["dv"], vlast)
            nc.sync.dma_start(out=d["hh"][tsl(ic, 128), :], in_=hst[:]).then_inc(sem["st"], 16)
        c.emit("sync", SHH)
        prev_st_hh = sth
        prev_mm_ic = c.cnt["mm"]

    st4b_total = c.cnt["st"]

    def FS4B():
        nc.sync.wait_ge(sem["st"], st4b_total)
    c.emit("sync", FS4B)
    c.barrier()

    c.lay("p4_hht", 0, 65536, BF16)
    c.lay("p4_wds", 65536, 32768, BF16)
    hht = c.buf("p4_hht")
    wds = c.buf("p4_wds")
    stages = [sm[:, 1024:1280].bitcast(BF16), sm[:, 1280:1536].bitcast(BF16)]

    bank_free = {}
    stage_free = [None, None]
    prev_mm_tb = 0
    for tb in range(4):
        ld_h = c.inc("ld", 16)

        def LHH(tb=tb, pmm=prev_mm_tb):
            if pmm:
                nc.sync.wait_ge(sem["mm"], pmm)
            nc.sync.dma_start(
                out=hht[:].rearrange("p (ic t) -> p ic t", ic=64),
                in_=d["hh"][:, tsl(tb, 1024)].rearrange("(ic p) t -> p ic t", p=128),
            ).then_inc(sem["ld"], 16)
        c.emit("sync", LHH)

        for hg in range(4):
            mm_before_hg = c.cnt["mm"]
            ld_w = c.inc("ld", 16)

            def LWD(hg=hg, pmm=mm_before_hg, first=(tb == 0 and hg == 0)):
                if not first:
                    nc.sync.wait_ge(sem["mm"], pmm)
                nc.sync.dma_start(
                    out=wds[:].rearrange("p (ic h) -> p ic h", ic=64),
                    in_=d["wdt"][:, tsl(hg, 512)].rearrange("(ic p) h -> p ic h", p=128),
                ).then_inc(sem["ld"], 16)
            c.emit("sync", LWD)

            for tc in range(8):
                i_glob = tb * 8 + tc
                pd = ps["a"] if tc % 2 == 0 else ps["b"]
                kd = "a" if tc % 2 == 0 else "b"
                md = c.inc("mm")

                def MD(tc=tc, pd=pd, ld_w=ld_w, ld_h=ld_h, gk=bank_free.get(kd)):
                    nc.tensor.wait_ge(sem["ld"], ld_w)
                    nc.tensor.wait_ge(sem["ld"], ld_h)
                    wait_all(nc.tensor, [gk])
                    mmi = None
                    for ic in range(64):
                        mmi = nc.tensor.matmul(
                            pd[:, 0:512],
                            hht[:, ic * 1024 + tc * 128: ic * 1024 + (tc + 1) * 128],
                            wds[:, tsl(ic, 512)],
                            start=(ic == 0), stop=(ic == 63))
                    mmi.then_inc(sem["mm"], 1)
                c.emit("tensor", MD)

                aev = c.inc("ac")
                stg_ = stages[tc % 2]

                def AEVD(md=md, pd=pd, stg_=stg_, gs=stage_free[tc % 2]):
                    nc.scalar.wait_ge(sem["mm"], md)
                    wait_all(nc.scalar, [gs])
                    nc.scalar.copy(stg_, pd[:, 0:512]).then_inc(sem["ac"], 1)
                c.emit("scalar", AEVD)
                bank_free[kd] = ("ac", aev)

                std = c.inc("st", 16)

                def SD(i_glob=i_glob, hg=hg, aev=aev, stg_=stg_):
                    nc.sync.wait_ge(sem["ac"], aev)
                    nc.sync.dma_start(out=d["ffn0b"][tsl(i_glob, 128), tsl(hg, 512)],
                                      in_=stg_).then_inc(sem["st"], 16)
                c.emit("sync", SD)
                stage_free[tc % 2] = ("st", std)
        prev_mm_tb = c.cnt["mm"]

    st_total = c.cnt["st"]

    def FS4C():
        nc.sync.wait_ge(sem["st"], st_total)
    c.emit("sync", FS4C)
    c.barrier()


def emit_p5_ln1(c):
    nc, d, sem = c.nc, c.dram, c.sem
    sm = c.sb["sm32"]
    spacer = c.const_tile[:, 63:64]
    c.lay("p5_f32", 0, 10240, F32)
    c.lay("p5_bf", 20480, 12288, BF16)
    t32 = c.buf("p5_f32")
    wbf = c.buf("p5_bf")
    invD = 1.0 / D
    eps = 1e-5
    gp_after = {}

    for i in range(NT):
        s_ = i % 2
        xv = t32[:, tsl(s_, 2048)]
        xn = t32[:, 4096 + s_ * 2048: 4096 + (s_ + 1) * 2048]
        junk = t32[:, 8192:10240]
        fbt = wbf[:, 8192 + s_ * 2048: 8192 + (s_ + 1) * 2048]
        col = lambda k: sm[:, s_ * 16 + k: s_ * 16 + k + 1]
        nmu, ssq, var_, rstd = col(0), col(1), col(2), col(3)

        ld_i = c.inc("ld", 32)

        def L1(i=i, xv=xv, fbt=fbt):
            nc.sync.dma_start(out=xv, in_=d["x1a"][tsl(i, 128), :]).then_inc(sem["ld"], 16)
            nc.sync.dma_start(out=fbt, in_=d["ffn0b"][tsl(i, 128), :]).then_inc(sem["ld"], 16)
        c.emit("sync", L1)

        gp_guard = gp_after.get(i - 2, 0)
        g0 = c.inc("gp")

        def G0(xv=xv, fbt=fbt, ld_i=ld_i, gp_guard=gp_guard):
            nc.gpsimd.wait_ge(sem["ld"], ld_i)
            nc.gpsimd.tensor_tensor(xv, xv, fbt, ALU.add).then_inc(sem["gp"], 1)
        c.emit("gpsimd", G0)

        c.inc("st", 16)

        def SX1(xv=xv, i=i, g0=g0):
            nc.sync.wait_ge(sem["gp"], g0)
            nc.sync.dma_start(out=d["x1"][tsl(i, 128), :], in_=xv).then_inc(sem["st"], 16)
        c.emit("sync", SX1)

        v1 = c.inc("dv")

        def V1(xv=xv, nmu=nmu, g0=g0, gp_guard=gp_guard):
            nc.vector.wait_ge(sem["gp"], g0)
            if gp_guard:
                nc.vector.wait_ge(sem["gp"], gp_guard)
            nc.vector.reduce_sum(nmu, xv, axis=AX.X, negate=True).then_inc(sem["dv"], 1)
        c.emit("vector", V1)

        a1 = c.inc("ac")

        def A1(xv=xv, junk=junk, ssq=ssq, g0=g0):
            nc.scalar.wait_ge(sem["gp"], g0)
            nc.scalar.activation(junk, xv, AF.Square, accum_out=ssq).then_inc(sem["ac"], 1)
        c.emit("scalar", A1)

        g1 = c.inc("gp")

        def G1(nmu=nmu, ssq=ssq, var_=var_, v1=v1, a1=a1):
            nc.gpsimd.wait_ge(sem["dv"], v1)
            nc.gpsimd.wait_ge(sem["ac"], a1)
            nc.gpsimd.tensor_scalar(nmu, nmu, invD, None, ALU.mult)
            nc.gpsimd.tensor_tensor(var_, nmu, nmu, ALU.mult)
            nc.gpsimd.tensor_scalar(ssq, ssq, invD, None, ALU.mult)
            nc.gpsimd.tensor_tensor(var_, ssq, var_, ALU.subtract).then_inc(sem["gp"], 1)
        c.emit("gpsimd", G1)
        gp_after[i] = c.cnt["gp"]

        a2 = c.inc("ac", 2)

        def A2(var_=var_, rstd=rstd, g1=g1):
            nc.scalar.wait_ge(sem["gp"], g1)
            nc.scalar.activation(rstd, var_, AF.Ln, bias=c.const(eps)).then_inc(sem["ac"], 1)
            nc.scalar.activation(spacer, spacer, AF.Identity)
            nc.scalar.activation(rstd, rstd, AF.Exp, scale=-0.5).then_inc(sem["ac"], 1)
        c.emit("scalar", A2)

        v2 = c.inc("dv")

        def V2(xv=xv, xn=xn, nmu=nmu, rstd=rstd, a2=a2):
            nc.vector.wait_ge(sem["ac"], a2)
            nc.vector.tensor_scalar(xn, xv, nmu, rstd, ALU.add, op1=ALU.mult).then_inc(sem["dv"], 1)
        c.emit("vector", V2)

        xnb = wbf[:, tsl(s_, 2048)]
        v3 = c.inc("dv")

        def V3(xn=xn, xnb=xnb):
            nc.vector.tensor_copy(xnb, xn).then_inc(sem["dv"], 1)
        c.emit("vector", V3)

        c.inc("st", 16)

        def L3(xnb=xnb, i=i, v3=v3):
            nc.sync.wait_ge(sem["dv"], v3)
            nc.sync.dma_start(out=d["xn1b"][tsl(i, 128), :], in_=xnb).then_inc(sem["st"], 16)
        c.emit("sync", L3)

        g2 = c.inc("gp")

        def G2(xn=xn, xnb=xnb, fbt=fbt, v3=v3):
            nc.gpsimd.wait_ge(sem["dv"], v3)
            nc.gpsimd.tensor_tensor(fbt, xn, xnb, ALU.subtract).then_inc(sem["gp"], 1)
        c.emit("gpsimd", G2)

        c.inc("st", 16)

        def L4(fbt=fbt, i=i, g2=g2):
            nc.sync.wait_ge(sem["gp"], g2)
            nc.sync.dma_start(out=d["xn1lo"][tsl(i, 128), :], in_=fbt).then_inc(sem["st"], 16)
        c.emit("sync", L4)

    st_total = c.cnt["st"]

    def FS():
        nc.sync.wait_ge(sem["st"], st_total)
    c.emit("sync", FS)
    c.barrier()


def emit_p6_qkv1(c):
    nc, d, sem = c.nc, c.dram, c.sem
    sm = c.sb["sm32"]
    ps = c.ps
    wait_all = _mk_waiter(sem)
    spacer = c.const_tile[:, 63:64]
    spacer2 = c.const_tile[:, 62:63]
    eps1 = 1e-6
    c.lay("p6_afm", 0, KC * T_LOC, BF16)
    c.lay("p6_wsl", 65536, 8192, BF16)
    c.lay("p6_q32", 73728, 512, F32)
    c.lay("p6_qn32", 74752, 512, F32)
    c.lay("p6_ctab", 75776, 256, BF16)
    c.lay("p6_stab", 76032, 256, BF16)
    c.lay("p6_ost", 77824, 2048, BF16)
    c.lay("p6_gqb", 79872, 2048, F32)
    c.lay("p6_gkb", 83968, 512, F32)
    c.lay("p6_whi", 84992, 128, BF16)
    c.lay("p6_wlo", 85120, 128, BF16)
    c.lay("p6_wr", 85248, 128, F32)
    afm = c.buf("p6_afm")
    wsl = c.buf("p6_wsl")
    q32 = c.buf("p6_q32")
    qn32 = c.buf("p6_qn32")
    ctab = c.buf("p6_ctab")
    stab = c.buf("p6_stab")
    ost = c.buf("p6_ost")
    gqb = c.buf("p6_gqb")
    gkb = c.buf("p6_gkb")
    whi = c.buf("p6_whi")
    wlo = c.buf("p6_wlo")
    wr32 = c.buf("p6_wr")
    rout = sm[:, 256:512]
    gumb = sm[:, 512:768]
    bbc = sm[:, 768:776]
    msq = sm[:, 800:804]
    rr = sm[:, 804:808]
    smx = sm[:, 808:809]
    ssum = sm[:, 809:810]
    sinv = sm[:, 810:811]
    sm8 = sm[:, 812:820]

    for kc in range(KC):
        c.inc("tp", 16)

        def TLX(kc=kc):
            nc.sync.dma_start_transpose(
                afm[:, tsl(kc, T_LOC)], d["xn1b"][:, tsl(kc, 128)]).then_inc(sem["tp"], 16)
        c.emit("sync", TLX)
    tp_x = c.cnt["tp"]

    ld_ot = c.inc("ld", 16 * 5)

    def LOT():
        nc.sync.dma_start(out=gqb[:], in_=d["gq"][:].broadcast_to([128, D])).then_inc(sem["ld"], 16)
        nc.sync.dma_start(out=gkb[:], in_=d["gk"][:].broadcast_to([128, 512])).then_inc(sem["ld"], 16)
        nc.sync.dma_start(out=bbc, in_=d["router_b"][:].broadcast_to([128, 8])).then_inc(sem["ld"], 16)
        nc.sync.dma_start(out=gumb.rearrange("p (a e) -> p a e", a=32),
                          in_=d["gumb"][:].rearrange("(a p) e -> p a e", p=128)).then_inc(sem["ld"], 16)
        nc.sync.dma_start(out=wr32.rearrange("p (cc e) -> p cc e", cc=16),
                          in_=d["router_w"][:].rearrange("(cc p) e -> p cc e", p=128)).then_inc(sem["ld"], 16)
    c.emit("sync", LOT)

    v_w = c.inc("dv", 2)

    def VW():
        nc.vector.wait_ge(sem["ld"], ld_ot)
        nc.vector.tensor_copy(whi[:], wr32[:]).then_inc(sem["dv"], 1)
        nc.vector.tensor_copy(spacer2, spacer)
        nc.vector.tensor_tensor(wlo[:], wr32[:], whi[:], ALU.subtract).then_inc(sem["dv"], 1)
    c.emit("vector", VW)

    bank_free = {}
    stage_free = [None, None]
    tab_free = [None, None]
    prev_mm_fg = 0
    prev_post = 0
    for fg in range(6):
        kind = "q" if fg < 4 else ("k" if fg == 4 else "v")
        wname = "wq" if fg < 4 else ("wk" if fg == 4 else "wv")
        wcol = tsl(fg, 512) if fg < 4 else slice(0, 512)
        g_sl = c.inc("wd", 16)

        def GSL(wname=wname, wcol=wcol, pmm=prev_mm_fg):
            if pmm:
                nc.gpsimd.wait_ge(sem["mm"], pmm)
            nc.gpsimd.dma_start(
                out=wsl[:].rearrange("p (cc f) -> p cc f", cc=16),
                in_=d[wname][:, wcol].rearrange("(cc p) f -> p cc f", p=128),
            ).then_inc(sem["wd"], 16)
        c.emit("gpsimd", GSL)

        for tt in range(NT):
            pt = tt % NTS
            pbank = ps["a"] if tt % 2 == 0 else ps["b"]
            bkey = "a" if tt % 2 == 0 else "b"
            mq = c.inc("mm")

            def MM(pbank=pbank, tt=tt, g_sl=g_sl, gk_=bank_free.get(bkey)):
                nc.tensor.wait_ge(sem["wd"], g_sl)
                nc.tensor.wait_ge(sem["tp"], tp_x)
                wait_all(nc.tensor, [gk_])
                mmi = None
                for kc in range(KC):
                    mmi = nc.tensor.matmul(pbank[:, 0:512],
                                           afm[:, kc * T_LOC + tt * 128: kc * T_LOC + (tt + 1) * 128],
                                           wsl[:, tsl(kc, 512)], start=(kc == 0), stop=(kc == KC - 1))
                mmi.then_inc(sem["mm"], 1)
            c.emit("tensor", MM)

            sslot = ost[:, tsl(tt % 2, 1024)][:, 0:512]
            sguard = stage_free[tt % 2]
            if kind == "v":
                a_e = c.inc("ac")

                def AE(pbank=pbank, sslot=sslot, mq=mq, sguard=sguard):
                    nc.scalar.wait_ge(sem["mm"], mq)
                    wait_all(nc.scalar, [sguard])
                    nc.scalar.copy(sslot, pbank[:, 0:512]).then_inc(sem["ac"], 1)
                c.emit("scalar", AE)
                bank_free[bkey] = ("ac", a_e)
                c.inc("st", 16)
                stq = c.cnt["st"]

                def SE(sslot=sslot, tt=tt, a_e=a_e):
                    nc.sync.wait_ge(sem["ac"], a_e)
                    nc.sync.dma_start(out=d["v1"][tsl(tt, 128), :], in_=sslot).then_inc(sem["st"], 16)
                c.emit("sync", SE)
                stage_free[tt % 2] = ("st", stq)
            else:
                tguard = ("dv", prev_post) if prev_post else None
                c.inc("wd", 32)
                wd_t = c.cnt["wd"]
                cs_src = "cos1q" if kind == "q" else "cos1k"
                sn_src = "sin1q" if kind == "q" else "sin1k"
                tcol = tsl(fg, 256) if kind == "q" else slice(0, 256)

                def LT(pt=pt, tguard=tguard, cs_src=cs_src, sn_src=sn_src, tcol=tcol):
                    wait_all(nc.gpsimd, [tguard])
                    nc.gpsimd.dma_start(out=ctab[:], in_=d[cs_src][tsl(pt, 128), tcol]).then_inc(sem["wd"], 16)
                    nc.gpsimd.dma_start(out=stab[:], in_=d[sn_src][tsl(pt, 128), tcol]).then_inc(sem["wd"], 16)
                c.emit("gpsimd", LT)

                a_e = c.inc("ac")

                def AE2(pbank=pbank, mq=mq, ppost=prev_post):
                    nc.scalar.wait_ge(sem["mm"], mq)
                    if ppost:
                        nc.scalar.wait_ge(sem["dv"], ppost)
                    nc.scalar.copy(q32[:], pbank[:, 0:512]).then_inc(sem["ac"], 1)
                c.emit("scalar", AE2)
                bank_free[bkey] = ("ac", a_e)

                a_sq = c.inc("ac")

                def ASQ(a_e=a_e):
                    nc.scalar.activation(qn32[:], q32[:], AF.Square).then_inc(sem["ac"], 1)
                c.emit("scalar", ASQ)

                v_rd = c.inc("dv")

                def VRD(a_sq=a_sq):
                    nc.vector.wait_ge(sem["ac"], a_sq)
                    nc.vector.reduce_sum(msq, qn32[:].rearrange("p (h f) -> p h f", h=4),
                                         axis=AX.X).then_inc(sem["dv"], 1)
                c.emit("vector", VRD)

                a_rr = c.inc("ac", 2)

                def ARR(v_rd=v_rd):
                    nc.scalar.wait_ge(sem["dv"], v_rd)
                    nc.scalar.activation(rr, msq, AF.Ln, bias=c.const(eps1), scale=1.0 / 128.0).then_inc(sem["ac"], 1)
                    nc.scalar.activation(spacer, spacer, AF.Identity)
                    nc.scalar.activation(rr, rr, AF.Exp, scale=-0.5).then_inc(sem["ac"], 1)
                c.emit("scalar", ARR)

                gslice = gqb[:, tsl(fg, 512)] if kind == "q" else gkb[:]
                v_pp = c.inc("dv", 2)

                def VPP(a_rr=a_rr, gslice=gslice, wd_t=wd_t, sguard=sguard, sslot=sslot):
                    nc.vector.wait_ge(sem["ac"], a_rr)
                    nc.vector.wait_ge(sem["wd"], wd_t)
                    wait_all(nc.vector, [sguard])
                    for h in range(4):
                        nc.vector.tensor_scalar(qn32[:, tsl(h, 128)], q32[:, tsl(h, 128)],
                                                rr[:, h:h + 1], RSC1, ALU.mult, op1=ALU.mult)
                    nc.vector.tensor_copy(spacer2, spacer)
                    nc.vector.tensor_tensor(q32[:], qn32[:], gslice, ALU.mult)
                    nc.vector.tensor_copy(spacer2, spacer)
                    qv = q32.rearrange("p (h f) -> p h f", h=4)
                    E = qv[:, :, 0:64]
                    O = qv[:, :, 64:128]
                    ct3 = ctab[:].rearrange("p (h f) -> p h f", h=4)
                    st3 = stab[:].rearrange("p (h f) -> p h f", h=4)
                    tm = [sm[:, 1024 + 256 * kk: 1280 + 256 * kk].rearrange("p (h f) -> p h f", h=4) for kk in range(3)]
                    ov = sslot.rearrange("p (h f) -> p h f", h=4)
                    nc.vector.tensor_tensor(tm[0], E, ct3, ALU.mult)
                    nc.vector.tensor_tensor(tm[1], O, st3, ALU.mult)
                    nc.vector.tensor_tensor(tm[2], O, ct3, ALU.mult)
                    nc.vector.tensor_tensor(ov[:, :, 0:64], tm[0], tm[1], ALU.subtract).then_inc(sem["dv"], 1)
                    nc.vector.tensor_tensor(tm[0], E, st3, ALU.mult)
                    nc.vector.tensor_tensor(ov[:, :, 64:128], tm[2], tm[0], ALU.add).then_inc(sem["dv"], 1)
                c.emit("vector", VPP)
                prev_post = v_pp
                tab_free[tt % 2] = ("dv", v_pp)

                dst = d["q1"] if kind == "q" else d["k1"]
                dcol = tsl(fg, 512) if kind == "q" else slice(0, 512)
                c.inc("st", 16)
                stq = c.cnt["st"]

                def SE2(sslot=sslot, dst=dst, dcol=dcol, tt=tt, v_pp=v_pp):
                    nc.sync.wait_ge(sem["dv"], v_pp)
                    nc.sync.dma_start(out=dst[tsl(tt, 128), dcol], in_=sslot).then_inc(sem["st"], 16)
                c.emit("sync", SE2)
                stage_free[tt % 2] = ("st", stq)
        prev_mm_fg = c.cnt["mm"]

    rps = ps["d"]
    prev_rv = 0
    for tt in range(NT):
        mr = c.inc("mm")

        def MRA(tt=tt, vw=v_w, prv=prev_rv):
            nc.tensor.wait_ge(sem["dv"], vw)
            if prv:
                nc.tensor.wait_ge(sem["dv"], prv)
            mmi = None
            for kc in range(KC):
                mmi = nc.tensor.matmul(rps[:, 0:8],
                                       afm[:, kc * T_LOC + tt * 128: kc * T_LOC + (tt + 1) * 128],
                                       whi[:, tsl(kc, 8)], start=(kc == 0), stop=False)
            for kc in range(KC):
                mmi = nc.tensor.matmul(rps[:, 0:8],
                                       afm[:, kc * T_LOC + tt * 128: kc * T_LOC + (tt + 1) * 128],
                                       wlo[:, tsl(kc, 8)], start=False, stop=(kc == KC - 1))
            mmi.then_inc(sem["mm"], 1)
        c.emit("tensor", MRA)

        rv = c.inc("dv")

        def VRA(mr=mr, tt=tt):
            nc.vector.wait_ge(sem["mm"], mr)
            nc.vector.tensor_copy(rout[:, tsl(tt, 8)], rps[:, 0:8]).then_inc(sem["dv"], 1)
        c.emit("vector", VRA)
        prev_rv = rv

    mm_afterA = c.cnt["mm"]
    for kc in range(KC):
        c.inc("tp", 16)

        def TLL(kc=kc, mm_afterA=mm_afterA, prev_rv=prev_rv):
            if kc == 0:
                nc.sync.wait_ge(sem["mm"], mm_afterA)
            nc.sync.dma_start_transpose(
                afm[:, tsl(kc, T_LOC)], d["xn1lo"][:, tsl(kc, 128)]).then_inc(sem["tp"], 16)
        c.emit("sync", TLL)
    tp_lo = c.cnt["tp"]

    prev_sm_st = 0
    prev_sm_ac = 0
    prev_ep = prev_rv
    for tt in range(NT):
        mr = c.inc("mm")

        def MRB(tt=tt, prv=prev_ep):
            nc.tensor.wait_ge(sem["tp"], tp_lo)
            if prv:
                nc.tensor.wait_ge(sem["dv"], prv)
            mmi = None
            for kc in range(KC):
                mmi = nc.tensor.matmul(rps[:, 0:8],
                                       afm[:, kc * T_LOC + tt * 128: kc * T_LOC + (tt + 1) * 128],
                                       whi[:, tsl(kc, 8)], start=(kc == 0), stop=(kc == KC - 1))
            mmi.then_inc(sem["mm"], 1)
        c.emit("tensor", MRB)

        v_ep = c.inc("dv", 2)
        a_ex = c.inc("ac")
        v_fin = c.inc("dv", 2)
        c.inc("st", 16)
        st_sm = c.cnt["st"]

        def VEP(mr=mr, tt=tt, pac=prev_sm_ac):
            nc.vector.wait_ge(sem["mm"], mr)
            if pac:
                nc.vector.wait_ge(sem["ac"], pac)
            nc.vector.tensor_tensor(sm8, rps[:, 0:8], rout[:, tsl(tt, 8)], ALU.add)
            nc.vector.tensor_copy(spacer2, spacer)
            nc.vector.tensor_tensor(sm8, sm8, gumb[:, tsl(tt, 8)], ALU.add)
            nc.vector.tensor_copy(spacer2, spacer)
            nc.vector.tensor_tensor(sm8, sm8, bbc, ALU.add)
            nc.vector.tensor_copy(spacer2, spacer)
            nc.vector.reduce_max(smx, sm8, axis=AX.X, negate=True).then_inc(sem["dv"], 2)
        c.emit("vector", VEP)

        def AEX(v_ep=v_ep):
            nc.scalar.wait_ge(sem["dv"], v_ep)
            nc.scalar.activation(qn32[:, 0:8], sm8, AF.Exp, bias=smx,
                                 accum_out=ssum).then_inc(sem["ac"], 1)
        c.emit("scalar", AEX)
        prev_sm_ac = a_ex

        def VFIN(a_ex=a_ex, tt=tt, pst=prev_sm_st):
            nc.vector.wait_ge(sem["ac"], a_ex)
            if pst:
                nc.vector.wait_ge(sem["st"], pst)
            nc.vector.reciprocal(sinv, ssum).then_inc(sem["dv"], 1)
            nc.vector.tensor_copy(spacer2, spacer)
            nc.vector.tensor_scalar(q32[:, 8:16], qn32[:, 0:8], sinv, None, ALU.mult).then_inc(sem["dv"], 1)
        c.emit("vector", VFIN)

        def SSM(tt=tt, v_fin=v_fin):
            nc.sync.wait_ge(sem["dv"], v_fin)
            nc.sync.dma_start(out=d["Sm"][tsl(tt, 128), :], in_=q32[:, 8:16]).then_inc(sem["st"], 16)
        c.emit("sync", SSM)
        prev_sm_st = st_sm
        prev_ep = v_ep

    st_total = c.cnt["st"]

    def FS():
        nc.sync.wait_ge(sem["st"], st_total)
    c.emit("sync", FS)
    c.barrier()


def emit_p7_attn1(c):
    nc, d, sem = c.nc, c.dram, c.sem
    sm = c.sb["sm32"]
    ps = c.ps
    wait_all = _mk_waiter(sem)
    spacer = c.const_tile[:, 63:64]
    spacer2 = c.const_tile[:, 62:63]
    c.lay("p7_ctx", 0, 32768, BF16)
    c.lay("p7_wo", 32768, 32768, BF16)
    c.lay("p7_qf", 65536, 4096, BF16)
    c.lay("p7_kf", 69632, 4096, BF16)
    c.lay("p7_vt", 73728, 4096, BF16)
    c.lay("p7_pb", 77824, 4096, BF16)
    c.lay("p7_pn", 81920, 4096, BF16)
    c.lay("p7_pT", 86016, 8192, BF16)
    c.lay("p7_x1t", 94208, 2048, F32)
    c.lay("p7_msk", 90112, 256, F32)
    ctxb = c.buf("p7_ctx")
    wob = c.buf("p7_wo")
    qfb = c.buf("p7_qf")
    kfb = c.buf("p7_kf")
    vtb = c.buf("p7_vt")
    pbb = c.buf("p7_pb")
    pnb = c.buf("p7_pn")
    pTb = c.buf("p7_pT")
    x1t = c.buf("p7_x1t")
    mskb = c.buf("p7_msk")

    bank_free = {}
    pb_free = [None, None]
    pn_free = [None, None]
    pT_free = None
    s4_free = None
    prev_sumread = 0
    wd_wo = c.inc("wd", 16)

    def GW():
        nc.gpsimd.dma_start(
            out=wob[:].rearrange("p (cc f) -> p cc f", cc=16),
            in_=d["wo"][:].rearrange("(cc p) f -> p cc f", p=128),
        ).then_inc(sem["wd"], 16)
    c.emit("gpsimd", GW)

    ld_msk = c.inc("ld", 16)

    def LMSK():
        nc.sync.dma_start(out=mskb[:, 0:128], in_=d["maskdiag"][:]).then_inc(sem["ld"], 16)
    c.emit("sync", LMSK)

    prev_head_mm = 0
    prev_grp_mm = 0
    for seq in range(B_LOC):
        for h in range(H0):
            g_id = h // 4
            sl = h % 2
            qf = qfb[:, tsl(sl, 2048)]
            tq = c.inc("tp", 16)

            def LDQ(seq=seq, h=h, qf=qf, guard_mm=prev_head_mm):
                if guard_mm:
                    nc.sync.wait_ge(sem["mm"], guard_mm)
                nc.sync.dma_start_transpose(
                    qf, d["q1"][tsl(seq, S), tsl(h, 128)]).then_inc(sem["tp"], 16)
            c.emit("sync", LDQ)

            if h % 4 == 0:
                gsl = g_id % 2
                kf = kfb[:, tsl(gsl, 2048)]
                vt = vtb[:, tsl(gsl, 2048)]
                tk = c.inc("tp", 16)
                lv = c.inc("ld", 16)

                def LDKV(seq=seq, g_id=g_id, kf=kf, vt=vt, guard_mm=prev_grp_mm):
                    if guard_mm:
                        nc.sync.wait_ge(sem["mm"], guard_mm)
                    nc.sync.dma_start_transpose(
                        kf, d["k1"][tsl(seq, S), tsl(g_id, 128)]).then_inc(sem["tp"], 16)
                    nc.sync.dma_start(
                        out=vt[:].rearrange("p (cc f) -> p cc f", cc=16),
                        in_=d["v1"][tsl(seq, S), tsl(g_id, 128)].rearrange("(cc p) f -> p cc f", p=128),
                    ).then_inc(sem["ld"], 16)
                c.emit("sync", LDKV)
                kv_sem = (tk, lv)
                prev_grp_mm_pending = c.cnt["mm"]
            tk, lv = kv_sem
            head_mm_before = c.cnt["mm"]

            for g in range(4):
                ncc = g + 1
                for q4 in range(4):
                    qt = g * 4 + q4
                    dof = q4 * 128
                    ms = c.inc("mm")

                    def MS(qf=qf, kf=kf, qt=qt, ncc=ncc, tq=tq, tk=tk, s4g=s4_free):
                        nc.tensor.wait_ge(sem["tp"], tq)
                        nc.tensor.wait_ge(sem["tp"], tk)
                        wait_all(nc.tensor, [s4g])
                        mmi = None
                        for cc in range(ncc):
                            mmi = nc.tensor.matmul(ps["s4"][:, tsl(cc, 512)],
                                                   qf[:, tsl(qt, 128)], kf[:, tsl(cc, 512)],
                                                   start=True, stop=True)
                        mmi.then_inc(sem["mm"], 1)
                    c.emit("tensor", MS)

                    vmk = c.inc("dv", 1)

                    def VMK(ms=ms, g=g, dof=dof):
                        nc.vector.wait_ge(sem["mm"], ms)
                        base = g * 512 + dof
                        nc.vector.tensor_tensor(ps["s4"][:, base:base + 128],
                                                ps["s4"][:, base:base + 128],
                                                mskb[:, 0:128], ALU.add)
                        if dof < 384:
                            nc.vector.memset(ps["s4"][:, base + 128:(g + 1) * 512], -1e30)
                        nc.vector.tensor_copy(spacer2, spacer).then_inc(sem["dv"], 1)
                    c.emit("vector", VMK)

                    width = ncc * 512
                    nmx = sm[:, 0:1]
                    sums = sm[:, 1:2]
                    inv = sm[:, 2:3]
                    vmx = c.inc("dv")

                    def VMX(vmk=vmk, nmx=nmx, width=width):
                        nc.vector.reduce_max(nmx, ps["s4"][:, 0:width], axis=AX.X,
                                             negate=True).then_inc(sem["dv"], 1)
                    c.emit("vector", VMX)

                    pb = pbb[:, tsl(qt % 2, 2048)]
                    aex = c.inc("ac")

                    def AEXP(vmx=vmx, pb=pb, nmx=nmx, sums=sums, width=width,
                             gpb=pb_free[qt % 2], psv=prev_sumread):
                        nc.scalar.wait_ge(sem["dv"], vmx)
                        wait_all(nc.scalar, [gpb])
                        if psv:
                            nc.scalar.wait_ge(sem["dv"], psv)
                        nc.scalar.activation(pb[:, 0:width], ps["s4"][:, 0:width], AF.Exp,
                                             bias=nmx, accum_out=sums).then_inc(sem["ac"], 1)
                    c.emit("scalar", AEXP)
                    s4_free = ("ac", aex)

                    pn = pnb[:, tsl(qt % 2, 2048)]
                    vsc = c.inc("dv", 2)

                    def VSCL(aex=aex, pn=pn, pb=pb, inv=inv, sums=sums, width=width,
                             gpn=pn_free[qt % 2]):
                        nc.vector.wait_ge(sem["ac"], aex)
                        wait_all(nc.vector, [gpn])
                        nc.vector.reciprocal(inv, sums).then_inc(sem["dv"], 1)
                        nc.vector.tensor_copy(spacer2, spacer)
                        nc.vector.tensor_scalar(pn[:, 0:width], pb[:, 0:width], inv,
                                                None, ALU.mult).then_inc(sem["dv"], 1)
                    c.emit("vector", VSCL)
                    pb_free[qt % 2] = ("dv", vsc)
                    prev_sumread = vsc

                    tpq = c.inc("tp", 16)

                    def TPQ(vsc=vsc, pn=pn, q4=q4, ncc=ncc, gpT=pT_free):
                        nc.sync.wait_ge(sem["dv"], vsc)
                        wait_all(nc.sync, [gpT])
                        nc.sync.dma_start_transpose(
                            pTb[:].rearrange("p (cc g2) -> p cc g2", cc=16)[:, 0:4 * ncc, tsl(q4, 128)],
                            pn.rearrange("p (cc f) -> p cc f", cc=16)[:, 0:4 * ncc, :],
                        ).then_inc(sem["tp"], 16)
                    c.emit("sync", TPQ)
                    pn_free[qt % 2] = ("tp", tpq)
                    tp_group_done = c.cnt["tp"]

                cps = ps["a"] if g % 2 == 0 else ps["b"]
                bkey = "a" if g % 2 == 0 else "b"
                mc = c.inc("mm")

                def MC(cps=cps, vt=vt, lv=lv, ncc=ncc, tp_need=tp_group_done,
                       gbank=bank_free.get(bkey)):
                    nc.tensor.wait_ge(sem["tp"], tp_need)
                    nc.tensor.wait_ge(sem["ld"], lv)
                    wait_all(nc.tensor, [gbank])
                    mmi = None
                    for cc in range(4 * ncc):
                        mmi = nc.tensor.matmul(cps[:, 0:512], vt[:, tsl(cc, 128)],
                                               pTb[:, tsl(cc, 512)],
                                               start=(cc == 0), stop=(cc == 4 * ncc - 1))
                    mmi.then_inc(sem["mm"], 1)
                c.emit("tensor", MC)
                pT_free = ("mm", mc)

                ce = c.inc("ac")

                def CEV(mc=mc, cps=cps, h=h, g=g):
                    nc.scalar.wait_ge(sem["mm"], mc)
                    nc.scalar.copy(ctxb[:, h * 2048 + g * 512: h * 2048 + (g + 1) * 512],
                                   cps[:, 0:512]).then_inc(sem["ac"], 1)
                c.emit("scalar", CEV)
                bank_free[bkey] = ("ac", ce)
            prev_head_mm = head_mm_before
            if h % 4 == 3:
                prev_grp_mm = prev_grp_mm_pending

        cev_all = c.cnt["ac"]
        for qt in range(NTS):
            i_glob = seq * NTS + qt
            lxb = c.inc("ld", 16)

            def LXB(i_glob=i_glob):
                nc.sync.dma_start(out=x1t[:, 0:2048],
                                  in_=d["x1"][tsl(i_glob, 128), :]).then_inc(sem["ld"], 16)
            c.emit("sync", LXB)

            pbanks = {0: ("a", ps["a"]), 1: ("b", ps["b"]), 2: ("cc", ps["c"]), 3: ("dd", ps["d"])}
            for og in range(4):
                bkey, bank = pbanks[og]
                mp = c.inc("mm")

                def MP(qt=qt, og=og, bank=bank, cev_all=cev_all, wd_wo=wd_wo,
                       gbank=bank_free.get(bkey)):
                    nc.tensor.wait_ge(sem["ac"], cev_all)
                    nc.tensor.wait_ge(sem["wd"], wd_wo)
                    wait_all(nc.tensor, [gbank])
                    mmi = None
                    for hh in range(16):
                        mmi = nc.tensor.matmul(
                            bank[:, 0:512],
                            ctxb[:, hh * 2048 + qt * 128: hh * 2048 + (qt + 1) * 128],
                            wob[:, hh * 2048 + og * 512: hh * 2048 + (og + 1) * 512],
                            start=(hh == 0), stop=(hh == 15))
                    mmi.then_inc(sem["mm"], 1)
                c.emit("tensor", MP)

                vp = c.inc("dv")

                def VPR(mp=mp, og=og, bank=bank, lxb=lxb):
                    nc.vector.wait_ge(sem["mm"], mp)
                    nc.vector.wait_ge(sem["ld"], lxb)
                    nc.vector.tensor_tensor(x1t[:, tsl(og, 512)], bank[:, 0:512],
                                            x1t[:, tsl(og, 512)], ALU.add).then_inc(sem["dv"], 1)
                c.emit("vector", VPR)
                bank_free[bkey] = ("dv", vp)

            vlast = c.cnt["dv"]
            c.inc("st", 16)

            def SX(i_glob=i_glob, vlast=vlast):
                nc.sync.wait_ge(sem["dv"], vlast)
                nc.sync.dma_start(out=d["partial"][tsl(i_glob, 128), :],
                                  in_=x1t[:, 0:2048]).then_inc(sem["st"], 16)
            c.emit("sync", SX)
        prev_head_mm = max(prev_head_mm, c.cnt["mm"])
        prev_grp_mm = max(prev_grp_mm, c.cnt["mm"])

    st_total = c.cnt["st"]

    def FS():
        nc.sync.wait_ge(sem["st"], st_total)
    c.emit("sync", FS)
    c.barrier()


def emit_p8_moe(c):
    nc, d, sem = c.nc, c.dram, c.sem
    sm = c.sb["sm32"]
    ps = c.ps
    wait_all = _mk_waiter(sem)
    c.lay("p8_afm", 0, KC * T_LOC, BF16)
    c.lay("p8_wg", 65536, 4096, BF16)
    c.lay("p8_wu", 69632, 4096, BF16)
    c.lay("p8_bg", 73728, 128, BF16)
    c.lay("p8_bu", 73856, 128, BF16)
    c.lay("p8_on", 73984, 512, BF16)
    c.lay("p8_hst", 74496, 4096, BF16)
    c.lay("p8_f32", 78592, 1024, F32)
    afm = c.buf("p8_afm")
    wgb = c.buf("p8_wg")
    wub = c.buf("p8_wu")
    bgb = c.buf("p8_bg")
    bub = c.buf("p8_bu")
    onb = c.buf("p8_on")
    hst = c.buf("p8_hst")
    f32s = c.buf("p8_f32")
    sst = sm[:, 1024:1280].bitcast(BF16)

    for kc in range(KC):
        c.inc("tp", 16)

        def TLX(kc=kc):
            nc.sync.dma_start_transpose(
                afm[:, tsl(kc, T_LOC)], d["xn1b"][:, tsl(kc, 128)]).then_inc(sem["tp"], 16)
        c.emit("sync", TLX)
    tp_x = c.cnt["tp"]

    ld_on = c.inc("ld", 16)

    def LON():
        nc.sync.dma_start(out=onb[0:1, :], in_=d["ones512"][:]).then_inc(sem["ld"], 16)
    c.emit("sync", LON)

    bank_free = {}
    prev_mm_fc = 0
    prev_st_hh = 0
    prev_dv_sil = 0
    for fc in range(NFC):
        w = 128 if fc < NFC - 1 else LASTF
        g_sl = c.inc("wd", 64)

        def GSL(fc=fc, w=w, pmm=prev_mm_fc):
            if pmm:
                nc.gpsimd.wait_ge(sem["mm"], pmm)
            nc.gpsimd.dma_start(
                out=wgb[:, 0:16 * w].rearrange("p (cc f) -> p cc f", cc=16),
                in_=d["w_gate"][:, fc * 128: fc * 128 + w].rearrange("(cc p) f -> p cc f", p=128),
            ).then_inc(sem["wd"], 16)
            nc.gpsimd.dma_start(
                out=wub[:, 0:16 * w].rearrange("p (cc f) -> p cc f", cc=16),
                in_=d["w_up"][:, fc * 128: fc * 128 + w].rearrange("(cc p) f -> p cc f", p=128),
            ).then_inc(sem["wd"], 16)
            nc.gpsimd.dma_start(out=bgb[0:1, 0:w],
                                in_=d["b_gate"][0:1, fc * 128: fc * 128 + w]).then_inc(sem["wd"], 16)
            nc.gpsimd.dma_start(out=bub[0:1, 0:w],
                                in_=d["b_up"][0:1, fc * 128: fc * 128 + w]).then_inc(sem["wd"], 16)
        c.emit("gpsimd", GSL)

        for tg in range(8):
            pg = ps["a"] if tg % 2 == 0 else ps["c"]
            pu = ps["b"] if tg % 2 == 0 else ps["d"]
            kg = "a" if tg % 2 == 0 else "cc"
            ku = "b" if tg % 2 == 0 else "dd"
            mg = c.inc("mm", 2)

            def MG(tg=tg, w=w, pg=pg, pu=pu, g_sl=g_sl,
                   ga=bank_free.get(kg), gb_=bank_free.get(ku)):
                nc.tensor.wait_ge(sem["wd"], g_sl)
                nc.tensor.wait_ge(sem["tp"], tp_x)
                nc.tensor.wait_ge(sem["ld"], ld_on)
                wait_all(nc.tensor, [ga, gb_])
                for hc in range(KC):
                    nc.tensor.matmul(pg[0:w, 0:512], wgb[:, hc * w: (hc + 1) * w],
                                     afm[:, hc * T_LOC + tg * 512: hc * T_LOC + (tg + 1) * 512],
                                     start=(hc == 0), stop=False)
                nc.tensor.matmul(pg[0:w, 0:512], bgb[0:1, 0:w], onb[0:1, 0:512],
                                 start=False, stop=True).then_inc(sem["mm"], 1)
                for hc in range(KC):
                    nc.tensor.matmul(pu[0:w, 0:512], wub[:, hc * w: (hc + 1) * w],
                                     afm[:, hc * T_LOC + tg * 512: hc * T_LOC + (tg + 1) * 512],
                                     start=(hc == 0), stop=False)
                nc.tensor.matmul(pu[0:w, 0:512], bub[0:1, 0:w], onb[0:1, 0:512],
                                 start=False, stop=True).then_inc(sem["mm"], 1)
            c.emit("tensor", MG)

            asi = c.inc("ac")

            def ASI(mg=mg, pg=pg, w=w, pdv=prev_dv_sil):
                nc.scalar.wait_ge(sem["mm"], mg - 1)
                if pdv:
                    nc.scalar.wait_ge(sem["dv"], pdv)
                nc.scalar.activation(sst[0:w, 0:512], pg[0:w, 0:512], AF.Silu).then_inc(sem["ac"], 1)
            c.emit("scalar", ASI)
            bank_free[kg] = ("ac", asi)

            vmu = c.inc("dv")

            def VMU(mg=mg, asi=asi, pu=pu, tg=tg, w=w, pst=prev_st_hh):
                nc.vector.wait_ge(sem["mm"], mg)
                nc.vector.wait_ge(sem["ac"], asi)
                if pst:
                    nc.vector.wait_ge(sem["st"], pst)
                nc.vector.tensor_tensor(hst[0:w, tsl(tg, 512)], sst[0:w, 0:512],
                                        pu[0:w, 0:512], ALU.mult).then_inc(sem["dv"], 1)
            c.emit("vector", VMU)
            bank_free[ku] = ("dv", vmu)
            prev_dv_sil = vmu

        sth = c.inc("st", 16)
        vlast = c.cnt["dv"]

        def SHH(fc=fc, w=w, vlast=vlast):
            nc.sync.wait_ge(sem["dv"], vlast)
            nc.sync.dma_start(out=d["hh1"][fc * 128: fc * 128 + w, :],
                              in_=hst[0:w, :]).then_inc(sem["st"], 16)
        c.emit("sync", SHH)
        prev_st_hh = sth
        prev_mm_fc = c.cnt["mm"]

    st8a_total = c.cnt["st"]

    def FS8A():
        nc.sync.wait_ge(sem["st"], st8a_total)
    c.emit("sync", FS8A)
    c.barrier()

    c.lay("p8_hht", 0, 45056, BF16)
    c.lay("p8_wds", 45056, 22528, BF16)
    c.lay("p8_bd", 67584, 512, BF16)
    hht = c.buf("p8_hht")
    wds = c.buf("p8_wds")
    bdb = c.buf("p8_bd")
    bank_free = {}
    stage_free = [None, None]
    prev_mm_tb = 0
    for tb in range(4):
        ld_h = c.inc("ld", 32)

        def LHH(tb=tb, pmm=prev_mm_tb):
            if pmm:
                nc.sync.wait_ge(sem["mm"], pmm)
            nc.sync.dma_start(
                out=hht[:, 0:43008].rearrange("p (fc t) -> p fc t", fc=42),
                in_=d["hh1"][0:5376, tsl(tb, 1024)].rearrange("(fc p) t -> p fc t", p=128),
            ).then_inc(sem["ld"], 16)
            nc.sync.dma_start(
                out=hht[0:LASTF, 43008:44032],
                in_=d["hh1"][5376:5461, tsl(tb, 1024)],
            ).then_inc(sem["ld"], 16)
        c.emit("sync", LHH)

        for hg in range(4):
            mm_before_hg = c.cnt["mm"]
            ld_w = c.inc("wd", 48)

            def LWD(hg=hg, pmm=mm_before_hg, first=(tb == 0 and hg == 0)):
                if not first:
                    nc.gpsimd.wait_ge(sem["mm"], pmm)
                nc.gpsimd.dma_start(
                    out=wds[:, 0:21504].rearrange("p (fc h) -> p fc h", fc=42),
                    in_=d["w_down"][0:5376, tsl(hg, 512)].rearrange("(fc p) h -> p fc h", p=128),
                ).then_inc(sem["wd"], 16)
                nc.gpsimd.dma_start(
                    out=wds[0:LASTF, 21504:22016],
                    in_=d["w_down"][5376:5461, tsl(hg, 512)],
                ).then_inc(sem["wd"], 16)
                nc.gpsimd.dma_start(out=bdb[0:1, :],
                                    in_=d["b_down"][0:1, tsl(hg, 512)]).then_inc(sem["wd"], 16)
            c.emit("gpsimd", LWD)

            for tc in range(8):
                i_glob = tb * 8 + tc
                pd = ps["a"] if tc % 2 == 0 else ps["b"]
                kd = "a" if tc % 2 == 0 else "b"
                md = c.inc("mm")

                def MD(tc=tc, pd=pd, ld_w=ld_w, ld_h=ld_h, gk_=bank_free.get(kd)):
                    nc.tensor.wait_ge(sem["wd"], ld_w)
                    nc.tensor.wait_ge(sem["ld"], ld_h)
                    wait_all(nc.tensor, [gk_])
                    for fc in range(42):
                        nc.tensor.matmul(
                            pd[:, 0:512],
                            hht[:, fc * 1024 + tc * 128: fc * 1024 + (tc + 1) * 128],
                            wds[:, tsl(fc, 512)],
                            start=(fc == 0), stop=False)
                    nc.tensor.matmul(
                        pd[:, 0:512],
                        hht[0:LASTF, 43008 + tc * 128: 43008 + (tc + 1) * 128],
                        wds[0:LASTF, 21504:22016],
                        start=False, stop=False)
                    nc.tensor.matmul(pd[:, 0:512], onb[0:1, tc * 16: tc * 16 + 128],
                                     bdb[0:1, 0:512], start=False, stop=True).then_inc(sem["mm"], 1)
                c.emit("tensor", MD)

                aev = c.inc("ac")
                stg_ = f32s[:, tsl(tc % 2, 512)]

                def AEVD(md=md, pd=pd, stg_=stg_, gs=stage_free[tc % 2]):
                    nc.scalar.wait_ge(sem["mm"], md)
                    wait_all(nc.scalar, [gs])
                    nc.scalar.copy(stg_, pd[:, 0:512]).then_inc(sem["ac"], 1)
                c.emit("scalar", AEVD)
                bank_free[kd] = ("ac", aev)

                std = c.inc("st", 16)

                def SD(i_glob=i_glob, hg=hg, aev=aev, stg_=stg_):
                    nc.sync.wait_ge(sem["ac"], aev)
                    nc.sync.dma_start(out=d["f1"][tsl(i_glob, 128), tsl(hg, 512)],
                                      in_=stg_).then_inc(sem["st"], 16)
                c.emit("sync", SD)
                stage_free[tc % 2] = ("st", std)
        prev_mm_tb = c.cnt["mm"]

    st_total = c.cnt["st"]

    def FS8():
        nc.sync.wait_ge(sem["st"], st_total)
    c.emit("sync", FS8)
    c.barrier()


def build_launch_A(dev_out=(), phases_sel=("p1", "p2")):
    nc = bass.Bass("TRN2", target_bir_lowering=False, debug=False, detect_race_conditions=False)
    stack = contextlib.ExitStack()
    c = Rec(nc, stack)

    def din(name, shape, dt):
        c.dram[name] = nc.dram_tensor(name, list(shape), dt, kind="ExternalInput").ap()

    def dout(name, shape, dt):
        c.dram[name] = nc.dram_tensor(name, list(shape), dt, kind="ExternalOutput").ap()

    def dint(name, shape, dt):
        kind = "ExternalOutput" if name in dev_out else "Internal"
        c.dram[name] = nc.dram_tensor(name, list(shape), dt, kind=kind).ap()

    din("fconst", (128, 64), F32)
    din("x0", (T_LOC, D), F32)
    dint("xb", (T_LOC, D), F32)
    dint("xn0b", (T_LOC, D), BF16)
    dint("zb", (T_LOC, D), BF16)
    if "p3" in phases_sel:
        din("W_out", (D, D), F32)
        dint("x1a", (T_LOC, D), F32)
    if "p4" in phases_sel:
        din("gate_w", (I0, D), I32)
        din("up_w", (I0, D), I32)
        din("down_w", (D, I0), I32)
        din("gate_scale_cols", (128, 64), F32)
        din("up_scale_cols", (128, 64), F32)
        din("down_scale_cols", (128, 16), F32)
        dint("wdt", (I0, D), BF16)
        dint("hh", (I0, T_LOC), BF16)
        dint("ffn0b", (T_LOC, D), BF16)
    if "p5" in phases_sel:
        dint("x1", (T_LOC, D), F32)
        dint("xn1b", (T_LOC, D), BF16)
        dint("xn1lo", (T_LOC, D), BF16)
    if "p6" in phases_sel:
        din("wq", (D, D), F32)
        din("wk", (D, 512), F32)
        din("wv", (D, 512), F32)
        din("gq", (1, D), F32)
        din("gk", (1, 512), F32)
        din("router_w", (D, 8), F32)
        din("router_b", (1, 8), F32)
        din("gumb", (T_LOC, 8), F32)
        din("cos1q", (S, 1024), F32)
        din("sin1q", (S, 1024), F32)
        din("cos1k", (S, 256), F32)
        din("sin1k", (S, 256), F32)
        dint("q1", (T_LOC, D), BF16)
        dint("k1", (T_LOC, 512), BF16)
        dint("v1", (T_LOC, 512), BF16)
        dout("Sm", (T_LOC, 8), F32)
    if "p7" in phases_sel:
        din("wo", (D, D), F32)
        din("maskdiag", (128, 128), F32)
        dout("partial", (T_LOC, D), F32)
    if "p8" in phases_sel:
        din("w_gate", (D, DFF1), F32)
        din("w_up", (D, DFF1), F32)
        din("w_down", (DFF1, D), F32)
        din("b_gate", (1, DFF1), F32)
        din("b_up", (1, DFF1), F32)
        din("b_down", (1, D), F32)
        din("ones512", (1, 512), BF16)
        dint("hh1", (DFF1, T_LOC), BF16)
        dout("f1", (T_LOC, D), F32)
    if "p2" in phases_sel:
        din("W_in", (D, 3 * D), F32)
        din("ln_w_cols", (128, 16), F32)
        din("ln_b_cols", (128, 16), F32)
        din("cos0", (S, 512), F32)
        din("sin0", (S, 512), F32)
        din("ones1", (1, 128), BF16)
        dint("q0", (T_LOC, D), BF16)
        dint("k0", (T_LOC, D), BF16)
        dint("v0", (T_LOC, D), BF16)

    with stack:
        c.sb["arena"] = stack.enter_context(nc.sbuf_tensor("arena", [128, 98304], BF16))
        c.sb["fconst"] = stack.enter_context(nc.sbuf_tensor("fconst_sb", [128, 64], F32))
        c.const_tile = c.sb["fconst"]
        c.sb["sm32"] = stack.enter_context(nc.sbuf_tensor("sm32", [128, 2048], F32))
        c.ps["s4"] = stack.enter_context(nc.psum_tensor("ps_s4", [128, 2048], F32))
        c.ps["a"] = stack.enter_context(nc.psum_tensor("ps_a", [128, 512], F32))
        c.ps["b"] = stack.enter_context(nc.psum_tensor("ps_b", [128, 512], F32))
        c.ps["c"] = stack.enter_context(nc.psum_tensor("ps_c", [128, 512], F32))
        c.ps["d"] = stack.enter_context(nc.psum_tensor("ps_d", [128, 512], F32))
        for s_ in ["ld", "st", "mm", "dv", "ac", "gp", "tp", "ph", "wd"]:
            c.new_sem(s_)
        cld = c.inc("ld", 16)

        def _ldc():
            nc.sync.dma_start(out=c.const_tile[:], in_=c.dram["fconst"][:]).then_inc(c.sem["ld"], 16)
        c.emit("sync", _ldc)
        for en in ("scalar", "vector", "gpsimd", "tensor"):
            def _w(en=en):
                getattr(nc, en).wait_ge(c.sem["ld"], cld)
            c.emit(en, _w)

        if "p1" in phases_sel:
            emit_p1_ln0(c)
        if "p2" in phases_sel:
            emit_p2_qkv0(c)
        if "p3" in phases_sel:
            emit_p3_attn0(c)
        if "p4" in phases_sel:
            emit_p4_ffn0(c)
        if "p5" in phases_sel:
            emit_p5_ln1(c)
        if "p6" in phases_sel:
            emit_p6_qkv1(c)
        if "p7" in phases_sel:
            emit_p7_attn1(c)
        if "p8" in phases_sel:
            emit_p8_moe(c)
        c.run_block()
    return nc, c


def build_launch_B():
    nc = bass.Bass("TRN2", target_bir_lowering=False, debug=False, detect_race_conditions=False)
    stack = contextlib.ExitStack()
    c = Rec(nc, stack)
    d = c.dram

    def din(name, shape, dt):
        d[name] = nc.dram_tensor(name, list(shape), dt, kind="ExternalInput").ap()

    din("SmL", (128, 2048 * 8 // 8), F32)
    din("Smloc", (T_LOC, 8), F32)
    din("partial", (T_LOC, D), F32)
    din("f1", (T_LOC, D), F32)
    din("Emat", (128, 8), F32)
    din("E2mat", (8, 128), F32)
    din("I8", (8, 8), F32)
    din("ones8_128", (8, 128), F32)
    din("fconst", (128, 64), F32)
    d["y"] = nc.dram_tensor("y", [T_LOC, D], F32, kind="ExternalOutput").ap()

    NIT = 48
    with stack:
        c.sb["a"] = stack.enter_context(nc.sbuf_tensor("ba", [128, 14336], F32))
        c.sb["fc"] = stack.enter_context(nc.sbuf_tensor("bfc", [128, 64], F32))
        c.const_tile = c.sb["fc"]
        c.ps["a"] = stack.enter_context(nc.psum_tensor("bpa", [128, 512], F32))
        c.ps["b"] = stack.enter_context(nc.psum_tensor("bpb", [128, 512], F32))
        for s_ in ["ld", "st", "mm", "dv", "ac", "gp", "ph"]:
            c.new_sem(s_)
        A = c.sb["a"]
        sem = c.sem
        ps = c.ps
        SmL = A[:, 0:2048]
        cmpb = A[:, 2048:4096]
        EmatB = A[:, 4096:4104]
        E2B = A[:, 4104:4232]
        I8B = A[:, 4232:4240]
        O81 = A[:, 4240:4368]
        lo8 = A[:, 4368:4369]
        hi8 = A[:, 4369:4370]
        mid8 = A[:, 4370:4371]
        cnt8 = A[:, 4371:4372]
        pred8 = A[:, 4372:4373]
        dd8 = A[:, 4373:4374]
        ee8 = A[:, 4374:4375]
        midb = A[:, 4375:4376]
        cntp = A[:, 4376:4377]
        D8 = A[:, 4377:4385]
        taub = A[:, 4385:4393]
        wcol = A[:, 4393:4394]
        mtile = A[:, 4400:4408]
        smt = [A[:, 4416:4424], A[:, 4424:4432]]
        prt = [A[:, 4608:6656], A[:, 6656:8704]]
        f1t = [A[:, 8704:10752], A[:, 10752:12800]]
        yst = [A[:, 12800:13312], A[:, 13312:13824]]
        spacer = c.const_tile[:, 63:64]
        spacer2 = c.const_tile[:, 62:63]

        ld0 = c.inc("ld", 16 * 6)

        def L0():
            nc.sync.dma_start(out=c.const_tile[:], in_=d["fconst"][:]).then_inc(sem["ld"], 16)
            nc.sync.dma_start(out=SmL, in_=d["SmL"][:]).then_inc(sem["ld"], 16)
            nc.sync.dma_start(out=EmatB, in_=d["Emat"][:]).then_inc(sem["ld"], 16)
            nc.sync.dma_start(out=E2B[0:8, :], in_=d["E2mat"][:]).then_inc(sem["ld"], 16)
            nc.sync.dma_start(out=I8B[0:8, :], in_=d["I8"][:]).then_inc(sem["ld"], 16)
            nc.sync.dma_start(out=O81[0:8, :], in_=d["ones8_128"][:]).then_inc(sem["ld"], 16)
        c.emit("sync", L0)

        g0 = c.inc("gp", 1)

        def G0():
            nc.gpsimd.wait_ge(sem["ld"], ld0)
            nc.gpsimd.memset(lo8, 0.0)
            nc.gpsimd.memset(hi8, 1.001).then_inc(sem["gp"], 1)
        c.emit("gpsimd", G0)

        prev_dv = 0
        prev_ac = 0
        gprev = g0
        for it in range(NIT):
            bank = ps["a"] if it % 2 == 0 else ps["b"]
            g1 = c.inc("gp")

            def G1(gprev=gprev, pac=prev_ac):
                nc.gpsimd.wait_ge(sem["gp"], gprev)
                if pac:
                    nc.gpsimd.wait_ge(sem["ac"], pac)
                nc.gpsimd.tensor_tensor(mid8, lo8, hi8, ALU.add)
                nc.gpsimd.tensor_scalar(mid8, mid8, 0.5, None, ALU.mult).then_inc(sem["gp"], 1)
            c.emit("gpsimd", G1)

            m1 = c.inc("mm")

            def M1(g1=g1, bank=bank, pdv=prev_dv):
                nc.tensor.wait_ge(sem["gp"], g1)
                if pdv:
                    nc.tensor.wait_ge(sem["dv"], pdv)
                nc.tensor.matmul(bank[:, 0:1], E2B[0:8, :], mid8[0:8, :],
                                 start=True, stop=True).then_inc(sem["mm"], 1)
            c.emit("tensor", M1)

            a1 = c.inc("ac")

            def A1(m1=m1, bank=bank):
                nc.scalar.wait_ge(sem["mm"], m1)
                nc.scalar.copy(midb, bank[:, 0:1]).then_inc(sem["ac"], 1)
            c.emit("scalar", A1)

            v1 = c.inc("dv")

            def V1(a1=a1):
                nc.vector.wait_ge(sem["ac"], a1)
                nc.vector.tensor_scalar(cmpb, SmL, midb, None, ALU.is_ge)
                nc.vector.tensor_copy(spacer2, spacer)
                nc.vector.reduce_sum(cntp, cmpb, axis=AX.X).then_inc(sem["dv"], 1)
            c.emit("vector", V1)
            prev_dv = v1

            m2 = c.inc("mm")

            def M2(v1=v1, bank=bank):
                nc.tensor.wait_ge(sem["dv"], v1)
                nc.tensor.matmul(bank[0:8, 4:5], EmatB, cntp,
                                 start=True, stop=True).then_inc(sem["mm"], 1)
            c.emit("tensor", M2)

            a2 = c.inc("ac")

            def A2(m2=m2, bank=bank):
                nc.scalar.wait_ge(sem["mm"], m2)
                nc.scalar.copy(cnt8[0:8, :], bank[0:8, 4:5]).then_inc(sem["ac"], 1)
            c.emit("scalar", A2)
            prev_ac = a2

            g2 = c.inc("gp")

            def G2(a2=a2):
                nc.gpsimd.wait_ge(sem["ac"], a2)
                nc.gpsimd.tensor_scalar(pred8, cnt8, 4096.0, None, ALU.is_ge)
                nc.gpsimd.tensor_tensor(dd8, mid8, lo8, ALU.subtract)
                nc.gpsimd.tensor_tensor(dd8, dd8, pred8, ALU.mult)
                nc.gpsimd.tensor_tensor(lo8, lo8, dd8, ALU.add)
                nc.gpsimd.tensor_tensor(ee8, mid8, hi8, ALU.subtract)
                nc.gpsimd.tensor_tensor(hi8, hi8, ee8, ALU.add)
                nc.gpsimd.tensor_tensor(ee8, ee8, pred8, ALU.mult)
                nc.gpsimd.tensor_tensor(hi8, hi8, ee8, ALU.subtract).then_inc(sem["gp"], 1)
            c.emit("gpsimd", G2)
            gprev = g2

        g3 = c.inc("gp")

        def G3(gprev=gprev):
            nc.gpsimd.wait_ge(sem["gp"], gprev)
            nc.gpsimd.tensor_scalar(D8[0:8, :], I8B[0:8, :], lo8[0:8, :], None, ALU.mult).then_inc(sem["gp"], 1)
        c.emit("gpsimd", G3)

        m3 = c.inc("mm")

        def M3(g3=g3):
            nc.tensor.wait_ge(sem["gp"], g3)
            nc.tensor.matmul(ps["a"][:, 8:16], O81[0:8, :], D8[0:8, :],
                             start=True, stop=True).then_inc(sem["mm"], 1)
        c.emit("tensor", M3)

        a3 = c.inc("ac")

        def A3(m3=m3):
            nc.scalar.wait_ge(sem["mm"], m3)
            nc.scalar.copy(taub, ps["a"][:, 8:16]).then_inc(sem["ac"], 1)
        c.emit("scalar", A3)

        st_done = {}
        dv_done = {}
        for i in range(NT):
            sl = i % 2
            ld_i = c.inc("ld", 48)

            def LD(i=i, sl=sl, pst=st_done.get(i - 2, 0)):
                if pst:
                    nc.sync.wait_ge(sem["st"], pst)
                nc.sync.dma_start(out=smt[sl], in_=d["Smloc"][tsl(i, 128), :]).then_inc(sem["ld"], 16)
                nc.sync.dma_start(out=prt[sl], in_=d["partial"][tsl(i, 128), :]).then_inc(sem["ld"], 16)
                nc.sync.dma_start(out=f1t[sl], in_=d["f1"][tsl(i, 128), :]).then_inc(sem["ld"], 16)
            c.emit("sync", LD)

            v2 = c.inc("dv", 5)

            def V2(i=i, sl=sl, ld_i=ld_i, a3=a3, pdv=dv_done.get(i - 2, 0)):
                nc.vector.wait_ge(sem["ld"], ld_i)
                nc.vector.wait_ge(sem["ac"], a3)
                if pdv:
                    nc.vector.wait_ge(sem["dv"], pdv)
                nc.vector.tensor_tensor(mtile, smt[sl], taub, ALU.is_ge)
                nc.vector.tensor_copy(spacer2, spacer)
                nc.vector.tensor_tensor(mtile, mtile, smt[sl], ALU.mult)
                nc.vector.tensor_copy(spacer2, spacer)
                nc.vector.reduce_sum(wcol, mtile, axis=AX.X)
                nc.vector.tensor_copy(spacer2, spacer).then_inc(sem["dv"], 1)
                for og in range(4):
                    nc.vector.scalar_tensor_tensor(
                        yst[sl] if False else prt[sl][:, tsl(og, 512)],
                        f1t[sl][:, tsl(og, 512)], wcol, prt[sl][:, tsl(og, 512)],
                        ALU.mult, ALU.add).then_inc(sem["dv"], 1)
            c.emit("vector", V2)
            dv_done[i] = c.cnt["dv"]

            c.inc("st", 16)

            def SY(i=i, sl=sl, v2=v2):
                nc.sync.wait_ge(sem["dv"], v2)
                nc.sync.dma_start(out=d["y"][tsl(i, 128), :], in_=prt[sl]).then_inc(sem["st"], 16)
            c.emit("sync", SY)
            st_done[i] = c.cnt["st"]

        st_total = c.cnt["st"]

        def FS():
            nc.sync.wait_ge(sem["st"], st_total)
        c.emit("sync", FS)
        c.run_block()
    return nc, c


PHASES_ALL = ("p1", "p2", "p3", "p4", "p5", "p6", "p7", "p8")


def make_inmap_A(inputs, C, fc, core):
    d = inputs
    x0 = d["L0_attn_X"].reshape(8, T_LOC, D)
    gum = C["gumbel"].reshape(8, T_LOC, 8)
    base = {
        "fconst": fc,
        "x0": np.ascontiguousarray(x0[core]),
        "gumb": np.ascontiguousarray(gum[core]),
        "W_in": d["L0_attn_W_in"],
        "W_out": d["L0_attn_W_out"],
        "ln_w_cols": np.ascontiguousarray(np.asarray(d["L0_attn_ln_w"]).reshape(16, 128).T),
        "ln_b_cols": np.ascontiguousarray(np.asarray(d["L0_attn_ln_b"]).reshape(16, 128).T),
        "gate_w": d["L0_ffn_gate_w"], "up_w": d["L0_ffn_up_w"], "down_w": d["L0_ffn_down_w"],
        "gate_scale_cols": np.ascontiguousarray(np.asarray(d["L0_ffn_gate_scale"]).reshape(64, 128).T),
        "up_scale_cols": np.ascontiguousarray(np.asarray(d["L0_ffn_up_scale"]).reshape(64, 128).T),
        "down_scale_cols": np.ascontiguousarray(np.asarray(d["L0_ffn_down_scale"]).reshape(16, 128).T),
        "cos0": C["cos0"], "sin0": C["sin0"], "ones1": C["ones1"], "ones512": C["ones512"],
        "wq": d["L1_attn_wq"], "wk": d["L1_attn_wk"], "wv": d["L1_attn_wv"],
        "gq": np.asarray(d["L1_attn_gq"]).reshape(1, D),
        "gk": np.asarray(d["L1_attn_gk"]).reshape(1, 512),
        "router_w": d["L1_ffn_router_w"],
        "router_b": np.asarray(d["L1_ffn_router_b"]).reshape(1, 8),
        "cos1q": C["cos1q"], "sin1q": C["sin1q"], "cos1k": C["cos1k"], "sin1k": C["sin1k"],
        "wo": d["L1_attn_wo"], "maskdiag": C["maskdiag"],
        "w_gate": d["L1_ffn_w_gate"], "w_up": d["L1_ffn_w_up"], "w_down": d["L1_ffn_w_down"],
        "b_gate": np.asarray(d["L1_ffn_b_gate"]).reshape(1, DFF1),
        "b_up": np.asarray(d["L1_ffn_b_up"]).reshape(1, DFF1),
        "b_down": np.asarray(d["L1_ffn_b_down"]).reshape(1, D),
    }
    return base


def kernel(**inputs):
    inputs = {k: np.asarray(v) for k, v in inputs.items()}
    C = host_constants()
    ncA, recA = build_launch_A(dev_out=(), phases_sel=PHASES_ALL)
    fcA = recA.const_array()
    in_mapsA = [make_inmap_A(inputs, C, fcA, cc) for cc in range(NCORE)]
    resA = run_bass_kernel_spmd(ncA, in_mapsA, core_ids=list(range(NCORE)))

    Sm_full = np.concatenate([np.asarray(resA.results[cc]["Sm"]) for cc in range(NCORE)], 0)
    SmL = np.ascontiguousarray(Sm_full.T.reshape(8, 16, 2048).reshape(128, 2048))

    ncB, recB = build_launch_B()
    fcB = recB.const_array()
    in_mapsB = []
    for cc in range(NCORE):
        in_mapsB.append({
            "fconst": fcB,
            "SmL": SmL,
            "Smloc": np.ascontiguousarray(Sm_full[cc * T_LOC:(cc + 1) * T_LOC]),
            "partial": np.asarray(resA.results[cc]["partial"]),
            "f1": np.asarray(resA.results[cc]["f1"]),
            "Emat": C["Emat"],
            "E2mat": np.ascontiguousarray(C["Emat"].T),
            "I8": C["I8"],
            "ones8_128": np.ones((8, 128), np.float32),
        })
    resB = run_bass_kernel_spmd(ncB, in_mapsB, core_ids=list(range(NCORE)))
    y = np.concatenate([np.asarray(resB.results[cc]["y"]) for cc in range(NCORE)], 0)
    return y.reshape(16, 2048, 2048).astype(np.float32)




# version 04
# speedup vs baseline: 1.0857x; 1.0857x over previous
import contextlib
import math
import numpy as np
import ml_dtypes
import concourse.bass as bass
import concourse.mybir as mybir
from concourse.bass_utils import run_bass_kernel_spmd

F32 = mybir.dt.float32
BF16 = mybir.dt.bfloat16
I32 = mybir.dt.int32
AF = mybir.ActivationFunctionType
ALU = mybir.AluOpType
AX = mybir.AxisListType
BF = ml_dtypes.bfloat16

NCORE = 8
B_LOC = 2
S = 2048
D = 2048
T_LOC = B_LOC * S
NT = T_LOC // 128
NTS = S // 128
KC = D // 128
H0 = 16
I0 = 8192
G1, HG1 = 4, 4
DFF1 = 5461
NFC = (DFF1 + 127) // 128
LASTF = DFF1 - 42 * 128
RSC = float(128.0 ** -0.5)
RSC1 = float(128.0 ** -0.25)


def tsl(i, n):
    return slice(i * n, (i + 1) * n)


def _mk_waiter(sem):
    def wait_all(eng, guards):
        for gsp in guards:
            if gsp:
                eng.wait_ge(sem[gsp[0]], gsp[1])
    return wait_all


def host_constants():
    c = {}
    t = np.arange(S, dtype=np.float64)
    invf = 1.0 / 10000.0 ** (np.arange(0, 64, 2, dtype=np.float64) / 64.0)
    ang = t[:, None] * invf[None, :]
    c["cos0"] = np.tile(np.cos(ang), (1, H0)).astype(np.float32)
    c["sin0"] = np.tile(np.sin(ang), (1, H0)).astype(np.float32)
    ang1 = t[:, None] * np.power(10000.0, -np.arange(64, dtype=np.float64) / 64.0)[None, :]
    c["cos1q"] = np.tile(np.cos(ang1), (1, H0)).astype(np.float32)
    c["sin1q"] = np.tile(np.sin(ang1), (1, H0)).astype(np.float32)
    c["cos1k"] = np.tile(np.cos(ang1), (1, G1)).astype(np.float32)
    c["sin1k"] = np.tile(np.sin(ang1), (1, G1)).astype(np.float32)
    import jax
    with jax.default_device(jax.devices("cpu")[0]):
        u = np.asarray(jax.random.uniform(jax.random.key(42), (32768, 8), dtype=np.float32))
    c["gumbel"] = (-np.log(-np.log(u + 1e-10) + 1e-10)).astype(np.float32)
    c["ones1"] = np.ones((1, 128), np.float32).astype(BF)
    c["ones512"] = np.ones((1, 512), np.float32).astype(BF)
    m = np.zeros((128, 128), np.float32)
    m[np.triu_indices(128, 1)] = -1e30
    c["maskdiag"] = m
    E = np.zeros((128, 8), np.float32)
    for p in range(128):
        E[p, p // 16] = 1.0
    c["Emat"] = E
    c["I8"] = np.eye(8, dtype=np.float32)
    return c


class Rec:
    ENGINES = ("sync", "tensor", "vector", "scalar", "gpsimd")

    def __init__(self, nc, stack):
        self.nc = nc
        self.stack = stack
        self.seq = {e: [] for e in self.ENGINES}
        self.sem = {}
        self.cnt = {}
        self.dram = {}
        self.sb = {}
        self.ps = {}
        self.consts = {}
        self.const_tile = None
        self.layout = {}

    def new_sem(self, name):
        self.sem[name] = self.stack.enter_context(self.nc.semaphore(name))
        self.cnt[name] = 0

    def inc(self, name, n=1):
        self.cnt[name] += n
        return self.cnt[name]

    def emit(self, engine, fn):
        self.seq[engine].append(fn)

    def lay(self, name, off, n, dt):
        self.layout[name] = (off, n, dt)

    def buf(self, name):
        off, n, dt = self.layout[name]
        a = self.sb["arena"]
        if dt == BF16:
            return a[:, off:off + n]
        return a[:, off:off + 2 * n].bitcast(dt)

    def const(self, val):
        val = float(val)
        if val not in self.consts:
            idx = len(self.consts)
            assert idx < 60
            self.consts[val] = idx
        i = self.consts[val]
        return self.const_tile[:, i:i + 1]

    def const_array(self):
        a = np.zeros((128, 64), np.float32)
        for v, i in self.consts.items():
            a[:, i] = v
        return a

    def run_block(self):
        nc = self.nc
        with nc.Block() as block:
            @block.sync
            def _(eng):
                for f in self.seq["sync"]:
                    f()

            @block.tensor
            def _(eng):
                for f in self.seq["tensor"]:
                    f()

            @block.vector
            def _(eng):
                for f in self.seq["vector"]:
                    f()

            @block.scalar
            def _(eng):
                for f in self.seq["scalar"]:
                    f()

            @block.gpsimd
            def _(eng):
                for f in self.seq["gpsimd"]:
                    f()

    def barrier(self):
        tgt = self.cnt["ph"] + 5
        self.cnt["ph"] = tgt
        nc, sems = self.nc, self.sem

        def mk(en):
            def f():
                eng = getattr(nc, en)
                eng.sem_inc(sems["ph"], 1)
                eng.wait_ge(sems["ph"], tgt)
            return f
        for e in self.ENGINES:
            self.emit(e, mk(e))


def emit_p1_ln0(c):
    nc, d, sem = c.nc, c.dram, c.sem
    sm = c.sb["sm32"]
    spacer = c.const_tile[:, 63:64]
    c.lay("p1_f32", 0, 10240, F32)
    c.lay("p1_bf", 20480, 8192, BF16)
    t32 = c.buf("p1_f32")
    wbf = c.buf("p1_bf")
    invD = 1.0 / D
    eps = 1e-5
    gp_after = {}

    for i in range(NT):
        s_ = i % 2
        xv = t32[:, tsl(s_, 2048)]
        xn = t32[:, 4096 + s_ * 2048: 4096 + (s_ + 1) * 2048]
        junk = t32[:, 8192:10240]
        col = lambda k: sm[:, s_ * 16 + k: s_ * 16 + k + 1]
        nmu, ssq, var_, rstd = col(0), col(1), col(2), col(3)
        nmu2, ssq2, var2, rstd2 = col(4), col(5), col(6), col(7)

        ld_i = c.inc("ld", 16)

        def L1(i=i, xv=xv):
            nc.sync.dma_start(out=xv, in_=d["x0"][tsl(i, 128), :]).then_inc(sem["ld"], 16)
        c.emit("sync", L1)

        gp_guard = gp_after.get(i - 2, 0)
        v1 = c.inc("dv")

        def V1(xv=xv, nmu=nmu, ld_i=ld_i, gp_guard=gp_guard):
            nc.vector.wait_ge(sem["ld"], ld_i)
            if gp_guard:
                nc.vector.wait_ge(sem["gp"], gp_guard)
            nc.vector.reduce_sum(nmu, xv, axis=AX.X, negate=True).then_inc(sem["dv"], 1)
        c.emit("vector", V1)

        a1 = c.inc("ac")

        def A1(xv=xv, junk=junk, ssq=ssq, ld_i=ld_i, gp_guard=gp_guard):
            nc.scalar.wait_ge(sem["ld"], ld_i)
            if gp_guard:
                nc.scalar.wait_ge(sem["gp"], gp_guard)
            nc.scalar.activation(junk, xv, AF.Square, accum_out=ssq).then_inc(sem["ac"], 1)
        c.emit("scalar", A1)

        g1 = c.inc("gp")

        def G1(nmu=nmu, ssq=ssq, var_=var_, v1=v1, a1=a1):
            nc.gpsimd.wait_ge(sem["dv"], v1)
            nc.gpsimd.wait_ge(sem["ac"], a1)
            nc.gpsimd.tensor_scalar(nmu, nmu, invD, None, ALU.mult)
            nc.gpsimd.tensor_tensor(var_, nmu, nmu, ALU.mult)
            nc.gpsimd.tensor_scalar(ssq, ssq, invD, None, ALU.mult)
            nc.gpsimd.tensor_tensor(var_, ssq, var_, ALU.subtract).then_inc(sem["gp"], 1)
        c.emit("gpsimd", G1)

        a2 = c.inc("ac", 2)

        def A2(var_=var_, rstd=rstd, g1=g1):
            nc.scalar.wait_ge(sem["gp"], g1)
            nc.scalar.activation(rstd, var_, AF.Ln, bias=c.const(eps)).then_inc(sem["ac"], 1)
            nc.scalar.activation(spacer, spacer, AF.Identity)
            nc.scalar.activation(rstd, rstd, AF.Exp, scale=-0.5).then_inc(sem["ac"], 1)
        c.emit("scalar", A2)

        v2 = c.inc("dv")

        def V2(xv=xv, xn=xn, nmu=nmu, rstd=rstd, a2=a2):
            nc.vector.wait_ge(sem["ac"], a2)
            nc.vector.tensor_scalar(xn, xv, nmu, rstd, ALU.add, op1=ALU.mult).then_inc(sem["dv"], 1)
        c.emit("vector", V2)

        g2 = c.inc("gp")

        def G2(xv=xv, xn=xn, v2=v2):
            nc.gpsimd.wait_ge(sem["dv"], v2)
            nc.gpsimd.tensor_tensor(xv, xv, xn, ALU.add).then_inc(sem["gp"], 1)
        c.emit("gpsimd", G2)

        c.inc("st", 16)

        def L2(xv=xv, i=i, g2=g2):
            nc.sync.wait_ge(sem["gp"], g2)
            nc.sync.dma_start(out=d["xb"][tsl(i, 128), :], in_=xv).then_inc(sem["st"], 16)
        c.emit("sync", L2)

        xnb = wbf[:, tsl(s_, 2048)]
        v3 = c.inc("dv")

        def V3(xn=xn, xnb=xnb):
            nc.vector.tensor_copy(xnb, xn).then_inc(sem["dv"], 1)
        c.emit("vector", V3)

        c.inc("st", 16)

        def L3(xnb=xnb, i=i, v3=v3):
            nc.sync.wait_ge(sem["dv"], v3)
            nc.sync.dma_start(out=d["xn0b"][tsl(i, 128), :], in_=xnb).then_inc(sem["st"], 16)
        c.emit("sync", L3)

        v4 = c.inc("dv")

        def V4(xn=xn, nmu2=nmu2):
            nc.vector.reduce_sum(nmu2, xn, axis=AX.X, negate=True).then_inc(sem["dv"], 1)
        c.emit("vector", V4)

        a3 = c.inc("ac")

        def A3(xn=xn, junk=junk, ssq2=ssq2, v2=v2):
            nc.scalar.wait_ge(sem["dv"], v2)
            nc.scalar.activation(junk, xn, AF.Square, accum_out=ssq2).then_inc(sem["ac"], 1)
        c.emit("scalar", A3)

        g3 = c.inc("gp")

        def G3(nmu2=nmu2, ssq2=ssq2, var2=var2, v4=v4, a3=a3):
            nc.gpsimd.wait_ge(sem["dv"], v4)
            nc.gpsimd.wait_ge(sem["ac"], a3)
            nc.gpsimd.tensor_scalar(nmu2, nmu2, invD, None, ALU.mult)
            nc.gpsimd.tensor_tensor(var2, nmu2, nmu2, ALU.mult)
            nc.gpsimd.tensor_scalar(ssq2, ssq2, invD, None, ALU.mult)
            nc.gpsimd.tensor_tensor(var2, ssq2, var2, ALU.subtract).then_inc(sem["gp"], 1)
        c.emit("gpsimd", G3)
        gp_after[i] = c.cnt["gp"]

        a4 = c.inc("ac", 2)

        def A4(var2=var2, rstd2=rstd2, g3=g3):
            nc.scalar.wait_ge(sem["gp"], g3)
            nc.scalar.activation(rstd2, var2, AF.Ln, bias=c.const(eps)).then_inc(sem["ac"], 1)
            nc.scalar.activation(spacer, spacer, AF.Identity)
            nc.scalar.activation(rstd2, rstd2, AF.Exp, scale=-0.5).then_inc(sem["ac"], 1)
        c.emit("scalar", A4)

        zbt = wbf[:, 4096 + s_ * 2048: 4096 + (s_ + 1) * 2048]
        v5 = c.inc("dv")

        def V5(xn=xn, zbt=zbt, nmu2=nmu2, rstd2=rstd2, a4=a4):
            nc.vector.wait_ge(sem["ac"], a4)
            nc.vector.tensor_scalar(zbt, xn, nmu2, rstd2, ALU.add, op1=ALU.mult).then_inc(sem["dv"], 1)
        c.emit("vector", V5)

        c.inc("st", 16)

        def L4(zbt=zbt, i=i, v5=v5):
            nc.sync.wait_ge(sem["dv"], v5)
            nc.sync.dma_start(out=d["zb"][tsl(i, 128), :], in_=zbt).then_inc(sem["st"], 16)
        c.emit("sync", L4)

    st_total = c.cnt["st"]

    def FS():
        nc.sync.wait_ge(sem["st"], st_total)
    c.emit("sync", FS)
    c.barrier()


P2_FG = list(range(12))


def emit_p2_qkv0(c):
    nc, d, sem = c.nc, c.dram, c.sem
    sm = c.sb["sm32"]
    c.lay("p2_afm", 0, KC * T_LOC, BF16)
    c.lay("p2_cos", 65536, 8192, BF16)
    c.lay("p2_sin", 73728, 8192, BF16)
    c.lay("p2_wsl", 81920, 8192, BF16)
    c.lay("p2_stg", 90112, 4096, BF16)
    afm = c.buf("p2_afm")
    cosb = c.buf("p2_cos")
    sinb = c.buf("p2_sin")
    wsl = c.buf("p2_wsl")
    stg = c.buf("p2_stg")
    ps = c.ps
    lnw = sm[:, 32:48]
    lnbc = sm[:, 48:64]
    lnb_bf = sm[:, 64:72].bitcast(BF16)
    beta = sm[:, 72:328].bitcast(BF16)
    tmpA = sm[:, 1024:1152]
    tmpB = sm[:, 1152:1280]
    tmpC = sm[:, 1280:1408]
    tmpD = sm[:, 1408:1536]
    tmpE = sm[:, 1536:1664]
    tmpF = sm[:, 1664:1792]
    evac32 = sm[:, 328:840]

    for kc in range(KC):
        c.inc("tp", 16)

        def TL(kc=kc):
            nc.sync.dma_start_transpose(
                afm[:, tsl(kc, T_LOC)], d["zb"][:, tsl(kc, 128)]).then_inc(sem["tp"], 16)
        c.emit("sync", TL)
    tp_z = c.cnt["tp"]

    c.inc("wd", 32)

    def GT():
        nc.gpsimd.dma_start(
            out=cosb[:].rearrange("p (a f) -> p a f", a=16),
            in_=d["cos0"][:].rearrange("(a p) f -> p a f", p=128)
        ).then_inc(sem["wd"], 16)
        nc.gpsimd.dma_start(
            out=sinb[:].rearrange("p (a f) -> p a f", a=16),
            in_=d["sin0"][:].rearrange("(a p) f -> p a f", p=128)
        ).then_inc(sem["wd"], 16)
    c.emit("gpsimd", GT)

    ld_ln = c.inc("ld", 48)

    def LLN():
        nc.sync.dma_start(out=lnw, in_=d["ln_w_cols"][:]).then_inc(sem["ld"], 16)
        nc.sync.dma_start(out=lnbc, in_=d["ln_b_cols"][:]).then_inc(sem["ld"], 16)
        nc.sync.dma_start(out=stg[0:1, 0:128], in_=d["ones1"][:]).then_inc(sem["ld"], 16)
    c.emit("sync", LLN)

    v_lnb = c.inc("dv")

    def VLNB():
        nc.vector.wait_ge(sem["ld"], ld_ln)
        nc.vector.tensor_copy(lnb_bf[:, 0:16], lnbc).then_inc(sem["dv"], 1)
    c.emit("vector", VLNB)

    gp_tab = c.cnt["wd"]

    def VWT():
        nc.vector.wait_ge(sem["wd"], gp_tab)
    c.emit("vector", VWT)
    evac_done = {}
    st_done = {}

    for fg in P2_FG:
        kind = "q" if fg < 4 else ("k" if fg < 8 else "v")
        mm_prev = c.cnt["mm"]
        g_sl = c.inc("wd", 16)

        def GSL(fg=fg, mm_prev=mm_prev):
            if fg > 0:
                nc.gpsimd.wait_ge(sem["mm"], mm_prev)
            nc.gpsimd.dma_start(
                out=wsl[:].rearrange("p (cc f) -> p cc f", cc=16),
                in_=d["W_in"][:, tsl(fg, 512)].rearrange("(cc p) f -> p cc f", p=128),
            ).then_inc(sem["wd"], 16)
        c.emit("gpsimd", GSL)

        mmb = c.inc("mm")
        prev_abe = c.cnt["ac"]

        def MB(fg=fg, g_sl=g_sl, prev_abe=prev_abe):
            nc.tensor.wait_ge(sem["wd"], g_sl)
            nc.tensor.wait_ge(sem["ac"], prev_abe)
            if fg == 0:
                nc.tensor.wait_ge(sem["dv"], v_lnb)
                nc.tensor.wait_ge(sem["ld"], ld_ln)
            mmi = None
            for kc in range(KC):
                mmi = nc.tensor.matmul(ps["d"][0:1, 0:512], lnb_bf[:, kc:kc + 1],
                                       wsl[:, tsl(kc, 512)], start=(kc == 0), stop=(kc == KC - 1))
            mmi.then_inc(sem["mm"], 1)
        c.emit("tensor", MB)

        a_be = c.inc("ac")

        def ABE(mmb=mmb):
            nc.scalar.wait_ge(sem["mm"], mmb)
            nc.scalar.copy(beta[0:1, 0:512], ps["d"][0:1, 0:512]).then_inc(sem["ac"], 1)
        c.emit("scalar", ABE)

        v_sc = c.inc("dv", KC)

        def VSC(mmb=mmb):
            nc.vector.wait_ge(sem["mm"], mmb)
            for kc in range(KC):
                nc.vector.tensor_scalar(wsl[:, tsl(kc, 512)], wsl[:, tsl(kc, 512)],
                                        lnw[:, kc:kc + 1], None, ALU.mult).then_inc(sem["dv"], 1)
        c.emit("vector", VSC)
        dv_slab = c.cnt["dv"]

        for tt in range(NT):
            pt = tt % NTS
            pbank = ps["a"] if tt % 2 == 0 else ps["b"]
            guard = evac_done.get((fg, tt - 2))
            if guard is None and fg > 0:
                guard = evac_done.get((fg - 1, NT - 2 + (tt % 2)))
            mq = c.inc("mm")

            def MM(fg=fg, tt=tt, pbank=pbank, guard=guard, dv_slab=dv_slab, a_be=a_be):
                nc.tensor.wait_ge(sem["dv"], dv_slab)
                nc.tensor.wait_ge(sem["ac"], a_be)
                if fg == 0 and tt == 0:
                    nc.tensor.wait_ge(sem["tp"], tp_z)
                if guard:
                    for s_g, v_g in guard:
                        nc.tensor.wait_ge(sem[s_g], v_g)
                for kc in range(KC):
                    nc.tensor.matmul(pbank[:, 0:512],
                                     afm[:, kc * T_LOC + tt * 128: kc * T_LOC + (tt + 1) * 128],
                                     wsl[:, tsl(kc, 512)], start=(kc == 0), stop=False)
                nc.tensor.matmul(pbank[:, 0:512], stg[0:1, 0:128],
                                 beta[0:1, 0:512], start=False, stop=True).then_inc(sem["mm"], 1)
            c.emit("tensor", MM)

            sslot = stg[:, 1024 + (tt % 3) * 1024: 1024 + (tt % 3) * 1024 + 512]
            slot_guard = st_done.get((fg, tt - 3))
            if slot_guard is None and fg > 0:
                slot_guard = st_done.get((fg - 1, NT - 3 + (tt % 3)))
            if kind == "v":
                a_e = c.inc("ac")

                def AE(pbank=pbank, sslot=sslot, mq=mq, slot_guard=slot_guard):
                    nc.scalar.wait_ge(sem["mm"], mq)
                    if slot_guard:
                        nc.scalar.wait_ge(sem["st"], slot_guard)
                    nc.scalar.copy(sslot, pbank[:, 0:512]).then_inc(sem["ac"], 1)
                c.emit("scalar", AE)
                c.inc("st", 16)

                def SE(sslot=sslot, fg=fg, tt=tt, a_e=a_e):
                    nc.sync.wait_ge(sem["ac"], a_e)
                    nc.sync.dma_start(out=d["v0"][tsl(tt, 128), tsl(fg - 8, 512)],
                                      in_=sslot).then_inc(sem["st"], 16)
                c.emit("sync", SE)
                evac_done[(fg, tt)] = [("ac", a_e)]
                st_done[(fg, tt)] = c.cnt["st"]
            else:
                a_ev = c.inc("ac")
                prev_vr = c.cnt["dv"]

                def AEV(pbank=pbank, mq=mq, prev_vr=prev_vr):
                    nc.scalar.wait_ge(sem["mm"], mq)
                    nc.scalar.wait_ge(sem["dv"], prev_vr)
                    nc.scalar.copy(evac32, pbank[:, 0:512]).then_inc(sem["ac"], 1)
                c.emit("scalar", AEV)
                rotv = evac32.rearrange("p (h x) -> p h x", h=4)[:, :, 0:64]
                rotp = rotv.rearrange("p h (j t) -> p h j t", t=2)
                Ev = rotp[:, :, :, 0]
                Od = rotp[:, :, :, 1]
                pass_ = evac32.rearrange("p (h x) -> p h x", h=4)[:, :, 64:128]
                cbase = pt * 512 + (fg % 4) * 128
                ct = cosb[:, cbase:cbase + 128].rearrange("p (h j) -> p h j", h=4)
                st_ = sinb[:, cbase:cbase + 128].rearrange("p (h j) -> p h j", h=4)
                oview = sslot.rearrange("p (h x) -> p h x", h=4)
                orot = oview[:, :, 0:64].rearrange("p h (j t) -> p h j t", t=2)
                oEv, oOd = orot[:, :, :, 0], orot[:, :, :, 1]
                opass = oview[:, :, 64:128]
                t3A = tmpA.rearrange("p (h j) -> p h j", h=4)
                t3B = tmpB.rearrange("p (h j) -> p h j", h=4)
                t3C = tmpC.rearrange("p (h j) -> p h j", h=4)
                t3D = tmpD.rearrange("p (h j) -> p h j", h=4)
                t3E = tmpE.rearrange("p (h j) -> p h j", h=4)
                t3F = tmpF.rearrange("p (h j) -> p h j", h=4)
                sc = RSC if kind == "q" else 1.0
                v_r = c.inc("dv", 2)
                a_p = c.inc("ac")

                def VR(Ev=Ev, Od=Od, ct=ct, st_=st_, t3A=t3A, t3B=t3B, t3C=t3C, t3D=t3D,
                       t3E=t3E, t3F=t3F, oEv=oEv, oOd=oOd, sc=sc, mq=mq, a_ev=a_ev,
                       slot_guard=slot_guard):
                    nc.vector.wait_ge(sem["ac"], a_ev)
                    if slot_guard:
                        nc.vector.wait_ge(sem["st"], slot_guard)
                    nc.vector.tensor_scalar(t3A, Ev, sc, None, ALU.mult)
                    nc.vector.tensor_scalar(t3B, Od, sc, None, ALU.mult)
                    nc.vector.tensor_tensor(t3C, t3A, ct, ALU.mult)
                    nc.vector.tensor_tensor(t3D, t3B, st_, ALU.mult)
                    nc.vector.tensor_tensor(t3E, t3B, ct, ALU.mult)
                    nc.vector.tensor_tensor(t3F, t3A, st_, ALU.mult)
                    nc.vector.tensor_tensor(oEv, t3C, t3D, ALU.subtract).then_inc(sem["dv"], 1)
                    nc.vector.tensor_tensor(oOd, t3E, t3F, ALU.add).then_inc(sem["dv"], 1)
                c.emit("vector", VR)

                def AP_(pass_=pass_, opass=opass, sc=sc, a_ev=a_ev, slot_guard=slot_guard):
                    nc.scalar.wait_ge(sem["ac"], a_ev)
                    if slot_guard:
                        nc.scalar.wait_ge(sem["st"], slot_guard)
                    nc.scalar.activation(opass, pass_, AF.Identity, scale=sc).then_inc(sem["ac"], 1)
                c.emit("scalar", AP_)

                dst = d["q0"] if kind == "q" else d["k0"]
                c.inc("st", 16)

                def SE2(sslot=sslot, dst=dst, fg=fg, tt=tt, v_r=v_r, a_p=a_p):
                    nc.sync.wait_ge(sem["dv"], v_r)
                    nc.sync.wait_ge(sem["ac"], a_p)
                    nc.sync.dma_start(out=dst[tsl(tt, 128), tsl(fg % 4, 512)],
                                      in_=sslot).then_inc(sem["st"], 16)
                c.emit("sync", SE2)
                evac_done[(fg, tt)] = [("ac", a_ev)]
                st_done[(fg, tt)] = c.cnt["st"]

    st_total = c.cnt["st"]

    def FS():
        nc.sync.wait_ge(sem["st"], st_total)
    c.emit("sync", FS)
    c.barrier()


def emit_p3_attn0(c):
    nc, d, sem = c.nc, c.dram, c.sem
    sm = c.sb["sm32"]
    ps = c.ps
    c.lay("p3_ctx", 0, 32768, BF16)
    c.lay("p3_wout", 32768, 32768, BF16)
    c.lay("p3_qf", 65536, 4096, BF16)
    c.lay("p3_kf", 69632, 4096, BF16)
    c.lay("p3_vt", 73728, 4096, BF16)
    c.lay("p3_pb", 77824, 4096, BF16)
    c.lay("p3_pn", 81920, 4096, BF16)
    c.lay("p3_pT", 86016, 8192, BF16)
    c.lay("p3_s32", 81920, 2048, F32)
    ctxb = c.buf("p3_ctx")
    woutb = c.buf("p3_wout")
    qfb = c.buf("p3_qf")
    kfb = c.buf("p3_kf")
    vtb = c.buf("p3_vt")
    pbb = c.buf("p3_pb")
    pnb = c.buf("p3_pn")
    pTb = c.buf("p3_pT")
    s32b = c.buf("p3_s32")
    xbt = [sm[:, 2048:4096], sm[:, 2048:4096]]
    one_col = c.const(1.0)

    bank_free = {}
    pb_free = [None, None]
    pn_free = [None, None]
    pT_free = None
    s4_free = None
    prev_sumread = 0
    s32_free = [None, None]
    wait_all = _mk_waiter(sem)

    wd_wout = c.inc("wd", 16)

    def GW():
        nc.gpsimd.dma_start(
            out=woutb[:].rearrange("p (cc f) -> p cc f", cc=16),
            in_=d["W_out"][:].rearrange("(cc p) f -> p cc f", p=128),
        ).then_inc(sem["wd"], 16)
    c.emit("gpsimd", GW)

    prev_head_mm = 0
    for seq in range(B_LOC):
        for h in range(H0):
            sl = h % 2
            qf = qfb[:, tsl(sl, 2048)]
            kf = kfb[:, tsl(sl, 2048)]
            vt = vtb[:, tsl(sl, 2048)]
            guard_mm = prev_head_mm
            tq = c.inc("tp", 32)
            lv = c.inc("ld", 16)

            def LD(seq=seq, h=h, qf=qf, kf=kf, vt=vt, guard_mm=guard_mm):
                if guard_mm:
                    nc.sync.wait_ge(sem["mm"], guard_mm)
                nc.sync.dma_start_transpose(
                    qf, d["q0"][tsl(seq, S), tsl(h, 128)]).then_inc(sem["tp"], 16)
                nc.sync.dma_start_transpose(
                    kf, d["k0"][tsl(seq, S), tsl(h, 128)]).then_inc(sem["tp"], 16)
                nc.sync.dma_start(
                    out=vt[:].rearrange("p (cc f) -> p cc f", cc=16),
                    in_=d["v0"][tsl(seq, S), tsl(h, 128)].rearrange("(cc p) f -> p cc f", p=128),
                ).then_inc(sem["ld"], 16)
            c.emit("sync", LD)
            head_mm_before = c.cnt["mm"]

            for g in range(4):
                for q4 in range(4):
                    qt = g * 4 + q4
                    ms = c.inc("mm")

                    def MS(qf=qf, kf=kf, qt=qt, tq=tq, s4g=s4_free):
                        nc.tensor.wait_ge(sem["tp"], tq)
                        wait_all(nc.tensor, [s4g])
                        mmi = None
                        for cc in range(4):
                            mmi = nc.tensor.matmul(ps["s4"][:, tsl(cc, 512)],
                                                   qf[:, tsl(qt, 128)], kf[:, tsl(cc, 512)],
                                                   start=True, stop=True)
                        mmi.then_inc(sem["mm"], 1)
                    c.emit("tensor", MS)

                    nmx = sm[:, 0:1]
                    sums = sm[:, 1:2]
                    inv = sm[:, 2:3]
                    s32 = s32b[:]
                    aevs = c.inc("ac")

                    def AEVS(ms=ms, s32=s32, gfree=s32_free[0]):
                        nc.scalar.wait_ge(sem["mm"], ms)
                        wait_all(nc.scalar, [gfree])
                        nc.scalar.copy(s32, ps["s4"][:]).then_inc(sem["ac"], 1)
                    c.emit("scalar", AEVS)
                    s4_free = ("ac", aevs)

                    vmx = c.inc("dv")

                    def VMX(aevs=aevs, nmx=nmx, s32=s32):
                        nc.vector.wait_ge(sem["ac"], aevs)
                        nc.vector.reduce_max(nmx, s32, axis=AX.X, negate=True).then_inc(sem["dv"], 1)
                    c.emit("vector", VMX)

                    pb = pbb[:, tsl(qt % 2, 2048)]
                    aex = c.inc("ac")

                    def AEXP(vmx=vmx, pb=pb, nmx=nmx, sums=sums, s32=s32,
                             gpb=pb_free[qt % 2], psv=prev_sumread):
                        nc.scalar.wait_ge(sem["dv"], vmx)
                        wait_all(nc.scalar, [gpb])
                        if psv:
                            nc.scalar.wait_ge(sem["dv"], psv)
                        nc.scalar.activation(pb, s32, AF.Exp, bias=nmx,
                                             accum_out=sums).then_inc(sem["ac"], 1)
                    c.emit("scalar", AEXP)
                    s32_free[0] = ("dv", c.cnt["dv"] + 1)

                    gi = c.inc("dv", 2)
                    vsc = gi

                    def VSCL(aex=aex, pb=pb, inv=inv, sums=sums):
                        nc.vector.wait_ge(sem["ac"], aex)
                        nc.vector.reciprocal(inv, sums).then_inc(sem["dv"], 1)
                        nc.vector.tensor_copy(c.const_tile[:, 62:63], c.const_tile[:, 63:64])
                        nc.vector.tensor_scalar(pb, pb, inv, None, ALU.mult).then_inc(sem["dv"], 1)
                    c.emit("vector", VSCL)
                    prev_sumread = vsc
                    s32_free[0] = ("dv", vmx)

                    tpq = c.inc("tp", 16)

                    def TPQ(vsc=vsc, pb=pb, q4=q4, gpT=pT_free):
                        nc.sync.wait_ge(sem["dv"], vsc)
                        wait_all(nc.sync, [gpT])
                        nc.sync.dma_start_transpose(
                            pTb[:].rearrange("p (cc g2) -> p cc g2", cc=16)[:, :, tsl(q4, 128)],
                            pb.rearrange("p (cc f) -> p cc f", cc=16),
                        ).then_inc(sem["tp"], 16)
                    c.emit("sync", TPQ)
                    pb_free[qt % 2] = ("tp", tpq)
                    tp_group_done = c.cnt["tp"]

                cps = ps["a"] if g % 2 == 0 else ps["b"]
                bkey = "a" if g % 2 == 0 else "b"
                mc = c.inc("mm")

                def MC(cps=cps, vt=vt, lv=lv, tp_need=tp_group_done,
                       gbank=bank_free.get(bkey)):
                    nc.tensor.wait_ge(sem["tp"], tp_need)
                    nc.tensor.wait_ge(sem["ld"], lv)
                    wait_all(nc.tensor, [gbank])
                    mmi = None
                    for cc in range(16):
                        mmi = nc.tensor.matmul(cps[:, 0:512], vt[:, tsl(cc, 128)],
                                               pTb[:, tsl(cc, 512)],
                                               start=(cc == 0), stop=(cc == 15))
                    mmi.then_inc(sem["mm"], 1)
                c.emit("tensor", MC)
                pT_free = ("mm", mc)

                ce = c.inc("ac")

                def CEV(mc=mc, cps=cps, h=h, g=g):
                    nc.scalar.wait_ge(sem["mm"], mc)
                    nc.scalar.copy(ctxb[:, h * 2048 + g * 512: h * 2048 + (g + 1) * 512],
                                   cps[:, 0:512]).then_inc(sem["ac"], 1)
                c.emit("scalar", CEV)
                bank_free[bkey] = ("ac", ce)
            prev_head_mm = head_mm_before

        cev_all = c.cnt["ac"]
        for qt in range(NTS):
            i_glob = seq * NTS + qt
            xslot = xbt[qt % 2]
            lxb = c.inc("ld", 16)

            def LXB(xslot=xslot, i_glob=i_glob):
                nc.sync.dma_start(out=xslot[:, 0:2048],
                                  in_=d["xb"][tsl(i_glob, 128), :]).then_inc(sem["ld"], 16)
            c.emit("sync", LXB)

            pbanks = {0: ("a", ps["a"]), 1: ("b", ps["b"]), 2: ("c", ps["c"]), 3: ("d", ps["d"])}
            for og in range(4):
                bkey, bank = pbanks[og]
                mp = c.inc("mm")

                def MP(qt=qt, og=og, bank=bank, cev_all=cev_all, wd_wout=wd_wout,
                       gbank=bank_free.get(bkey)):
                    nc.tensor.wait_ge(sem["ac"], cev_all)
                    nc.tensor.wait_ge(sem["wd"], wd_wout)
                    wait_all(nc.tensor, [gbank])
                    mmi = None
                    for hh in range(16):
                        mmi = nc.tensor.matmul(
                            bank[:, 0:512],
                            ctxb[:, hh * 2048 + qt * 128: hh * 2048 + (qt + 1) * 128],
                            woutb[:, hh * 2048 + og * 512: hh * 2048 + (og + 1) * 512],
                            start=(hh == 0), stop=(hh == 15))
                    mmi.then_inc(sem["mm"], 1)
                c.emit("tensor", MP)

                vp = c.inc("dv")

                def VPR(mp=mp, og=og, bank=bank, xslot=xslot, lxb=lxb):
                    nc.vector.wait_ge(sem["mm"], mp)
                    nc.vector.wait_ge(sem["ld"], lxb)
                    nc.vector.tensor_tensor(xslot[:, tsl(og, 512)], bank[:, 0:512],
                                            xslot[:, tsl(og, 512)], ALU.add).then_inc(sem["dv"], 1)
                c.emit("vector", VPR)
                bank_free[bkey] = ("dv", vp)

            vlast = c.cnt["dv"]
            c.inc("st", 16)

            def SX(xslot=xslot, i_glob=i_glob, vlast=vlast):
                nc.sync.wait_ge(sem["dv"], vlast)
                nc.sync.dma_start(out=d["x1a"][tsl(i_glob, 128), :],
                                  in_=xslot[:, 0:2048]).then_inc(sem["st"], 16)
            c.emit("sync", SX)
        mm_proj_done = c.cnt["mm"]
        prev_head_mm = max(prev_head_mm, mm_proj_done)

    st_total = c.cnt["st"]

    def FS():
        nc.sync.wait_ge(sem["st"], st_total)
    c.emit("sync", FS)
    c.barrier()


def emit_p4_ffn0(c):
    nc, d, sem = c.nc, c.dram, c.sem
    sm = c.sb["sm32"]
    ps = c.ps
    wait_all = _mk_waiter(sem)

    c.lay("p4_dwi", 0, 4096, I32)
    c.lay("p4_dwb", 16384, 4096, BF16)
    c.lay("p4_dwt", 20480, 4096, BF16)
    dwi = c.buf("p4_dwi")
    dwb = c.buf("p4_dwb")
    dwt = c.buf("p4_dwt")
    bias_c = sm[:, 8:9]
    scale_c = sm[:, 9:25]

    ld_sc = c.inc("ld", 16)

    def LSC():
        nc.sync.dma_start(out=scale_c, in_=d["down_scale_cols"][:]).then_inc(sem["ld"], 16)
    c.emit("sync", LSC)

    prev_tp4a = 0
    prev_st4a = 0
    prev_ac4a = 0
    for hc in range(16):
        for half in range(2):
            ld_i = c.inc("ld", 16)

            def L4A(hc=hc, half=half, pst=prev_st4a):
                if pst:
                    nc.sync.wait_ge(sem["st"], pst)
                nc.sync.dma_start(out=dwi[:],
                                  in_=d["down_w"][tsl(hc, 128), tsl(half, 4096)]).then_inc(sem["ld"], 16)
            c.emit("sync", L4A)

            gb = c.inc("gp")

            def GB(hc=hc, pac=prev_ac4a):
                nc.gpsimd.wait_ge(sem["ld"], ld_sc)
                if pac:
                    nc.gpsimd.wait_ge(sem["ac"], pac)
                nc.gpsimd.tensor_scalar(bias_c, scale_c[:, hc:hc + 1], -128.0, None, ALU.mult).then_inc(sem["gp"], 1)
            c.emit("gpsimd", GB)

            aq = c.inc("ac")

            def A4A(hc=hc, ld_i=ld_i, gb=gb, ptp=prev_tp4a):
                nc.scalar.wait_ge(sem["ld"], ld_i)
                nc.scalar.wait_ge(sem["gp"], gb)
                if ptp:
                    nc.scalar.wait_ge(sem["tp"], ptp)
                nc.scalar.activation(dwb[:], dwi[:], AF.Identity,
                                     bias=bias_c, scale=scale_c[:, hc:hc + 1]).then_inc(sem["ac"], 1)
            c.emit("scalar", A4A)
            prev_ac4a = aq

            tpq = c.inc("tp", 16)

            def T4A(aq=aq, pst=prev_st4a):
                nc.sync.wait_ge(sem["ac"], aq)
                if pst:
                    nc.sync.wait_ge(sem["st"], pst)
                nc.sync.dma_start_transpose(
                    dwt[:].rearrange("p (ic f) -> p ic f", ic=32),
                    dwb[:].rearrange("p (ic f) -> p ic f", ic=32),
                ).then_inc(sem["tp"], 16)
            c.emit("sync", T4A)
            prev_tp4a = tpq

            stq = c.inc("st", 16)

            def S4A(hc=hc, half=half, tpq=tpq):
                nc.sync.wait_ge(sem["tp"], tpq)
                nc.sync.dma_start(
                    out=d["wdt"][tsl(half, 4096), tsl(hc, 128)].rearrange("(ic p) h -> p ic h", p=128),
                    in_=dwt[:].rearrange("p (ic f) -> p ic f", ic=32),
                ).then_inc(sem["st"], 16)
            c.emit("sync", S4A)
            prev_st4a = stq

    c.lay("p4_afm", 0, KC * T_LOC, BF16)
    c.lay("p4_gwi", 65536, 2048, I32)
    c.lay("p4_uwi", 73728, 2048, I32)
    c.lay("p4_gbf", 81920, 2048, BF16)
    c.lay("p4_ubf", 83968, 2048, BF16)
    c.lay("p4_gT", 86016, 2048, BF16)
    c.lay("p4_uT", 88064, 2048, BF16)
    c.lay("p4_hst", 90112, 4096, BF16)
    afm = c.buf("p4_afm")
    gwi = c.buf("p4_gwi")
    uwi = c.buf("p4_uwi")
    gbf = c.buf("p4_gbf")
    ubf = c.buf("p4_ubf")
    gT = c.buf("p4_gT")
    uT = c.buf("p4_uT")
    hst = c.buf("p4_hst")
    gsc = sm[:, 32:96]
    usc = sm[:, 96:160]
    gbias = sm[:, 8:9]
    ubias = sm[:, 25:26]
    sst = sm[:, 1024:1280].bitcast(BF16)
    st4a_total = c.cnt["st"]

    def FS4A():
        nc.sync.wait_ge(sem["st"], st4a_total)
    c.emit("sync", FS4A)
    c.barrier()

    for kc in range(KC):
        c.inc("tp", 16)

        def TLX(kc=kc):
            nc.sync.dma_start_transpose(
                afm[:, tsl(kc, T_LOC)], d["xn0b"][:, tsl(kc, 128)]).then_inc(sem["tp"], 16)
        c.emit("sync", TLX)
    tp_x = c.cnt["tp"]

    ld_gs = c.inc("ld", 32)

    def LGS():
        nc.sync.dma_start(out=gsc, in_=d["gate_scale_cols"][:]).then_inc(sem["ld"], 16)
        nc.sync.dma_start(out=usc, in_=d["up_scale_cols"][:]).then_inc(sem["ld"], 16)
    c.emit("sync", LGS)

    bank_free = {}
    prev_mm_ic = 0
    prev_tp_ic = 0
    prev_st_hh = 0
    prev_dv_sil = 0
    for ic in range(64):
        ld_i = c.inc("ld", 32)

        def L4B(ic=ic, pmm=prev_mm_ic):
            if pmm:
                nc.sync.wait_ge(sem["mm"], pmm)
            nc.sync.dma_start(out=gwi[:], in_=d["gate_w"][tsl(ic, 128), :]).then_inc(sem["ld"], 16)
            nc.sync.dma_start(out=uwi[:], in_=d["up_w"][tsl(ic, 128), :]).then_inc(sem["ld"], 16)
        c.emit("sync", L4B)

        gb2 = c.inc("gp", 2)
        prev_adq = c.cnt["ac"]

        def GB2(ic=ic, pac=prev_adq):
            nc.gpsimd.wait_ge(sem["ld"], ld_gs)
            if pac:
                nc.gpsimd.wait_ge(sem["ac"], pac)
            nc.gpsimd.tensor_scalar(gbias, gsc[:, ic:ic + 1], -128.0, None, ALU.mult).then_inc(sem["gp"], 1)
            nc.gpsimd.tensor_scalar(ubias, usc[:, ic:ic + 1], -128.0, None, ALU.mult).then_inc(sem["gp"], 1)
        c.emit("gpsimd", GB2)

        adq = c.inc("ac", 2)

        def A4B(ic=ic, ld_i=ld_i, gb2=gb2, ptp=prev_tp_ic):
            nc.scalar.wait_ge(sem["ld"], ld_i)
            nc.scalar.wait_ge(sem["gp"], gb2)
            if ptp:
                nc.scalar.wait_ge(sem["tp"], ptp)
            nc.scalar.activation(gbf[:], gwi[:], AF.Identity,
                                 bias=gbias, scale=gsc[:, ic:ic + 1]).then_inc(sem["ac"], 1)
            nc.scalar.activation(ubf[:], uwi[:], AF.Identity,
                                 bias=ubias, scale=usc[:, ic:ic + 1]).then_inc(sem["ac"], 1)
        c.emit("scalar", A4B)

        tp2 = c.inc("tp", 32)

        def T4B(adq=adq, pmm=prev_mm_ic):
            nc.sync.wait_ge(sem["ac"], adq)
            if pmm:
                nc.sync.wait_ge(sem["mm"], pmm)
            nc.sync.dma_start_transpose(
                gT[:].rearrange("p (hc f) -> p hc f", hc=16),
                gbf[:].rearrange("p (hc f) -> p hc f", hc=16)).then_inc(sem["tp"], 16)
            nc.sync.dma_start_transpose(
                uT[:].rearrange("p (hc f) -> p hc f", hc=16),
                ubf[:].rearrange("p (hc f) -> p hc f", hc=16)).then_inc(sem["tp"], 16)
        c.emit("sync", T4B)
        prev_tp_ic = tp2

        for tg in range(8):
            pg = ps["a"] if tg % 2 == 0 else ps["c"]
            pu = ps["b"] if tg % 2 == 0 else ps["d"]
            kg = "a" if tg % 2 == 0 else "cc"
            ku = "b" if tg % 2 == 0 else "dd"
            mg = c.inc("mm", 2)

            def MG(tg=tg, pg=pg, pu=pu, tp2=tp2,
                   ga=bank_free.get(kg), gb_=bank_free.get(ku)):
                nc.tensor.wait_ge(sem["tp"], tp2)
                nc.tensor.wait_ge(sem["tp"], tp_x)
                wait_all(nc.tensor, [ga, gb_])
                mmi = None
                for hc in range(KC):
                    mmi = nc.tensor.matmul(pg[:, 0:512], gT[:, tsl(hc, 128)],
                                           afm[:, hc * T_LOC + tg * 512: hc * T_LOC + (tg + 1) * 512],
                                           start=(hc == 0), stop=(hc == KC - 1))
                mmi.then_inc(sem["mm"], 1)
                for hc in range(KC):
                    mmi = nc.tensor.matmul(pu[:, 0:512], uT[:, tsl(hc, 128)],
                                           afm[:, hc * T_LOC + tg * 512: hc * T_LOC + (tg + 1) * 512],
                                           start=(hc == 0), stop=(hc == KC - 1))
                mmi.then_inc(sem["mm"], 1)
            c.emit("tensor", MG)

            asi = c.inc("ac")

            def ASI(mg=mg, pg=pg, pdv=prev_dv_sil):
                nc.scalar.wait_ge(sem["mm"], mg - 1)
                if pdv:
                    nc.scalar.wait_ge(sem["dv"], pdv)
                nc.scalar.activation(sst, pg[:, 0:512], AF.Silu).then_inc(sem["ac"], 1)
            c.emit("scalar", ASI)
            bank_free[kg] = ("ac", asi)

            vmu = c.inc("dv")

            def VMU(mg=mg, asi=asi, pu=pu, tg=tg, pst=prev_st_hh):
                nc.vector.wait_ge(sem["mm"], mg)
                nc.vector.wait_ge(sem["ac"], asi)
                if pst:
                    nc.vector.wait_ge(sem["st"], pst)
                nc.vector.tensor_tensor(hst[:, tsl(tg, 512)], sst, pu[:, 0:512],
                                        ALU.mult).then_inc(sem["dv"], 1)
            c.emit("vector", VMU)
            bank_free[ku] = ("dv", vmu)
            prev_dv_sil = vmu

        sth = c.inc("st", 16)
        vlast = c.cnt["dv"]

        def SHH(ic=ic, vlast=vlast):
            nc.sync.wait_ge(sem["dv"], vlast)
            nc.sync.dma_start(out=d["hh"][tsl(ic, 128), :], in_=hst[:]).then_inc(sem["st"], 16)
        c.emit("sync", SHH)
        prev_st_hh = sth
        prev_mm_ic = c.cnt["mm"]

    st4b_total = c.cnt["st"]

    def FS4B():
        nc.sync.wait_ge(sem["st"], st4b_total)
    c.emit("sync", FS4B)
    c.barrier()

    c.lay("p4_hht", 0, 32768, BF16)
    c.lay("p4_wds", 32768, 32768, BF16)
    hht = c.buf("p4_hht")
    wds = c.buf("p4_wds")
    stages = [sm[:, 1024:1280].bitcast(BF16), sm[:, 1280:1536].bitcast(BF16)]

    bank_free = {}
    stage_free = [None, None]
    prev_mm_tb = 0
    for tb in range(8):
        ld_h = c.inc("ld", 16)

        def LHH(tb=tb, pmm=prev_mm_tb):
            if pmm:
                nc.sync.wait_ge(sem["mm"], pmm)
            nc.sync.dma_start(
                out=hht[:].rearrange("p (ic t) -> p ic t", ic=64),
                in_=d["hh"][:, tsl(tb, 512)].rearrange("(ic p) t -> p ic t", p=128),
            ).then_inc(sem["ld"], 16)
        c.emit("sync", LHH)

        for hg in range(4):
            mm_before_hg = c.cnt["mm"]
            ld_w = c.inc("ld", 16)

            def LWD(hg=hg, pmm=mm_before_hg, first=(tb == 0 and hg == 0)):
                if not first:
                    nc.sync.wait_ge(sem["mm"], pmm)
                nc.sync.dma_start(
                    out=wds[:].rearrange("p (ic h) -> p ic h", ic=64),
                    in_=d["wdt"][:, tsl(hg, 512)].rearrange("(ic p) h -> p ic h", p=128),
                ).then_inc(sem["ld"], 16)
            c.emit("sync", LWD)

            for tc in range(4):
                i_glob = tb * 4 + tc
                pd = ps["a"] if tc % 2 == 0 else ps["b"]
                kd = "a" if tc % 2 == 0 else "b"
                md = c.inc("mm")

                def MD(tc=tc, pd=pd, ld_w=ld_w, ld_h=ld_h, gk=bank_free.get(kd)):
                    nc.tensor.wait_ge(sem["ld"], ld_w)
                    nc.tensor.wait_ge(sem["ld"], ld_h)
                    wait_all(nc.tensor, [gk])
                    mmi = None
                    for ic in range(64):
                        mmi = nc.tensor.matmul(
                            pd[:, 0:512],
                            hht[:, ic * 512 + tc * 128: ic * 512 + (tc + 1) * 128],
                            wds[:, tsl(ic, 512)],
                            start=(ic == 0), stop=(ic == 63))
                    mmi.then_inc(sem["mm"], 1)
                c.emit("tensor", MD)

                aev = c.inc("ac")
                stg_ = stages[tc % 2]

                def AEVD(md=md, pd=pd, stg_=stg_, gs=stage_free[tc % 2]):
                    nc.scalar.wait_ge(sem["mm"], md)
                    wait_all(nc.scalar, [gs])
                    nc.scalar.copy(stg_, pd[:, 0:512]).then_inc(sem["ac"], 1)
                c.emit("scalar", AEVD)
                bank_free[kd] = ("ac", aev)

                std = c.inc("st", 16)

                def SD(i_glob=i_glob, hg=hg, aev=aev, stg_=stg_):
                    nc.sync.wait_ge(sem["ac"], aev)
                    nc.sync.dma_start(out=d["ffn0b"][tsl(i_glob, 128), tsl(hg, 512)],
                                      in_=stg_).then_inc(sem["st"], 16)
                c.emit("sync", SD)
                stage_free[tc % 2] = ("st", std)
        prev_mm_tb = c.cnt["mm"]

    st_total = c.cnt["st"]

    def FS4C():
        nc.sync.wait_ge(sem["st"], st_total)
    c.emit("sync", FS4C)
    c.barrier()


def emit_p5_ln1(c):
    nc, d, sem = c.nc, c.dram, c.sem
    sm = c.sb["sm32"]
    spacer = c.const_tile[:, 63:64]
    c.lay("p5_f32", 0, 10240, F32)
    c.lay("p5_bf", 20480, 12288, BF16)
    t32 = c.buf("p5_f32")
    wbf = c.buf("p5_bf")
    invD = 1.0 / D
    eps = 1e-5
    gp_after = {}

    for i in range(NT):
        s_ = i % 2
        xv = t32[:, tsl(s_, 2048)]
        xn = t32[:, 4096 + s_ * 2048: 4096 + (s_ + 1) * 2048]
        junk = t32[:, 8192:10240]
        fbt = wbf[:, 8192 + s_ * 2048: 8192 + (s_ + 1) * 2048]
        col = lambda k: sm[:, s_ * 16 + k: s_ * 16 + k + 1]
        nmu, ssq, var_, rstd = col(0), col(1), col(2), col(3)

        ld_i = c.inc("ld", 32)

        def L1(i=i, xv=xv, fbt=fbt):
            nc.sync.dma_start(out=xv, in_=d["x1a"][tsl(i, 128), :]).then_inc(sem["ld"], 16)
            nc.sync.dma_start(out=fbt, in_=d["ffn0b"][tsl(i, 128), :]).then_inc(sem["ld"], 16)
        c.emit("sync", L1)

        gp_guard = gp_after.get(i - 2, 0)
        g0 = c.inc("gp")

        def G0(xv=xv, fbt=fbt, ld_i=ld_i, gp_guard=gp_guard):
            nc.gpsimd.wait_ge(sem["ld"], ld_i)
            nc.gpsimd.tensor_tensor(xv, xv, fbt, ALU.add).then_inc(sem["gp"], 1)
        c.emit("gpsimd", G0)

        c.inc("st", 16)

        def SX1(xv=xv, i=i, g0=g0):
            nc.sync.wait_ge(sem["gp"], g0)
            nc.sync.dma_start(out=d["x1"][tsl(i, 128), :], in_=xv).then_inc(sem["st"], 16)
        c.emit("sync", SX1)

        v1 = c.inc("dv")

        def V1(xv=xv, nmu=nmu, g0=g0, gp_guard=gp_guard):
            nc.vector.wait_ge(sem["gp"], g0)
            if gp_guard:
                nc.vector.wait_ge(sem["gp"], gp_guard)
            nc.vector.reduce_sum(nmu, xv, axis=AX.X, negate=True).then_inc(sem["dv"], 1)
        c.emit("vector", V1)

        a1 = c.inc("ac")

        def A1(xv=xv, junk=junk, ssq=ssq, g0=g0):
            nc.scalar.wait_ge(sem["gp"], g0)
            nc.scalar.activation(junk, xv, AF.Square, accum_out=ssq).then_inc(sem["ac"], 1)
        c.emit("scalar", A1)

        g1 = c.inc("gp")

        def G1(nmu=nmu, ssq=ssq, var_=var_, v1=v1, a1=a1):
            nc.gpsimd.wait_ge(sem["dv"], v1)
            nc.gpsimd.wait_ge(sem["ac"], a1)
            nc.gpsimd.tensor_scalar(nmu, nmu, invD, None, ALU.mult)
            nc.gpsimd.tensor_tensor(var_, nmu, nmu, ALU.mult)
            nc.gpsimd.tensor_scalar(ssq, ssq, invD, None, ALU.mult)
            nc.gpsimd.tensor_tensor(var_, ssq, var_, ALU.subtract).then_inc(sem["gp"], 1)
        c.emit("gpsimd", G1)
        gp_after[i] = c.cnt["gp"]

        a2 = c.inc("ac", 2)

        def A2(var_=var_, rstd=rstd, g1=g1):
            nc.scalar.wait_ge(sem["gp"], g1)
            nc.scalar.activation(rstd, var_, AF.Ln, bias=c.const(eps)).then_inc(sem["ac"], 1)
            nc.scalar.activation(spacer, spacer, AF.Identity)
            nc.scalar.activation(rstd, rstd, AF.Exp, scale=-0.5).then_inc(sem["ac"], 1)
        c.emit("scalar", A2)

        v2 = c.inc("dv")

        def V2(xv=xv, xn=xn, nmu=nmu, rstd=rstd, a2=a2):
            nc.vector.wait_ge(sem["ac"], a2)
            nc.vector.tensor_scalar(xn, xv, nmu, rstd, ALU.add, op1=ALU.mult).then_inc(sem["dv"], 1)
        c.emit("vector", V2)

        xnb = wbf[:, tsl(s_, 2048)]
        v3 = c.inc("dv")

        def V3(xn=xn, xnb=xnb):
            nc.vector.tensor_copy(xnb, xn).then_inc(sem["dv"], 1)
        c.emit("vector", V3)

        c.inc("st", 16)

        def L3(xnb=xnb, i=i, v3=v3):
            nc.sync.wait_ge(sem["dv"], v3)
            nc.sync.dma_start(out=d["xn1b"][tsl(i, 128), :], in_=xnb).then_inc(sem["st"], 16)
        c.emit("sync", L3)

        g2 = c.inc("gp")

        def G2(xn=xn, xnb=xnb, fbt=fbt, v3=v3):
            nc.gpsimd.wait_ge(sem["dv"], v3)
            nc.gpsimd.tensor_tensor(fbt, xn, xnb, ALU.subtract).then_inc(sem["gp"], 1)
        c.emit("gpsimd", G2)

        c.inc("st", 16)

        def L4(fbt=fbt, i=i, g2=g2):
            nc.sync.wait_ge(sem["gp"], g2)
            nc.sync.dma_start(out=d["xn1lo"][tsl(i, 128), :], in_=fbt).then_inc(sem["st"], 16)
        c.emit("sync", L4)

    st_total = c.cnt["st"]

    def FS():
        nc.sync.wait_ge(sem["st"], st_total)
    c.emit("sync", FS)
    c.barrier()


def emit_p6_qkv1(c):
    nc, d, sem = c.nc, c.dram, c.sem
    sm = c.sb["sm32"]
    ps = c.ps
    wait_all = _mk_waiter(sem)
    spacer = c.const_tile[:, 63:64]
    spacer2 = c.const_tile[:, 62:63]
    eps1 = 1e-6
    c.lay("p6_afm", 0, KC * T_LOC, BF16)
    c.lay("p6_wsl", 65536, 8192, BF16)
    c.lay("p6_q32", 73728, 512, F32)
    c.lay("p6_qn32", 74752, 512, F32)
    c.lay("p6_ctab", 75776, 256, BF16)
    c.lay("p6_stab", 76032, 256, BF16)
    c.lay("p6_ost", 77824, 2048, BF16)
    c.lay("p6_gqb", 79872, 2048, F32)
    c.lay("p6_gkb", 83968, 512, F32)
    c.lay("p6_whi", 84992, 128, BF16)
    c.lay("p6_wlo", 85120, 128, BF16)
    c.lay("p6_wr", 85248, 128, F32)
    afm = c.buf("p6_afm")
    wsl = c.buf("p6_wsl")
    q32 = c.buf("p6_q32")
    qn32 = c.buf("p6_qn32")
    ctab = c.buf("p6_ctab")
    stab = c.buf("p6_stab")
    ost = c.buf("p6_ost")
    gqb = c.buf("p6_gqb")
    gkb = c.buf("p6_gkb")
    whi = c.buf("p6_whi")
    wlo = c.buf("p6_wlo")
    wr32 = c.buf("p6_wr")
    rout = sm[:, 256:512]
    gumb = sm[:, 512:768]
    bbc = sm[:, 768:776]
    msq = sm[:, 800:804]
    rr = sm[:, 804:808]
    smx = sm[:, 808:809]
    ssum = sm[:, 809:810]
    sinv = sm[:, 810:811]
    sm8 = sm[:, 812:820]

    for kc in range(KC):
        c.inc("tp", 16)

        def TLX(kc=kc):
            nc.sync.dma_start_transpose(
                afm[:, tsl(kc, T_LOC)], d["xn1b"][:, tsl(kc, 128)]).then_inc(sem["tp"], 16)
        c.emit("sync", TLX)
    tp_x = c.cnt["tp"]

    ld_ot = c.inc("ld", 16 * 5)

    def LOT():
        nc.sync.dma_start(out=gqb[:], in_=d["gq"][:].broadcast_to([128, D])).then_inc(sem["ld"], 16)
        nc.sync.dma_start(out=gkb[:], in_=d["gk"][:].broadcast_to([128, 512])).then_inc(sem["ld"], 16)
        nc.sync.dma_start(out=bbc, in_=d["router_b"][:].broadcast_to([128, 8])).then_inc(sem["ld"], 16)
        nc.sync.dma_start(out=gumb.rearrange("p (a e) -> p a e", a=32),
                          in_=d["gumb"][:].rearrange("(a p) e -> p a e", p=128)).then_inc(sem["ld"], 16)
        nc.sync.dma_start(out=wr32.rearrange("p (cc e) -> p cc e", cc=16),
                          in_=d["router_w"][:].rearrange("(cc p) e -> p cc e", p=128)).then_inc(sem["ld"], 16)
    c.emit("sync", LOT)

    v_w = c.inc("dv", 2)

    def VW():
        nc.vector.wait_ge(sem["ld"], ld_ot)
        nc.vector.tensor_copy(whi[:], wr32[:]).then_inc(sem["dv"], 1)
        nc.vector.tensor_copy(spacer2, spacer)
        nc.vector.tensor_tensor(wlo[:], wr32[:], whi[:], ALU.subtract).then_inc(sem["dv"], 1)
    c.emit("vector", VW)

    bank_free = {}
    stage_free = [None, None]
    tab_free = [None, None]
    prev_mm_fg = 0
    prev_post = 0
    for fg in range(6):
        kind = "q" if fg < 4 else ("k" if fg == 4 else "v")
        wname = "wq" if fg < 4 else ("wk" if fg == 4 else "wv")
        wcol = tsl(fg, 512) if fg < 4 else slice(0, 512)
        g_sl = c.inc("wd", 16)

        def GSL(wname=wname, wcol=wcol, pmm=prev_mm_fg):
            if pmm:
                nc.gpsimd.wait_ge(sem["mm"], pmm)
            nc.gpsimd.dma_start(
                out=wsl[:].rearrange("p (cc f) -> p cc f", cc=16),
                in_=d[wname][:, wcol].rearrange("(cc p) f -> p cc f", p=128),
            ).then_inc(sem["wd"], 16)
        c.emit("gpsimd", GSL)

        for tt in range(NT):
            pt = tt % NTS
            pbank = ps["a"] if tt % 2 == 0 else ps["b"]
            bkey = "a" if tt % 2 == 0 else "b"
            mq = c.inc("mm")

            def MM(pbank=pbank, tt=tt, g_sl=g_sl, gk_=bank_free.get(bkey)):
                nc.tensor.wait_ge(sem["wd"], g_sl)
                nc.tensor.wait_ge(sem["tp"], tp_x)
                wait_all(nc.tensor, [gk_])
                mmi = None
                for kc in range(KC):
                    mmi = nc.tensor.matmul(pbank[:, 0:512],
                                           afm[:, kc * T_LOC + tt * 128: kc * T_LOC + (tt + 1) * 128],
                                           wsl[:, tsl(kc, 512)], start=(kc == 0), stop=(kc == KC - 1))
                mmi.then_inc(sem["mm"], 1)
            c.emit("tensor", MM)

            sslot = ost[:, tsl(tt % 2, 1024)][:, 0:512]
            sguard = stage_free[tt % 2]
            if kind == "v":
                a_e = c.inc("ac")

                def AE(pbank=pbank, sslot=sslot, mq=mq, sguard=sguard):
                    nc.scalar.wait_ge(sem["mm"], mq)
                    wait_all(nc.scalar, [sguard])
                    nc.scalar.copy(sslot, pbank[:, 0:512]).then_inc(sem["ac"], 1)
                c.emit("scalar", AE)
                bank_free[bkey] = ("ac", a_e)
                c.inc("st", 16)
                stq = c.cnt["st"]

                def SE(sslot=sslot, tt=tt, a_e=a_e):
                    nc.sync.wait_ge(sem["ac"], a_e)
                    nc.sync.dma_start(out=d["v1"][tsl(tt, 128), :], in_=sslot).then_inc(sem["st"], 16)
                c.emit("sync", SE)
                stage_free[tt % 2] = ("st", stq)
            else:
                tguard = ("dv", prev_post) if prev_post else None
                c.inc("wd", 32)
                wd_t = c.cnt["wd"]
                cs_src = "cos1q" if kind == "q" else "cos1k"
                sn_src = "sin1q" if kind == "q" else "sin1k"
                tcol = tsl(fg, 256) if kind == "q" else slice(0, 256)

                def LT(pt=pt, tguard=tguard, cs_src=cs_src, sn_src=sn_src, tcol=tcol):
                    wait_all(nc.gpsimd, [tguard])
                    nc.gpsimd.dma_start(out=ctab[:], in_=d[cs_src][tsl(pt, 128), tcol]).then_inc(sem["wd"], 16)
                    nc.gpsimd.dma_start(out=stab[:], in_=d[sn_src][tsl(pt, 128), tcol]).then_inc(sem["wd"], 16)
                c.emit("gpsimd", LT)

                a_e = c.inc("ac")

                def AE2(pbank=pbank, mq=mq, ppost=prev_post):
                    nc.scalar.wait_ge(sem["mm"], mq)
                    if ppost:
                        nc.scalar.wait_ge(sem["dv"], ppost)
                    nc.scalar.copy(q32[:], pbank[:, 0:512]).then_inc(sem["ac"], 1)
                c.emit("scalar", AE2)
                bank_free[bkey] = ("ac", a_e)

                a_sq = c.inc("ac")

                def ASQ(a_e=a_e):
                    nc.scalar.activation(qn32[:], q32[:], AF.Square).then_inc(sem["ac"], 1)
                c.emit("scalar", ASQ)

                v_rd = c.inc("dv")

                def VRD(a_sq=a_sq):
                    nc.vector.wait_ge(sem["ac"], a_sq)
                    nc.vector.reduce_sum(msq, qn32[:].rearrange("p (h f) -> p h f", h=4),
                                         axis=AX.X).then_inc(sem["dv"], 1)
                c.emit("vector", VRD)

                a_rr = c.inc("ac", 2)

                def ARR(v_rd=v_rd):
                    nc.scalar.wait_ge(sem["dv"], v_rd)
                    nc.scalar.activation(rr, msq, AF.Ln, bias=c.const(eps1), scale=1.0 / 128.0).then_inc(sem["ac"], 1)
                    nc.scalar.activation(spacer, spacer, AF.Identity)
                    nc.scalar.activation(rr, rr, AF.Exp, scale=-0.5).then_inc(sem["ac"], 1)
                c.emit("scalar", ARR)

                gslice = gqb[:, tsl(fg, 512)] if kind == "q" else gkb[:]
                v_pp = c.inc("dv", 2)

                def VPP(a_rr=a_rr, gslice=gslice, wd_t=wd_t, sguard=sguard, sslot=sslot):
                    nc.vector.wait_ge(sem["ac"], a_rr)
                    nc.vector.wait_ge(sem["wd"], wd_t)
                    wait_all(nc.vector, [sguard])
                    for h in range(4):
                        nc.vector.tensor_scalar(qn32[:, tsl(h, 128)], q32[:, tsl(h, 128)],
                                                rr[:, h:h + 1], RSC1, ALU.mult, op1=ALU.mult)
                    nc.vector.tensor_copy(spacer2, spacer)
                    nc.vector.tensor_tensor(q32[:], qn32[:], gslice, ALU.mult)
                    nc.vector.tensor_copy(spacer2, spacer)
                    qv = q32.rearrange("p (h f) -> p h f", h=4)
                    E = qv[:, :, 0:64]
                    O = qv[:, :, 64:128]
                    ct3 = ctab[:].rearrange("p (h f) -> p h f", h=4)
                    st3 = stab[:].rearrange("p (h f) -> p h f", h=4)
                    tm = [sm[:, 1024 + 256 * kk: 1280 + 256 * kk].rearrange("p (h f) -> p h f", h=4) for kk in range(3)]
                    ov = sslot.rearrange("p (h f) -> p h f", h=4)
                    nc.vector.tensor_tensor(tm[0], E, ct3, ALU.mult)
                    nc.vector.tensor_tensor(tm[1], O, st3, ALU.mult)
                    nc.vector.tensor_tensor(tm[2], O, ct3, ALU.mult)
                    nc.vector.tensor_tensor(ov[:, :, 0:64], tm[0], tm[1], ALU.subtract).then_inc(sem["dv"], 1)
                    nc.vector.tensor_tensor(tm[0], E, st3, ALU.mult)
                    nc.vector.tensor_tensor(ov[:, :, 64:128], tm[2], tm[0], ALU.add).then_inc(sem["dv"], 1)
                c.emit("vector", VPP)
                prev_post = v_pp
                tab_free[tt % 2] = ("dv", v_pp)

                dst = d["q1"] if kind == "q" else d["k1"]
                dcol = tsl(fg, 512) if kind == "q" else slice(0, 512)
                c.inc("st", 16)
                stq = c.cnt["st"]

                def SE2(sslot=sslot, dst=dst, dcol=dcol, tt=tt, v_pp=v_pp):
                    nc.sync.wait_ge(sem["dv"], v_pp)
                    nc.sync.dma_start(out=dst[tsl(tt, 128), dcol], in_=sslot).then_inc(sem["st"], 16)
                c.emit("sync", SE2)
                stage_free[tt % 2] = ("st", stq)
        prev_mm_fg = c.cnt["mm"]

    rps = ps["d"]
    prev_rv = 0
    for tt in range(NT):
        mr = c.inc("mm")

        def MRA(tt=tt, vw=v_w, prv=prev_rv):
            nc.tensor.wait_ge(sem["dv"], vw)
            if prv:
                nc.tensor.wait_ge(sem["dv"], prv)
            mmi = None
            for kc in range(KC):
                mmi = nc.tensor.matmul(rps[:, 0:8],
                                       afm[:, kc * T_LOC + tt * 128: kc * T_LOC + (tt + 1) * 128],
                                       whi[:, tsl(kc, 8)], start=(kc == 0), stop=False)
            for kc in range(KC):
                mmi = nc.tensor.matmul(rps[:, 0:8],
                                       afm[:, kc * T_LOC + tt * 128: kc * T_LOC + (tt + 1) * 128],
                                       wlo[:, tsl(kc, 8)], start=False, stop=(kc == KC - 1))
            mmi.then_inc(sem["mm"], 1)
        c.emit("tensor", MRA)

        rv = c.inc("dv")

        def VRA(mr=mr, tt=tt):
            nc.vector.wait_ge(sem["mm"], mr)
            nc.vector.tensor_copy(rout[:, tsl(tt, 8)], rps[:, 0:8]).then_inc(sem["dv"], 1)
        c.emit("vector", VRA)
        prev_rv = rv

    mm_afterA = c.cnt["mm"]
    for kc in range(KC):
        c.inc("tp", 16)

        def TLL(kc=kc, mm_afterA=mm_afterA, prev_rv=prev_rv):
            if kc == 0:
                nc.sync.wait_ge(sem["mm"], mm_afterA)
            nc.sync.dma_start_transpose(
                afm[:, tsl(kc, T_LOC)], d["xn1lo"][:, tsl(kc, 128)]).then_inc(sem["tp"], 16)
        c.emit("sync", TLL)
    tp_lo = c.cnt["tp"]

    prev_sm_st = 0
    prev_sm_ac = 0
    prev_ep = prev_rv
    for tt in range(NT):
        mr = c.inc("mm")

        def MRB(tt=tt, prv=prev_ep):
            nc.tensor.wait_ge(sem["tp"], tp_lo)
            if prv:
                nc.tensor.wait_ge(sem["dv"], prv)
            mmi = None
            for kc in range(KC):
                mmi = nc.tensor.matmul(rps[:, 0:8],
                                       afm[:, kc * T_LOC + tt * 128: kc * T_LOC + (tt + 1) * 128],
                                       whi[:, tsl(kc, 8)], start=(kc == 0), stop=(kc == KC - 1))
            mmi.then_inc(sem["mm"], 1)
        c.emit("tensor", MRB)

        v_ep = c.inc("dv", 2)
        a_ex = c.inc("ac")
        v_fin = c.inc("dv", 2)
        c.inc("st", 16)
        st_sm = c.cnt["st"]

        def VEP(mr=mr, tt=tt, pac=prev_sm_ac):
            nc.vector.wait_ge(sem["mm"], mr)
            if pac:
                nc.vector.wait_ge(sem["ac"], pac)
            nc.vector.tensor_tensor(sm8, rps[:, 0:8], rout[:, tsl(tt, 8)], ALU.add)
            nc.vector.tensor_copy(spacer2, spacer)
            nc.vector.tensor_tensor(sm8, sm8, gumb[:, tsl(tt, 8)], ALU.add)
            nc.vector.tensor_copy(spacer2, spacer)
            nc.vector.tensor_tensor(sm8, sm8, bbc, ALU.add)
            nc.vector.tensor_copy(spacer2, spacer)
            nc.vector.reduce_max(smx, sm8, axis=AX.X, negate=True).then_inc(sem["dv"], 2)
        c.emit("vector", VEP)

        def AEX(v_ep=v_ep):
            nc.scalar.wait_ge(sem["dv"], v_ep)
            nc.scalar.activation(qn32[:, 0:8], sm8, AF.Exp, bias=smx,
                                 accum_out=ssum).then_inc(sem["ac"], 1)
        c.emit("scalar", AEX)
        prev_sm_ac = a_ex

        def VFIN(a_ex=a_ex, tt=tt, pst=prev_sm_st):
            nc.vector.wait_ge(sem["ac"], a_ex)
            if pst:
                nc.vector.wait_ge(sem["st"], pst)
            nc.vector.reciprocal(sinv, ssum).then_inc(sem["dv"], 1)
            nc.vector.tensor_copy(spacer2, spacer)
            nc.vector.tensor_scalar(q32[:, 8:16], qn32[:, 0:8], sinv, None, ALU.mult).then_inc(sem["dv"], 1)
        c.emit("vector", VFIN)

        def SSM(tt=tt, v_fin=v_fin):
            nc.sync.wait_ge(sem["dv"], v_fin)
            nc.sync.dma_start(out=d["Sm"][tsl(tt, 128), :], in_=q32[:, 8:16]).then_inc(sem["st"], 16)
        c.emit("sync", SSM)
        prev_sm_st = st_sm
        prev_ep = v_ep

    st_total = c.cnt["st"]

    def FS():
        nc.sync.wait_ge(sem["st"], st_total)
    c.emit("sync", FS)
    c.barrier()


def emit_p7_attn1(c):
    nc, d, sem = c.nc, c.dram, c.sem
    sm = c.sb["sm32"]
    ps = c.ps
    wait_all = _mk_waiter(sem)
    spacer = c.const_tile[:, 63:64]
    spacer2 = c.const_tile[:, 62:63]
    c.lay("p7_ctx", 0, 32768, BF16)
    c.lay("p7_wo", 32768, 32768, BF16)
    c.lay("p7_qf", 65536, 4096, BF16)
    c.lay("p7_kf", 69632, 4096, BF16)
    c.lay("p7_vt", 73728, 4096, BF16)
    c.lay("p7_pb", 77824, 4096, BF16)
    c.lay("p7_pn", 81920, 4096, BF16)
    c.lay("p7_pT", 86016, 8192, BF16)
    c.lay("p7_s32", 81920, 2048, F32)
    c.lay("p7_msk", 90112, 256, F32)
    ctxb = c.buf("p7_ctx")
    wob = c.buf("p7_wo")
    qfb = c.buf("p7_qf")
    kfb = c.buf("p7_kf")
    vtb = c.buf("p7_vt")
    pbb = c.buf("p7_pb")
    pnb = c.buf("p7_pn")
    pTb = c.buf("p7_pT")
    s32b = c.buf("p7_s32")
    x1t = sm[:, 2048:4096]
    mskb = c.buf("p7_msk")

    bank_free = {}
    pb_free = [None, None]
    pn_free = [None, None]
    pT_free = None
    s4_free = None
    s32_free = [None, None]
    prev_sumread = 0
    wd_wo = c.inc("wd", 16)

    def GW():
        nc.gpsimd.dma_start(
            out=wob[:].rearrange("p (cc f) -> p cc f", cc=16),
            in_=d["wo"][:].rearrange("(cc p) f -> p cc f", p=128),
        ).then_inc(sem["wd"], 16)
    c.emit("gpsimd", GW)

    ld_msk = c.inc("ld", 16)

    def LMSK():
        nc.sync.dma_start(out=mskb[:, 0:128], in_=d["maskdiag"][:]).then_inc(sem["ld"], 16)
    c.emit("sync", LMSK)

    prev_head_mm = 0
    prev_grp_mm = 0
    for seq in range(B_LOC):
        for h in range(H0):
            g_id = h // 4
            sl = h % 2
            qf = qfb[:, tsl(sl, 2048)]
            tq = c.inc("tp", 16)

            def LDQ(seq=seq, h=h, qf=qf, guard_mm=prev_head_mm):
                if guard_mm:
                    nc.sync.wait_ge(sem["mm"], guard_mm)
                nc.sync.dma_start_transpose(
                    qf, d["q1"][tsl(seq, S), tsl(h, 128)]).then_inc(sem["tp"], 16)
            c.emit("sync", LDQ)

            if h % 4 == 0:
                gsl = g_id % 2
                kf = kfb[:, tsl(gsl, 2048)]
                vt = vtb[:, tsl(gsl, 2048)]
                tk = c.inc("tp", 16)
                lv = c.inc("ld", 16)

                def LDKV(seq=seq, g_id=g_id, kf=kf, vt=vt, guard_mm=prev_grp_mm):
                    if guard_mm:
                        nc.sync.wait_ge(sem["mm"], guard_mm)
                    nc.sync.dma_start_transpose(
                        kf, d["k1"][tsl(seq, S), tsl(g_id, 128)]).then_inc(sem["tp"], 16)
                    nc.sync.dma_start(
                        out=vt[:].rearrange("p (cc f) -> p cc f", cc=16),
                        in_=d["v1"][tsl(seq, S), tsl(g_id, 128)].rearrange("(cc p) f -> p cc f", p=128),
                    ).then_inc(sem["ld"], 16)
                c.emit("sync", LDKV)
                kv_sem = (tk, lv)
                prev_grp_mm_pending = c.cnt["mm"]
            tk, lv = kv_sem
            head_mm_before = c.cnt["mm"]

            for g in range(4):
                ncc = g + 1
                for q4 in range(4):
                    qt = g * 4 + q4
                    dof = q4 * 128
                    ms = c.inc("mm")

                    def MS(qf=qf, kf=kf, qt=qt, ncc=ncc, tq=tq, tk=tk, s4g=s4_free):
                        nc.tensor.wait_ge(sem["tp"], tq)
                        nc.tensor.wait_ge(sem["tp"], tk)
                        wait_all(nc.tensor, [s4g])
                        mmi = None
                        for cc in range(ncc):
                            mmi = nc.tensor.matmul(ps["s4"][:, tsl(cc, 512)],
                                                   qf[:, tsl(qt, 128)], kf[:, tsl(cc, 512)],
                                                   start=True, stop=True)
                        mmi.then_inc(sem["mm"], 1)
                    c.emit("tensor", MS)

                    vmk = c.inc("dv", 1)

                    def VMK(ms=ms, g=g, dof=dof):
                        nc.vector.wait_ge(sem["mm"], ms)
                        base = g * 512 + dof
                        nc.vector.tensor_tensor(ps["s4"][:, base:base + 128],
                                                ps["s4"][:, base:base + 128],
                                                mskb[:, 0:128], ALU.add)
                        if dof < 384:
                            nc.vector.memset(ps["s4"][:, base + 128:(g + 1) * 512], -1e30)
                        nc.vector.tensor_copy(spacer2, spacer).then_inc(sem["dv"], 1)
                    c.emit("vector", VMK)

                    width = ncc * 512
                    nmx = sm[:, 0:1]
                    sums = sm[:, 1:2]
                    inv = sm[:, 2:3]
                    s32 = s32b[:]
                    aevs = c.inc("ac")

                    def AEVS(vmk=vmk, s32=s32, width=width, gfree=s32_free[0]):
                        nc.scalar.wait_ge(sem["dv"], vmk)
                        wait_all(nc.scalar, [gfree])
                        nc.scalar.copy(s32[:, 0:width], ps["s4"][:, 0:width]).then_inc(sem["ac"], 1)
                    c.emit("scalar", AEVS)
                    s4_free = ("ac", aevs)

                    vmx = c.inc("dv")

                    def VMX(aevs=aevs, nmx=nmx, s32=s32, width=width):
                        nc.vector.wait_ge(sem["ac"], aevs)
                        nc.vector.reduce_max(nmx, s32[:, 0:width], axis=AX.X,
                                             negate=True).then_inc(sem["dv"], 1)
                    c.emit("vector", VMX)

                    pb = pbb[:, tsl(qt % 2, 2048)]
                    aex = c.inc("ac")

                    def AEXP(vmx=vmx, pb=pb, nmx=nmx, sums=sums, width=width, s32=s32,
                             gpb=pb_free[qt % 2], psv=prev_sumread):
                        nc.scalar.wait_ge(sem["dv"], vmx)
                        wait_all(nc.scalar, [gpb])
                        if psv:
                            nc.scalar.wait_ge(sem["dv"], psv)
                        nc.scalar.activation(pb[:, 0:width], s32[:, 0:width], AF.Exp,
                                             bias=nmx, accum_out=sums).then_inc(sem["ac"], 1)
                    c.emit("scalar", AEXP)
                    s32_free[0] = ("dv", vmx)

                    vsc = c.inc("dv", 2)

                    def VSCL(aex=aex, pb=pb, inv=inv, sums=sums, width=width):
                        nc.vector.wait_ge(sem["ac"], aex)
                        nc.vector.reciprocal(inv, sums).then_inc(sem["dv"], 1)
                        nc.vector.tensor_copy(spacer2, spacer)
                        nc.vector.tensor_scalar(pb[:, 0:width], pb[:, 0:width], inv,
                                                None, ALU.mult).then_inc(sem["dv"], 1)
                    c.emit("vector", VSCL)
                    prev_sumread = vsc

                    tpq = c.inc("tp", 16)

                    def TPQ(vsc=vsc, pb=pb, q4=q4, ncc=ncc, gpT=pT_free):
                        nc.sync.wait_ge(sem["dv"], vsc)
                        wait_all(nc.sync, [gpT])
                        nc.sync.dma_start_transpose(
                            pTb[:].rearrange("p (cc g2) -> p cc g2", cc=16)[:, 0:4 * ncc, tsl(q4, 128)],
                            pb.rearrange("p (cc f) -> p cc f", cc=16)[:, 0:4 * ncc, :],
                        ).then_inc(sem["tp"], 16)
                    c.emit("sync", TPQ)
                    pb_free[qt % 2] = ("tp", tpq)
                    tp_group_done = c.cnt["tp"]

                cps = ps["a"] if g % 2 == 0 else ps["b"]
                bkey = "a" if g % 2 == 0 else "b"
                mc = c.inc("mm")

                def MC(cps=cps, vt=vt, lv=lv, ncc=ncc, tp_need=tp_group_done,
                       gbank=bank_free.get(bkey)):
                    nc.tensor.wait_ge(sem["tp"], tp_need)
                    nc.tensor.wait_ge(sem["ld"], lv)
                    wait_all(nc.tensor, [gbank])
                    mmi = None
                    for cc in range(4 * ncc):
                        mmi = nc.tensor.matmul(cps[:, 0:512], vt[:, tsl(cc, 128)],
                                               pTb[:, tsl(cc, 512)],
                                               start=(cc == 0), stop=(cc == 4 * ncc - 1))
                    mmi.then_inc(sem["mm"], 1)
                c.emit("tensor", MC)
                pT_free = ("mm", mc)

                ce = c.inc("ac")

                def CEV(mc=mc, cps=cps, h=h, g=g):
                    nc.scalar.wait_ge(sem["mm"], mc)
                    nc.scalar.copy(ctxb[:, h * 2048 + g * 512: h * 2048 + (g + 1) * 512],
                                   cps[:, 0:512]).then_inc(sem["ac"], 1)
                c.emit("scalar", CEV)
                bank_free[bkey] = ("ac", ce)
            prev_head_mm = head_mm_before
            if h % 4 == 3:
                prev_grp_mm = prev_grp_mm_pending

        cev_all = c.cnt["ac"]
        for qt in range(NTS):
            i_glob = seq * NTS + qt
            lxb = c.inc("ld", 16)

            def LXB(i_glob=i_glob):
                nc.sync.dma_start(out=x1t[:, 0:2048],
                                  in_=d["x1"][tsl(i_glob, 128), :]).then_inc(sem["ld"], 16)
            c.emit("sync", LXB)

            pbanks = {0: ("a", ps["a"]), 1: ("b", ps["b"]), 2: ("cc", ps["c"]), 3: ("dd", ps["d"])}
            for og in range(4):
                bkey, bank = pbanks[og]
                mp = c.inc("mm")

                def MP(qt=qt, og=og, bank=bank, cev_all=cev_all, wd_wo=wd_wo,
                       gbank=bank_free.get(bkey)):
                    nc.tensor.wait_ge(sem["ac"], cev_all)
                    nc.tensor.wait_ge(sem["wd"], wd_wo)
                    wait_all(nc.tensor, [gbank])
                    mmi = None
                    for hh in range(16):
                        mmi = nc.tensor.matmul(
                            bank[:, 0:512],
                            ctxb[:, hh * 2048 + qt * 128: hh * 2048 + (qt + 1) * 128],
                            wob[:, hh * 2048 + og * 512: hh * 2048 + (og + 1) * 512],
                            start=(hh == 0), stop=(hh == 15))
                    mmi.then_inc(sem["mm"], 1)
                c.emit("tensor", MP)

                vp = c.inc("dv")

                def VPR(mp=mp, og=og, bank=bank, lxb=lxb):
                    nc.vector.wait_ge(sem["mm"], mp)
                    nc.vector.wait_ge(sem["ld"], lxb)
                    nc.vector.tensor_tensor(x1t[:, tsl(og, 512)], bank[:, 0:512],
                                            x1t[:, tsl(og, 512)], ALU.add).then_inc(sem["dv"], 1)
                c.emit("vector", VPR)
                bank_free[bkey] = ("dv", vp)

            vlast = c.cnt["dv"]
            c.inc("st", 16)

            def SX(i_glob=i_glob, vlast=vlast):
                nc.sync.wait_ge(sem["dv"], vlast)
                nc.sync.dma_start(out=d["partial"][tsl(i_glob, 128), :],
                                  in_=x1t[:, 0:2048]).then_inc(sem["st"], 16)
            c.emit("sync", SX)
        prev_head_mm = max(prev_head_mm, c.cnt["mm"])
        prev_grp_mm = max(prev_grp_mm, c.cnt["mm"])

    st_total = c.cnt["st"]

    def FS():
        nc.sync.wait_ge(sem["st"], st_total)
    c.emit("sync", FS)
    c.barrier()


def emit_p8_moe(c):
    nc, d, sem = c.nc, c.dram, c.sem
    sm = c.sb["sm32"]
    ps = c.ps
    wait_all = _mk_waiter(sem)
    c.lay("p8_afm", 0, KC * T_LOC, BF16)
    c.lay("p8_wg", 65536, 4096, BF16)
    c.lay("p8_wu", 69632, 4096, BF16)
    c.lay("p8_bg", 73728, 128, BF16)
    c.lay("p8_bu", 73856, 128, BF16)
    c.lay("p8_on", 73984, 512, BF16)
    c.lay("p8_hst", 74496, 4096, BF16)
    c.lay("p8_f32", 78592, 1024, F32)
    afm = c.buf("p8_afm")
    wgb = c.buf("p8_wg")
    wub = c.buf("p8_wu")
    bgb = c.buf("p8_bg")
    bub = c.buf("p8_bu")
    onb = c.buf("p8_on")
    hst = c.buf("p8_hst")
    f32s = c.buf("p8_f32")
    sst = sm[:, 1024:1280].bitcast(BF16)

    for kc in range(KC):
        c.inc("tp", 16)

        def TLX(kc=kc):
            nc.sync.dma_start_transpose(
                afm[:, tsl(kc, T_LOC)], d["xn1b"][:, tsl(kc, 128)]).then_inc(sem["tp"], 16)
        c.emit("sync", TLX)
    tp_x = c.cnt["tp"]

    ld_on = c.inc("ld", 16)

    def LON():
        nc.sync.dma_start(out=onb[0:1, :], in_=d["ones512"][:]).then_inc(sem["ld"], 16)
    c.emit("sync", LON)

    bank_free = {}
    prev_mm_fc = 0
    prev_st_hh = 0
    prev_dv_sil = 0
    for fc in range(NFC):
        w = 128 if fc < NFC - 1 else LASTF
        g_sl = c.inc("wd", 64)

        def GSL(fc=fc, w=w, pmm=prev_mm_fc):
            if pmm:
                nc.gpsimd.wait_ge(sem["mm"], pmm)
            nc.gpsimd.dma_start(
                out=wgb[:, 0:16 * w].rearrange("p (cc f) -> p cc f", cc=16),
                in_=d["w_gate"][:, fc * 128: fc * 128 + w].rearrange("(cc p) f -> p cc f", p=128),
            ).then_inc(sem["wd"], 16)
            nc.gpsimd.dma_start(
                out=wub[:, 0:16 * w].rearrange("p (cc f) -> p cc f", cc=16),
                in_=d["w_up"][:, fc * 128: fc * 128 + w].rearrange("(cc p) f -> p cc f", p=128),
            ).then_inc(sem["wd"], 16)
            nc.gpsimd.dma_start(out=bgb[0:1, 0:w],
                                in_=d["b_gate"][0:1, fc * 128: fc * 128 + w]).then_inc(sem["wd"], 16)
            nc.gpsimd.dma_start(out=bub[0:1, 0:w],
                                in_=d["b_up"][0:1, fc * 128: fc * 128 + w]).then_inc(sem["wd"], 16)
        c.emit("gpsimd", GSL)

        for tg in range(8):
            pg = ps["a"] if tg % 2 == 0 else ps["c"]
            pu = ps["b"] if tg % 2 == 0 else ps["d"]
            kg = "a" if tg % 2 == 0 else "cc"
            ku = "b" if tg % 2 == 0 else "dd"
            mg = c.inc("mm", 2)

            def MG(tg=tg, w=w, pg=pg, pu=pu, g_sl=g_sl,
                   ga=bank_free.get(kg), gb_=bank_free.get(ku)):
                nc.tensor.wait_ge(sem["wd"], g_sl)
                nc.tensor.wait_ge(sem["tp"], tp_x)
                nc.tensor.wait_ge(sem["ld"], ld_on)
                wait_all(nc.tensor, [ga, gb_])
                for hc in range(KC):
                    nc.tensor.matmul(pg[0:w, 0:512], wgb[:, hc * w: (hc + 1) * w],
                                     afm[:, hc * T_LOC + tg * 512: hc * T_LOC + (tg + 1) * 512],
                                     start=(hc == 0), stop=False)
                nc.tensor.matmul(pg[0:w, 0:512], bgb[0:1, 0:w], onb[0:1, 0:512],
                                 start=False, stop=True).then_inc(sem["mm"], 1)
                for hc in range(KC):
                    nc.tensor.matmul(pu[0:w, 0:512], wub[:, hc * w: (hc + 1) * w],
                                     afm[:, hc * T_LOC + tg * 512: hc * T_LOC + (tg + 1) * 512],
                                     start=(hc == 0), stop=False)
                nc.tensor.matmul(pu[0:w, 0:512], bub[0:1, 0:w], onb[0:1, 0:512],
                                 start=False, stop=True).then_inc(sem["mm"], 1)
            c.emit("tensor", MG)

            asi = c.inc("ac")

            def ASI(mg=mg, pg=pg, w=w, pdv=prev_dv_sil):
                nc.scalar.wait_ge(sem["mm"], mg - 1)
                if pdv:
                    nc.scalar.wait_ge(sem["dv"], pdv)
                nc.scalar.activation(sst[0:w, 0:512], pg[0:w, 0:512], AF.Silu).then_inc(sem["ac"], 1)
            c.emit("scalar", ASI)
            bank_free[kg] = ("ac", asi)

            vmu = c.inc("dv")

            def VMU(mg=mg, asi=asi, pu=pu, tg=tg, w=w, pst=prev_st_hh):
                nc.vector.wait_ge(sem["mm"], mg)
                nc.vector.wait_ge(sem["ac"], asi)
                if pst:
                    nc.vector.wait_ge(sem["st"], pst)
                nc.vector.tensor_tensor(hst[0:w, tsl(tg, 512)], sst[0:w, 0:512],
                                        pu[0:w, 0:512], ALU.mult).then_inc(sem["dv"], 1)
            c.emit("vector", VMU)
            bank_free[ku] = ("dv", vmu)
            prev_dv_sil = vmu

        sth = c.inc("st", 16)
        vlast = c.cnt["dv"]

        def SHH(fc=fc, w=w, vlast=vlast):
            nc.sync.wait_ge(sem["dv"], vlast)
            nc.sync.dma_start(out=d["hh1"][fc * 128: fc * 128 + w, :],
                              in_=hst[0:w, :]).then_inc(sem["st"], 16)
        c.emit("sync", SHH)
        prev_st_hh = sth
        prev_mm_fc = c.cnt["mm"]

    st8a_total = c.cnt["st"]

    def FS8A():
        nc.sync.wait_ge(sem["st"], st8a_total)
    c.emit("sync", FS8A)
    c.barrier()

    c.lay("p8_hht", 0, 45056, BF16)
    c.lay("p8_wds", 45056, 22528, BF16)
    c.lay("p8_bd", 67584, 512, BF16)
    hht = c.buf("p8_hht")
    wds = c.buf("p8_wds")
    bdb = c.buf("p8_bd")
    bank_free = {}
    stage_free = [None, None]
    prev_mm_tb = 0
    for tb in range(4):
        ld_h = c.inc("ld", 32)

        def LHH(tb=tb, pmm=prev_mm_tb):
            if pmm:
                nc.sync.wait_ge(sem["mm"], pmm)
            nc.sync.dma_start(
                out=hht[:, 0:43008].rearrange("p (fc t) -> p fc t", fc=42),
                in_=d["hh1"][0:5376, tsl(tb, 1024)].rearrange("(fc p) t -> p fc t", p=128),
            ).then_inc(sem["ld"], 16)
            nc.sync.dma_start(
                out=hht[0:LASTF, 43008:44032],
                in_=d["hh1"][5376:5461, tsl(tb, 1024)],
            ).then_inc(sem["ld"], 16)
        c.emit("sync", LHH)

        for hg in range(4):
            mm_before_hg = c.cnt["mm"]
            ld_w = c.inc("wd", 48)

            def LWD(hg=hg, pmm=mm_before_hg, first=(tb == 0 and hg == 0)):
                if not first:
                    nc.gpsimd.wait_ge(sem["mm"], pmm)
                nc.gpsimd.dma_start(
                    out=wds[:, 0:21504].rearrange("p (fc h) -> p fc h", fc=42),
                    in_=d["w_down"][0:5376, tsl(hg, 512)].rearrange("(fc p) h -> p fc h", p=128),
                ).then_inc(sem["wd"], 16)
                nc.gpsimd.dma_start(
                    out=wds[0:LASTF, 21504:22016],
                    in_=d["w_down"][5376:5461, tsl(hg, 512)],
                ).then_inc(sem["wd"], 16)
                nc.gpsimd.dma_start(out=bdb[0:1, :],
                                    in_=d["b_down"][0:1, tsl(hg, 512)]).then_inc(sem["wd"], 16)
            c.emit("gpsimd", LWD)

            for tc in range(8):
                i_glob = tb * 8 + tc
                pd = ps["a"] if tc % 2 == 0 else ps["b"]
                kd = "a" if tc % 2 == 0 else "b"
                md = c.inc("mm")

                def MD(tc=tc, pd=pd, ld_w=ld_w, ld_h=ld_h, gk_=bank_free.get(kd)):
                    nc.tensor.wait_ge(sem["wd"], ld_w)
                    nc.tensor.wait_ge(sem["ld"], ld_h)
                    wait_all(nc.tensor, [gk_])
                    for fc in range(42):
                        nc.tensor.matmul(
                            pd[:, 0:512],
                            hht[:, fc * 1024 + tc * 128: fc * 1024 + (tc + 1) * 128],
                            wds[:, tsl(fc, 512)],
                            start=(fc == 0), stop=False)
                    nc.tensor.matmul(
                        pd[:, 0:512],
                        hht[0:LASTF, 43008 + tc * 128: 43008 + (tc + 1) * 128],
                        wds[0:LASTF, 21504:22016],
                        start=False, stop=False)
                    nc.tensor.matmul(pd[:, 0:512], onb[0:1, tc * 16: tc * 16 + 128],
                                     bdb[0:1, 0:512], start=False, stop=True).then_inc(sem["mm"], 1)
                c.emit("tensor", MD)

                aev = c.inc("ac")
                stg_ = f32s[:, tsl(tc % 2, 512)]

                def AEVD(md=md, pd=pd, stg_=stg_, gs=stage_free[tc % 2]):
                    nc.scalar.wait_ge(sem["mm"], md)
                    wait_all(nc.scalar, [gs])
                    nc.scalar.copy(stg_, pd[:, 0:512]).then_inc(sem["ac"], 1)
                c.emit("scalar", AEVD)
                bank_free[kd] = ("ac", aev)

                std = c.inc("st", 16)

                def SD(i_glob=i_glob, hg=hg, aev=aev, stg_=stg_):
                    nc.sync.wait_ge(sem["ac"], aev)
                    nc.sync.dma_start(out=d["f1"][tsl(i_glob, 128), tsl(hg, 512)],
                                      in_=stg_).then_inc(sem["st"], 16)
                c.emit("sync", SD)
                stage_free[tc % 2] = ("st", std)
        prev_mm_tb = c.cnt["mm"]

    st_total = c.cnt["st"]

    def FS8():
        nc.sync.wait_ge(sem["st"], st_total)
    c.emit("sync", FS8)
    c.barrier()


def build_launch_A(dev_out=(), phases_sel=("p1", "p2")):
    nc = bass.Bass("TRN2", target_bir_lowering=False, debug=False, detect_race_conditions=False)
    stack = contextlib.ExitStack()
    c = Rec(nc, stack)

    def din(name, shape, dt):
        c.dram[name] = nc.dram_tensor(name, list(shape), dt, kind="ExternalInput").ap()

    def dout(name, shape, dt):
        c.dram[name] = nc.dram_tensor(name, list(shape), dt, kind="ExternalOutput").ap()

    def dint(name, shape, dt):
        kind = "ExternalOutput" if name in dev_out else "Internal"
        c.dram[name] = nc.dram_tensor(name, list(shape), dt, kind=kind).ap()

    din("fconst", (128, 64), F32)
    din("x0", (T_LOC, D), F32)
    dint("xb", (T_LOC, D), F32)
    dint("xn0b", (T_LOC, D), BF16)
    dint("zb", (T_LOC, D), BF16)
    if "p3" in phases_sel:
        din("W_out", (D, D), F32)
        dint("x1a", (T_LOC, D), F32)
    if "p4" in phases_sel:
        din("gate_w", (I0, D), I32)
        din("up_w", (I0, D), I32)
        din("down_w", (D, I0), I32)
        din("gate_scale_cols", (128, 64), F32)
        din("up_scale_cols", (128, 64), F32)
        din("down_scale_cols", (128, 16), F32)
        dint("wdt", (I0, D), BF16)
        dint("hh", (I0, T_LOC), BF16)
        dint("ffn0b", (T_LOC, D), BF16)
    if "p5" in phases_sel:
        dint("x1", (T_LOC, D), F32)
        dint("xn1b", (T_LOC, D), BF16)
        dint("xn1lo", (T_LOC, D), BF16)
    if "p6" in phases_sel:
        din("wq", (D, D), F32)
        din("wk", (D, 512), F32)
        din("wv", (D, 512), F32)
        din("gq", (1, D), F32)
        din("gk", (1, 512), F32)
        din("router_w", (D, 8), F32)
        din("router_b", (1, 8), F32)
        din("gumb", (T_LOC, 8), F32)
        din("cos1q", (S, 1024), F32)
        din("sin1q", (S, 1024), F32)
        din("cos1k", (S, 256), F32)
        din("sin1k", (S, 256), F32)
        dint("q1", (T_LOC, D), BF16)
        dint("k1", (T_LOC, 512), BF16)
        dint("v1", (T_LOC, 512), BF16)
        dout("Sm", (T_LOC, 8), F32)
    if "p7" in phases_sel:
        din("wo", (D, D), F32)
        din("maskdiag", (128, 128), F32)
        dout("partial", (T_LOC, D), F32)
    if "p8" in phases_sel:
        din("w_gate", (D, DFF1), F32)
        din("w_up", (D, DFF1), F32)
        din("w_down", (DFF1, D), F32)
        din("b_gate", (1, DFF1), F32)
        din("b_up", (1, DFF1), F32)
        din("b_down", (1, D), F32)
        din("ones512", (1, 512), BF16)
        dint("hh1", (DFF1, T_LOC), BF16)
        dout("f1", (T_LOC, D), F32)
    if "p2" in phases_sel:
        din("W_in", (D, 3 * D), F32)
        din("ln_w_cols", (128, 16), F32)
        din("ln_b_cols", (128, 16), F32)
        din("cos0", (S, 512), F32)
        din("sin0", (S, 512), F32)
        din("ones1", (1, 128), BF16)
        dint("q0", (T_LOC, D), BF16)
        dint("k0", (T_LOC, D), BF16)
        dint("v0", (T_LOC, D), BF16)

    with stack:
        c.sb["arena"] = stack.enter_context(nc.sbuf_tensor("arena", [128, 94208], BF16))
        c.sb["fconst"] = stack.enter_context(nc.sbuf_tensor("fconst_sb", [128, 64], F32))
        c.const_tile = c.sb["fconst"]
        c.sb["sm32"] = stack.enter_context(nc.sbuf_tensor("sm32", [128, 4096], F32))
        c.ps["s4"] = stack.enter_context(nc.psum_tensor("ps_s4", [128, 2048], F32))
        c.ps["a"] = stack.enter_context(nc.psum_tensor("ps_a", [128, 512], F32))
        c.ps["b"] = stack.enter_context(nc.psum_tensor("ps_b", [128, 512], F32))
        c.ps["c"] = stack.enter_context(nc.psum_tensor("ps_c", [128, 512], F32))
        c.ps["d"] = stack.enter_context(nc.psum_tensor("ps_d", [128, 512], F32))
        for s_ in ["ld", "st", "mm", "dv", "ac", "gp", "tp", "ph", "wd"]:
            c.new_sem(s_)
        cld = c.inc("ld", 16)

        def _ldc():
            nc.sync.dma_start(out=c.const_tile[:], in_=c.dram["fconst"][:]).then_inc(c.sem["ld"], 16)
        c.emit("sync", _ldc)
        for en in ("scalar", "vector", "gpsimd", "tensor"):
            def _w(en=en):
                getattr(nc, en).wait_ge(c.sem["ld"], cld)
            c.emit(en, _w)

        if "p1" in phases_sel:
            emit_p1_ln0(c)
        if "p2" in phases_sel:
            emit_p2_qkv0(c)
        if "p3" in phases_sel:
            emit_p3_attn0(c)
        if "p4" in phases_sel:
            emit_p4_ffn0(c)
        if "p5" in phases_sel:
            emit_p5_ln1(c)
        if "p6" in phases_sel:
            emit_p6_qkv1(c)
        if "p7" in phases_sel:
            emit_p7_attn1(c)
        if "p8" in phases_sel:
            emit_p8_moe(c)
        c.run_block()
    return nc, c


def build_launch_B():
    nc = bass.Bass("TRN2", target_bir_lowering=False, debug=False, detect_race_conditions=False)
    stack = contextlib.ExitStack()
    c = Rec(nc, stack)
    d = c.dram

    def din(name, shape, dt):
        d[name] = nc.dram_tensor(name, list(shape), dt, kind="ExternalInput").ap()

    din("SmL", (128, 2048 * 8 // 8), F32)
    din("Smloc", (T_LOC, 8), F32)
    din("partial", (T_LOC, D), F32)
    din("f1", (T_LOC, D), F32)
    din("Emat", (128, 8), F32)
    din("E2mat", (8, 128), F32)
    din("I8", (8, 8), F32)
    din("ones8_128", (8, 128), F32)
    din("fconst", (128, 64), F32)
    d["y"] = nc.dram_tensor("y", [T_LOC, D], F32, kind="ExternalOutput").ap()

    NIT = 48
    with stack:
        c.sb["a"] = stack.enter_context(nc.sbuf_tensor("ba", [128, 14336], F32))
        c.sb["fc"] = stack.enter_context(nc.sbuf_tensor("bfc", [128, 64], F32))
        c.const_tile = c.sb["fc"]
        c.ps["a"] = stack.enter_context(nc.psum_tensor("bpa", [128, 512], F32))
        c.ps["b"] = stack.enter_context(nc.psum_tensor("bpb", [128, 512], F32))
        for s_ in ["ld", "st", "mm", "dv", "ac", "gp", "ph"]:
            c.new_sem(s_)
        A = c.sb["a"]
        sem = c.sem
        ps = c.ps
        SmL = A[:, 0:2048]
        cmpb = A[:, 2048:4096]
        EmatB = A[:, 4096:4104]
        E2B = A[:, 4104:4232]
        I8B = A[:, 4232:4240]
        O81 = A[:, 4240:4368]
        lo8 = A[:, 4368:4369]
        hi8 = A[:, 4369:4370]
        mid8 = A[:, 4370:4371]
        cnt8 = A[:, 4371:4372]
        pred8 = A[:, 4372:4373]
        dd8 = A[:, 4373:4374]
        ee8 = A[:, 4374:4375]
        midb = A[:, 4375:4376]
        cntp = A[:, 4376:4377]
        D8 = A[:, 4377:4385]
        taub = A[:, 4385:4393]
        wcol = A[:, 4393:4394]
        mtile = A[:, 4400:4408]
        smt = [A[:, 4416:4424], A[:, 4424:4432]]
        prt = [A[:, 4608:6656], A[:, 6656:8704]]
        f1t = [A[:, 8704:10752], A[:, 10752:12800]]
        yst = [A[:, 12800:13312], A[:, 13312:13824]]
        spacer = c.const_tile[:, 63:64]
        spacer2 = c.const_tile[:, 62:63]

        ld0 = c.inc("ld", 16 * 6)

        def L0():
            nc.sync.dma_start(out=c.const_tile[:], in_=d["fconst"][:]).then_inc(sem["ld"], 16)
            nc.sync.dma_start(out=SmL, in_=d["SmL"][:]).then_inc(sem["ld"], 16)
            nc.sync.dma_start(out=EmatB, in_=d["Emat"][:]).then_inc(sem["ld"], 16)
            nc.sync.dma_start(out=E2B[0:8, :], in_=d["E2mat"][:]).then_inc(sem["ld"], 16)
            nc.sync.dma_start(out=I8B[0:8, :], in_=d["I8"][:]).then_inc(sem["ld"], 16)
            nc.sync.dma_start(out=O81[0:8, :], in_=d["ones8_128"][:]).then_inc(sem["ld"], 16)
        c.emit("sync", L0)

        g0 = c.inc("gp", 1)

        def G0():
            nc.gpsimd.wait_ge(sem["ld"], ld0)
            nc.gpsimd.memset(lo8, 0.0)
            nc.gpsimd.memset(hi8, 1.001).then_inc(sem["gp"], 1)
        c.emit("gpsimd", G0)

        prev_dv = 0
        prev_ac = 0
        gprev = g0
        for it in range(NIT):
            bank = ps["a"] if it % 2 == 0 else ps["b"]
            g1 = c.inc("gp")

            def G1(gprev=gprev, pac=prev_ac):
                nc.gpsimd.wait_ge(sem["gp"], gprev)
                if pac:
                    nc.gpsimd.wait_ge(sem["ac"], pac)
                nc.gpsimd.tensor_tensor(mid8, lo8, hi8, ALU.add)
                nc.gpsimd.tensor_scalar(mid8, mid8, 0.5, None, ALU.mult).then_inc(sem["gp"], 1)
            c.emit("gpsimd", G1)

            m1 = c.inc("mm")

            def M1(g1=g1, bank=bank, pdv=prev_dv):
                nc.tensor.wait_ge(sem["gp"], g1)
                if pdv:
                    nc.tensor.wait_ge(sem["dv"], pdv)
                nc.tensor.matmul(bank[:, 0:1], E2B[0:8, :], mid8[0:8, :],
                                 start=True, stop=True).then_inc(sem["mm"], 1)
            c.emit("tensor", M1)

            a1 = c.inc("ac")

            def A1(m1=m1, bank=bank):
                nc.scalar.wait_ge(sem["mm"], m1)
                nc.scalar.copy(midb, bank[:, 0:1]).then_inc(sem["ac"], 1)
            c.emit("scalar", A1)

            v1 = c.inc("dv")

            def V1(a1=a1):
                nc.vector.wait_ge(sem["ac"], a1)
                nc.vector.tensor_scalar(cmpb, SmL, midb, None, ALU.is_ge)
                nc.vector.tensor_copy(spacer2, spacer)
                nc.vector.reduce_sum(cntp, cmpb, axis=AX.X).then_inc(sem["dv"], 1)
            c.emit("vector", V1)
            prev_dv = v1

            m2 = c.inc("mm")

            def M2(v1=v1, bank=bank):
                nc.tensor.wait_ge(sem["dv"], v1)
                nc.tensor.matmul(bank[0:8, 4:5], EmatB, cntp,
                                 start=True, stop=True).then_inc(sem["mm"], 1)
            c.emit("tensor", M2)

            a2 = c.inc("ac")

            def A2(m2=m2, bank=bank):
                nc.scalar.wait_ge(sem["mm"], m2)
                nc.scalar.copy(cnt8[0:8, :], bank[0:8, 4:5]).then_inc(sem["ac"], 1)
            c.emit("scalar", A2)
            prev_ac = a2

            g2 = c.inc("gp")

            def G2(a2=a2):
                nc.gpsimd.wait_ge(sem["ac"], a2)
                nc.gpsimd.tensor_scalar(pred8, cnt8, 4096.0, None, ALU.is_ge)
                nc.gpsimd.tensor_tensor(dd8, mid8, lo8, ALU.subtract)
                nc.gpsimd.tensor_tensor(dd8, dd8, pred8, ALU.mult)
                nc.gpsimd.tensor_tensor(lo8, lo8, dd8, ALU.add)
                nc.gpsimd.tensor_tensor(ee8, mid8, hi8, ALU.subtract)
                nc.gpsimd.tensor_tensor(hi8, hi8, ee8, ALU.add)
                nc.gpsimd.tensor_tensor(ee8, ee8, pred8, ALU.mult)
                nc.gpsimd.tensor_tensor(hi8, hi8, ee8, ALU.subtract).then_inc(sem["gp"], 1)
            c.emit("gpsimd", G2)
            gprev = g2

        g3 = c.inc("gp")

        def G3(gprev=gprev):
            nc.gpsimd.wait_ge(sem["gp"], gprev)
            nc.gpsimd.tensor_scalar(D8[0:8, :], I8B[0:8, :], lo8[0:8, :], None, ALU.mult).then_inc(sem["gp"], 1)
        c.emit("gpsimd", G3)

        m3 = c.inc("mm")

        def M3(g3=g3):
            nc.tensor.wait_ge(sem["gp"], g3)
            nc.tensor.matmul(ps["a"][:, 8:16], O81[0:8, :], D8[0:8, :],
                             start=True, stop=True).then_inc(sem["mm"], 1)
        c.emit("tensor", M3)

        a3 = c.inc("ac")

        def A3(m3=m3):
            nc.scalar.wait_ge(sem["mm"], m3)
            nc.scalar.copy(taub, ps["a"][:, 8:16]).then_inc(sem["ac"], 1)
        c.emit("scalar", A3)

        st_done = {}
        dv_done = {}
        for i in range(NT):
            sl = i % 2
            ld_i = c.inc("ld", 48)

            def LD(i=i, sl=sl, pst=st_done.get(i - 2, 0)):
                if pst:
                    nc.sync.wait_ge(sem["st"], pst)
                nc.sync.dma_start(out=smt[sl], in_=d["Smloc"][tsl(i, 128), :]).then_inc(sem["ld"], 16)
                nc.sync.dma_start(out=prt[sl], in_=d["partial"][tsl(i, 128), :]).then_inc(sem["ld"], 16)
                nc.sync.dma_start(out=f1t[sl], in_=d["f1"][tsl(i, 128), :]).then_inc(sem["ld"], 16)
            c.emit("sync", LD)

            v2 = c.inc("dv", 5)

            def V2(i=i, sl=sl, ld_i=ld_i, a3=a3, pdv=dv_done.get(i - 2, 0)):
                nc.vector.wait_ge(sem["ld"], ld_i)
                nc.vector.wait_ge(sem["ac"], a3)
                if pdv:
                    nc.vector.wait_ge(sem["dv"], pdv)
                nc.vector.tensor_tensor(mtile, smt[sl], taub, ALU.is_ge)
                nc.vector.tensor_copy(spacer2, spacer)
                nc.vector.tensor_tensor(mtile, mtile, smt[sl], ALU.mult)
                nc.vector.tensor_copy(spacer2, spacer)
                nc.vector.reduce_sum(wcol, mtile, axis=AX.X)
                nc.vector.tensor_copy(spacer2, spacer).then_inc(sem["dv"], 1)
                for og in range(4):
                    nc.vector.scalar_tensor_tensor(
                        yst[sl] if False else prt[sl][:, tsl(og, 512)],
                        f1t[sl][:, tsl(og, 512)], wcol, prt[sl][:, tsl(og, 512)],
                        ALU.mult, ALU.add).then_inc(sem["dv"], 1)
            c.emit("vector", V2)
            dv_done[i] = c.cnt["dv"]

            c.inc("st", 16)

            def SY(i=i, sl=sl, v2=v2):
                nc.sync.wait_ge(sem["dv"], v2)
                nc.sync.dma_start(out=d["y"][tsl(i, 128), :], in_=prt[sl]).then_inc(sem["st"], 16)
            c.emit("sync", SY)
            st_done[i] = c.cnt["st"]

        st_total = c.cnt["st"]

        def FS():
            nc.sync.wait_ge(sem["st"], st_total)
        c.emit("sync", FS)
        c.run_block()
    return nc, c


PHASES_ALL = ("p1", "p2", "p3", "p4", "p5", "p6", "p7", "p8")


def make_inmap_A(inputs, C, fc, core):
    d = inputs
    x0 = d["L0_attn_X"].reshape(8, T_LOC, D)
    gum = C["gumbel"].reshape(8, T_LOC, 8)
    base = {
        "fconst": fc,
        "x0": np.ascontiguousarray(x0[core]),
        "gumb": np.ascontiguousarray(gum[core]),
        "W_in": d["L0_attn_W_in"],
        "W_out": d["L0_attn_W_out"],
        "ln_w_cols": np.ascontiguousarray(np.asarray(d["L0_attn_ln_w"]).reshape(16, 128).T),
        "ln_b_cols": np.ascontiguousarray(np.asarray(d["L0_attn_ln_b"]).reshape(16, 128).T),
        "gate_w": d["L0_ffn_gate_w"], "up_w": d["L0_ffn_up_w"], "down_w": d["L0_ffn_down_w"],
        "gate_scale_cols": np.ascontiguousarray(np.asarray(d["L0_ffn_gate_scale"]).reshape(64, 128).T),
        "up_scale_cols": np.ascontiguousarray(np.asarray(d["L0_ffn_up_scale"]).reshape(64, 128).T),
        "down_scale_cols": np.ascontiguousarray(np.asarray(d["L0_ffn_down_scale"]).reshape(16, 128).T),
        "cos0": C["cos0"], "sin0": C["sin0"], "ones1": C["ones1"], "ones512": C["ones512"],
        "wq": d["L1_attn_wq"], "wk": d["L1_attn_wk"], "wv": d["L1_attn_wv"],
        "gq": np.asarray(d["L1_attn_gq"]).reshape(1, D),
        "gk": np.asarray(d["L1_attn_gk"]).reshape(1, 512),
        "router_w": d["L1_ffn_router_w"],
        "router_b": np.asarray(d["L1_ffn_router_b"]).reshape(1, 8),
        "cos1q": C["cos1q"], "sin1q": C["sin1q"], "cos1k": C["cos1k"], "sin1k": C["sin1k"],
        "wo": d["L1_attn_wo"], "maskdiag": C["maskdiag"],
        "w_gate": d["L1_ffn_w_gate"], "w_up": d["L1_ffn_w_up"], "w_down": d["L1_ffn_w_down"],
        "b_gate": np.asarray(d["L1_ffn_b_gate"]).reshape(1, DFF1),
        "b_up": np.asarray(d["L1_ffn_b_up"]).reshape(1, DFF1),
        "b_down": np.asarray(d["L1_ffn_b_down"]).reshape(1, D),
    }
    return base


def kernel(**inputs):
    inputs = {k: np.asarray(v) for k, v in inputs.items()}
    C = host_constants()
    ncA, recA = build_launch_A(dev_out=(), phases_sel=PHASES_ALL)
    fcA = recA.const_array()
    in_mapsA = [make_inmap_A(inputs, C, fcA, cc) for cc in range(NCORE)]
    resA = run_bass_kernel_spmd(ncA, in_mapsA, core_ids=list(range(NCORE)))

    Sm_full = np.concatenate([np.asarray(resA.results[cc]["Sm"]) for cc in range(NCORE)], 0)
    SmL = np.ascontiguousarray(Sm_full.T.reshape(8, 16, 2048).reshape(128, 2048))

    ncB, recB = build_launch_B()
    fcB = recB.const_array()
    in_mapsB = []
    for cc in range(NCORE):
        in_mapsB.append({
            "fconst": fcB,
            "SmL": SmL,
            "Smloc": np.ascontiguousarray(Sm_full[cc * T_LOC:(cc + 1) * T_LOC]),
            "partial": np.asarray(resA.results[cc]["partial"]),
            "f1": np.asarray(resA.results[cc]["f1"]),
            "Emat": C["Emat"],
            "E2mat": np.ascontiguousarray(C["Emat"].T),
            "I8": C["I8"],
            "ones8_128": np.ones((8, 128), np.float32),
        })
    resB = run_bass_kernel_spmd(ncB, in_mapsB, core_ids=list(range(NCORE)))
    y = np.concatenate([np.asarray(resB.results[cc]["y"]) for cc in range(NCORE)], 0)
    return y.reshape(16, 2048, 2048).astype(np.float32)


